# version 04
# baseline (speedup 1.0000x reference)
from contextlib import ExitStack

import numpy as np
import ml_dtypes

import concourse.bass as bass
import concourse.mybir as mybir
from concourse.bass_utils import run_bass_kernel_spmd

F32 = mybir.dt.float32
BF16 = mybir.dt.bfloat16
F16 = mybir.dt.float16
AOT = mybir.AluOpType
AFT = mybir.ActivationFunctionType
AXX = mybir.AxisListType.X

MAGIC = 12582912.0
NEG = -1.0e30
LN_EPS = 1e-5
Q_START, Q_END = 611, 2047


class Cfg:
    def __init__(self, D=4096, F=16384, S=2048, NH=32, KVH=8):
        self.D, self.F, self.S, self.NH, self.KVH = D, F, S, NH, KVH
        self.G = NH // KVH
        self.DC = D // 128
        self.FC = F // 128
        self.J = S // 256
        self.T = 128 * self.J
        self.DSL = D // 512
        self.UPSL = F // 512
        self.FT = F // 128
        self.NFB = F // 2048
        self.WCNT = S - 1 - 611


FULL = Cfg()
TINY = Cfg(D=512, F=2048, S=1024, NH=8, KVH=2)



def _host_prepare(cfg, L0_attn_x, L0_attn_gamma, L0_ffn_w_up, L0_ffn_w_down,
                  L1_attn_w_qkv, L1_attn_w_dense, L1_ffn_W, L1_ffn_b, L1_ffn_A,
                  L1_ffn_B, L1_ffn_router_W, L1_ffn_router_b):
    c = cfg
    f32 = np.float32

    def t(a):
        return np.ascontiguousarray(np.asarray(a).T)

    w_upT = np.ascontiguousarray(
        t(L0_ffn_w_up).reshape(c.D, c.FT, 128).transpose(1, 0, 2))
    w_downT = t(L0_ffn_w_down)
    w_qkvT = t(L1_attn_w_qkv)
    wq = np.ascontiguousarray(
        w_qkvT[:, : c.NH * 128].reshape(c.D, c.NH, 128).transpose(1, 0, 2))
    wk = np.ascontiguousarray(w_qkvT[:, c.NH * 128: (c.NH + c.KVH) * 128])
    wv = np.ascontiguousarray(w_qkvT[:, (c.NH + c.KVH) * 128:])
    w_denseT = np.ascontiguousarray(
        t(L1_attn_w_dense).reshape(c.NH * 128, c.DSL, 512).transpose(1, 0, 2))
    WT = np.ascontiguousarray(
        t(L1_ffn_W).reshape(c.D, c.DSL, 512).transpose(1, 0, 2))
    b_row = np.asarray(L1_ffn_b, f32).reshape(1, c.D)
    ER = np.asarray(L1_ffn_A).shape[0] * np.asarray(L1_ffn_A).shape[1]
    A_T = t(np.asarray(L1_ffn_A).reshape(ER, c.D))
    B_T = np.ascontiguousarray(
        np.transpose(np.asarray(L1_ffn_B), (0, 2, 1)).reshape(ER, c.D))
    rW_T = t(L1_ffn_router_W)
    rb_row = np.asarray(L1_ffn_router_b, f32).reshape(1, -1)
    gamma_b = np.ascontiguousarray(
        np.broadcast_to(np.asarray(L0_attn_gamma).reshape(1, 1), (128, 1)).astype(f32))
    eye = np.eye(128, dtype=ml_dtypes.bfloat16)
    eye32 = np.eye(128, dtype=f32)
    ones_col = np.ones((128, 1), f32)
    ones_row = np.ones((1, 128), f32)
    consts_v = np.ascontiguousarray(np.broadcast_to(
        np.array([[LN_EPS, MAGIC, 0.0625]], f32), (128, 3)))
    tri = np.where(np.tril(np.ones((128, 128), bool)), 0.0, NEG).astype(f32)
    negf = np.full((128, 128), NEG, f32)
    zf = np.zeros((128, 128), f32)
    repm = np.zeros((8, 128), f32)
    for e in range(8):
        repm[e, e * 16:(e + 1) * 16] = 2.0

    x = np.asarray(L0_attn_x)
    in_maps, metas = [], []
    for core in range(8):
        b, p = core // 2, core % 2
        own = np.concatenate([np.arange(128 * (2 * j + p), 128 * (2 * j + p) + 128)
                              for j in range(c.J)])
        par = np.concatenate([np.arange(128 * (2 * j + 1 - p), 128 * (2 * j + 1 - p) + 128)
                              for j in range(c.J)])
        xbc = np.ascontiguousarray(x[b][np.concatenate([own, par])])
        mask0 = tri if p == 0 else zf
        mask1 = negf if p == 0 else tri
        rwin_row = np.zeros((1, c.T), f32)
        for j in range(c.J):
            g = 128 * (2 * j + p) + np.arange(128)
            rwin_row[0, j * 128:(j + 1) * 128] = (
                (g >= Q_START) & (g < c.S - 1)).astype(f32)
        nsh = c.D * c.UPSL // 8
        in_maps.append(dict(
            xb=xbc, gamma_b=gamma_b, eye=eye, eye32=eye32, ones_col=ones_col,
            ones_row=ones_row, consts_v=consts_v,
            mask0=mask0, mask1=mask1, rwin_row=rwin_row, repm=repm,
            w_upT=w_upT, w_downT=w_downT, wq=wq, wk=wk, wv=wv, w_denseT=w_denseT,
            WT=WT, b_row=b_row, A_T=A_T, B_T=B_T, rW_T=rW_T, rb_row=rb_row,
            wup_shard=np.ascontiguousarray(w_upT.reshape(-1, 512)[core * nsh:(core + 1) * nsh]),
            wdn_shard=np.ascontiguousarray(w_downT[core * (c.F // 8):(core + 1) * (c.F // 8)]),
        ))
        metas.append(dict(b=b, own=own))
    return in_maps, metas



class KB:
    ENGS = ("T", "V", "S", "G", "Y")

    def __init__(self, nc, stack):
        self.nc = nc
        self.stack = stack
        self.ops = {k: [] for k in self.ENGS}
        self.cnt = {}
        self.sems = {}
        self._waited = {k: {} for k in self.ENGS}
        self.region_toks = []

    def _sem(self, name):
        if name not in self.sems:
            self.sems[name] = self.stack.enter_context(self.nc.semaphore(name))
            self.cnt[name] = 0
        return self.sems[name]

    def wait(self, eng, tok):
        if tok is None:
            return
        name, val = tok
        if val <= 0:
            return
        sem = self._sem(name)
        assert val <= self.cnt[name], (eng, name, val, self.cnt[name])
        if self._waited[eng].get(name, 0) >= val:
            return
        self._waited[eng][name] = val
        self.ops[eng].append(lambda e, s=sem, v=val: e.wait_ge(s, v))

    def emit(self, eng, fn, waits=(), inc=True):
        for tk in waits:
            self.wait(eng, tk)
        if inc:
            sem = self._sem(eng)
            self.cnt[eng] += 1
            tok = (eng, self.cnt[eng])
            self.ops[eng].append(lambda e, f=fn, s=sem: f(e).then_inc(s, 1))
            return tok
        self.ops[eng].append(fn)
        return None

    def dma(self, q, out, in_, waits=(), sbuf_dest=True, ch="g", **kw):
        assert q in ("Y", "S", "G")
        w = list(waits) + (self.region_toks if sbuf_dest else [])
        for tk in w:
            self.wait(q, tk)
        name = "d" + q + "_" + ch
        sem = self._sem(name)
        self.cnt[name] += 16
        tok = (name, self.cnt[name])
        self.ops[q].append(lambda e, o=out, i=in_, k=dict(kw), s=sem:
                           e.dma_start(out=o, in_=i, **k).then_inc(s, 16))
        return tok

    def tok(self, name):
        v = self.cnt.get(name, 0)
        return (name, v) if v else None

    def close_regions(self):
        toks = [(k, self.cnt.get(k, 0)) for k in ("T", "V", "S", "G")]
        toks = [t for t in toks if t[1] > 0]
        dtoks = [(k, v) for k, v in self.cnt.items()
                 if k.startswith("d") and v > 0]
        for k in ("T", "V", "S", "G"):
            for t in toks:
                if t[0] != k:
                    self.wait(k, t)
            for t in dtoks:
                self.wait(k, t)
        self.region_toks = toks + dtoks

    def finish(self):
        for q, v in sorted(self.cnt.items()):
            if q.startswith("d") and v:
                self.wait("Y", (q, v))

    def install(self, block):
        attr = {"T": "tensor", "V": "vector", "S": "scalar", "G": "gpsimd", "Y": "sync"}
        for k, a in attr.items():
            if not self.ops[k]:
                continue
            def body(e, ops=self.ops[k]):
                for f in ops:
                    f(e)
            getattr(block, a)(body)


class Bank:
    def __init__(self, ap):
        self.ap = ap
        self.free = None


def _toks(x):
    if x is None:
        return []
    if isinstance(x, list):
        return [t for t in x if t is not None]
    return [x]


def mm_group(kb, bank, out_ap, mms, waits=()):
    n = len(mms)
    first = _toks(bank.free) + list(waits)
    tok = None
    for i, (l, r) in enumerate(mms):
        tok = kb.emit(
            "T",
            lambda e, o=out_ap, l=l, r=r, st=(i == 0), sp=(i == n - 1):
                e.matmul(o, l, r, start=st, stop=sp),
            waits=first if i == 0 else (), inc=(i == n - 1))
    return tok


class TP:

    def __init__(self, kb, banks, eye_ap):
        self.kb = kb
        self.banks = banks
        self.eye = eye_ap
        self.i = 0

    def run(self, src_ap, dst_ap, waits=(), evac="V"):
        bank = self.banks[self.i % len(self.banks)]
        self.i += 1
        t = self.kb.emit("T", lambda e, o=bank.ap, s=src_ap: e.transpose(o, s, self.eye),
                         waits=_toks(bank.free) + list(waits))
        if evac == "V":
            fn = lambda e, d=dst_ap, s=bank.ap: e.tensor_copy(d, s)
        else:
            fn = lambda e, d=dst_ap, s=bank.ap: e.copy(d, s)
        ev = self.kb.emit(evac, fn, waits=[t])
        bank.free = ev
        return t, ev



def build_nc(cfg, dbg=None):
    c = cfg
    J, DC, FC, D, F, T = c.J, c.DC, c.FC, c.D, c.F, c.T
    S2 = 2 * T
    KT = 2 * J
    nc = bass.Bass("TRN2", target_bir_lowering=False, debug=False)

    dt = lambda n, sh, ty=F32, kind="ExternalInput": nc.dram_tensor(n, sh, ty, kind=kind).ap()
    di = lambda n, sh, ty=F32: nc.dram_tensor(n, sh, ty).ap()

    xb = dt("xb", [S2, D])
    gamma_b = dt("gamma_b", [128, 1])
    eye = dt("eye", [128, 128], BF16)
    ones_col = dt("ones_col", [128, 1])
    ones_row = dt("ones_row", [1, 128])
    consts_v = dt("consts_v", [128, 3])
    mask0 = dt("mask0", [128, 128])
    mask1 = dt("mask1", [128, 128])
    rwin_row = dt("rwin_row", [1, T])
    eye32 = dt("eye32", [128, 128])
    repm = dt("repm", [8, 128])
    w_upT = dt("w_upT", [c.FT, D, 128])
    w_downT = dt("w_downT", [F, D])
    wq = dt("wq", [c.NH, D, 128])
    wk = dt("wk", [D, c.KVH * 128])
    wv = dt("wv", [D, c.KVH * 128])
    w_denseT = dt("w_denseT", [c.DSL, c.NH * 128, 512])
    WT = dt("WT", [c.DSL, D, 512])
    b_row = dt("b_row", [1, D])
    A_T = dt("A_T", [D, 128])
    B_T = dt("B_T", [128, D])
    rW_T = dt("rW_T", [D, 8])
    rb_row = dt("rb_row", [1, 8])
    nsh = c.D * c.UPSL // 8
    wup_shard = dt("wup_shard", [nsh, 512])
    wdn_shard = dt("wdn_shard", [F // 8, D])
    out = dt("out", [T, D], F32, "ExternalOutput")
    outv = out.rearrange("(t p) d -> t p d", p=128)

    dbgs = dt("dbgs", [128, 32 * J], F32, "ExternalOutput") if dbg else None
    dbge = dt("dbge", [128, 4096], F32, "ExternalOutput") if dbg else None
    v_park = di("v_park", [J, 128, D])
    x1_park = di("x1_park", [J, 128, D])
    sc_src = di("sc_src", [1, 16])
    sc_dst = di("sc_dst", [1, 16])
    rh_pk = di("rh_pk", [c.FT, 128, T], F16)
    pdn = di("pdn", [c.NFB, J, 128, D])
    x2_park = di("x2_park", [J, 128, D])
    v3_park = di("v3_park", [J, 128, D])
    x3_park = di("x3_park", [J, 128, D])
    b4_park = di("b4_park", [J, 128, D])
    qT_pk = di("qT_pk", [c.NH, 128, T], BF16)
    att_pk = di("att_pk", [c.NH, 128, T], BF16)
    kv_srcK = di("kv_srcK", [128, c.KVH * T], BF16)
    kv_srcV = di("kv_srcV", [128, c.KVH * T], BF16)
    kv_dstK = di("kv_dstK", [256, c.KVH * T], BF16)
    kv_dstV = di("kv_dstV", [256, c.KVH * T], BF16)
    q_src = di("q_src", [D])
    q_dst = di("q_dst", [D])
    rt_d = di("rt_d", [8])
    s2_d = di("s2_d", [T])

    xbv = xb.rearrange("(n p) d -> n p d", p=128)

    stack = ExitStack()
    kb = KB(nc, stack)

    def sb(st_, n, sh, ty=F32, side="left"):
        return st_.enter_context(nc.sbuf_tensor(n, sh, ty, side=side))

    def psb(n, sh, ty=F32):
        return Bank(stack.enter_context(nc.psum_tensor(n, sh, ty))[:])

    with stack:
        eye_s = sb(stack, "eye_s", [128, 128], BF16, side="right")
        mk0_s = sb(stack, "mk0_s", [128, 128], side="right")
        mk1_s = sb(stack, "mk1_s", [128, 128], side="right")
        gam_s = sb(stack, "gam_s", [128, 1], side="right")
        onec_s = sb(stack, "onec_s", [128, 1], side="right")
        oner_s = sb(stack, "oner_s", [1, 128], side="right")
        cv_s = sb(stack, "cv_s", [128, 3], side="right")
        eye32_s = sb(stack, "eye32_s", [128, 128], side="right")
        repm_s = sb(stack, "repm_s", [8, 128], side="right")
        rwr_s = sb(stack, "rwr_s", [1, T], side="right")
        stat = sb(stack, "stat", [128, 32 * J], side="right")
        sqp = sb(stack, "sqp", [128, J * c.DSL], side="right")
        sc_s = sb(stack, "sc_s", [1, 16], side="right")
        scb_s = sb(stack, "scb_s", [128, 8], side="right")

        ld_eye = kb.dma("Y", eye_s[:], eye, ch="c0")
        ld_mk0 = kb.dma("Y", mk0_s[:], mask0, ch="c1")
        ld_mk1 = kb.dma("Y", mk1_s[:], mask1, ch="c2")
        ld_gam = kb.dma("Y", gam_s[:], gamma_b, ch="c3")
        ld_one = kb.dma("Y", onec_s[:], ones_col, ch="c4")
        ld_oner = kb.dma("Y", oner_s[:], ones_row, ch="c7")
        ld_cv = kb.dma("Y", cv_s[:], consts_v, ch="c5")
        ld_eye32 = kb.dma("Y", eye32_s[:], eye32, ch="c6")
        ld_repm = kb.dma("Y", repm_s[:], repm, ch="c8")
        ld_rwr = kb.dma("Y", rwr_s[:], rwin_row, ch="c9")
        z_stat = kb.emit("V", lambda e: e.memset(stat[:], 0.0))
        z_sqp = kb.emit("V", lambda e: e.memset(sqp[:], 0.0))

        C_MIN, C_ESUM, C_PVS = 0, 1, 2
        C_SUM, C_MU, C_VAR, C_RST, C_NMU = 3, 4, 5, 6, 7
        C_AM1, C_S1, C_IS1 = 8, 9, 10
        C_AM2, C_S2, C_C2 = 11, 12, 13
        C_T0, C_T1, C_T2, C_T3 = 14, 15, 16, 17
        C_C1, C_HMX, C_S2P, C_R2M = 18, 19, 20, 21

        def st(j, col, n=1):
            return stat[:, 32 * j + col:32 * j + col + n]

        mmb = [psb(f"mm{i}", [128, 512]) for i in range(6)]
        tpb = [psb("tp0", [128, 1024], BF16), psb("tp1", [128, 1024], BF16)]
        tp = TP(kb, [Bank(b.ap[:, 0:128]) for b in tpb], eye_s[:])

        w1_stack = stack.enter_context(ExitStack())
        accw = sb(w1_stack, "accw", [128, 2])
        shard_b = [sb(w1_stack, f"shard_b{i}", [128, 4096], BF16) for i in range(2)]
        z0 = kb.emit("V", lambda e: e.memset(accw[:], 0.0))
        upv = wup_shard.rearrange("(n p) k -> n p k", p=128)
        dnv = wdn_shard.rearrange("(n p) k -> n p k", p=128)
        jobs = []
        k_up = 4096 // 512
        n_up_tiles = nsh // 128
        for i in range(0, n_up_tiles, k_up):
            n = min(k_up, n_up_tiles - i)
            jobs.append((0, upv[i:i + n], n, 512))
        k_dn = max(1, 4096 // D)
        n_dn_tiles = F // 8 // 128
        for i in range(0, n_dn_tiles, k_dn):
            n = min(k_dn, n_dn_tiles - i)
            jobs.append((1, dnv[i:i + n], n, D))
        sh_free = [z0, z0]
        vtok = z0
        for ji, (which, ap, n, kw_) in enumerate(jobs):
            bi = ji % 2
            width = n * kw_
            dsb = shard_b[bi][:, 0:width]
            dtok = kb.dma("G", dsb.rearrange("p (n k) -> p n k", n=n),
                          ap.rearrange("n p k -> p n k"),
                          waits=[sh_free[bi]], ch=f"sh{bi}")
            rtok = kb.emit("V", lambda e, d=dsb: e.reduce_sum(
                st(0, C_T3), d, AXX, apply_absolute_value=True), waits=[dtok, vtok])
            vtok = kb.emit("V", lambda e, w=which: e.tensor_add(
                accw[:, w:w + 1], accw[:, w:w + 1], st(0, C_T3)), waits=[rtok])
            sh_free[bi] = vtok
        tsc = kb.emit("T", lambda e: e.matmul(mmb[4].ap[0:1, 0:2], onec_s[:, 0:1],
                                              accw[:], start=True, stop=True),
                      waits=[vtok, ld_one, mmb[4].free])
        zs = kb.emit("V", lambda e: e.memset(sc_s[:], 0.0))
        cpy = kb.emit("V", lambda e: e.tensor_copy(sc_s[:, 0:2], mmb[4].ap[0:1, 0:2]),
                      waits=[tsc, zs])
        mmb[4].free = cpy
        dcc = kb.dma("G", sc_src, sc_s[:], waits=[cpy], sbuf_dest=False, ch="sc")
        cc1 = kb.emit("G", lambda e: e.collective_compute(
            "AllReduce", AOT.add, replica_groups=[[0, 1, 2, 3, 4, 5, 6, 7]],
            ins=[sc_src], outs=[sc_dst]), waits=[dcc])
        dcb = kb.dma("G", sc_s[:], sc_dst, waits=[cc1], ch="sc")
        t1 = kb.emit("V", lambda e: e.tensor_scalar(sc_s[:, 2:3], sc_s[:, 0:1],
                                                    1.0 / (D * F), LN_EPS, AOT.mult, AOT.add),
                     waits=[dcb, cpy])
        t2 = kb.emit("V", lambda e: e.tensor_scalar(sc_s[:, 4:5], sc_s[:, 1:2],
                                                    1.0 / (D * F), LN_EPS, AOT.mult, AOT.add),
                     waits=[dcb])
        t3 = kb.emit("V", lambda e: e.reciprocal(sc_s[:, 3:4], sc_s[:, 2:3]), waits=[t1])
        sc_done = kb.emit("V", lambda e: e.reciprocal(sc_s[:, 5:6], sc_s[:, 4:5]),
                          waits=[t2, t3])
        tb2 = kb.emit("T", lambda e: e.matmul(mmb[5].ap[:, 0:8], oner_s[0:1, :],
                                              sc_s[0:1, 0:8], start=True, stop=True),
                      waits=[sc_done, ld_oner, mmb[5].free])
        scb_done = kb.emit("V", lambda e: e.tensor_copy(scb_s[:], mmb[5].ap[:, 0:8]),
                           waits=[tb2])
        mmb[5].free = scb_done
        w1_stack.close()
        kb.close_regions()

        pt_stack = stack.enter_context(ExitStack())
        ab_stack = stack.enter_context(ExitStack())
        JH = max(1, J // 2)
        xq_fm = sb(ab_stack, "xq_fm", [128, DC * T], BF16)
        pan_tm = sb(ab_stack, "pan_tm", [128, 2 * D], BF16)
        pan_fm = sb(ab_stack, "pan_fm", [128, DC * 512], BF16)
        e_buf = sb(ab_stack, "e_buf", [128, JH * S2])
        p_buf = sb(ab_stack, "p_buf", [128, S2], BF16)
        p_T = sb(pt_stack, "p_T", [128, KT * T], BF16, side="right")

        NPAN = S2 // 512
        OWN_PAN = T // 512
        pan_rd_free = [None]
        xq_built = [False]
        pvs_tok = [None] * J
        pT_last = [None]
        min_tok = [None] * J

        def load_panel(pnl, into_own):
            ready = None
            for h in range(2):
                dtok = kb.dma("G", pan_tm[:].rearrange("p (t d) -> p t d", t=2),
                              xbv[4 * pnl + 2 * h: 4 * pnl + 2 * h + 2]
                              .rearrange("t p d -> p t d"),
                              waits=[pan_rd_free[0]], ch=f"pan{h}")
                for tt in range(2):
                    for dc in range(DC):
                        src = pan_tm[:, tt * D + dc * 128: tt * D + dc * 128 + 128]
                        if into_own:
                            col = (4 * pnl + 2 * h + tt) * 128
                            dst = xq_fm[:, dc * T + col: dc * T + col + 128]
                        else:
                            dst = pan_fm[:, dc * 512 + (2 * h + tt) * 128:
                                         dc * 512 + (2 * h + tt) * 128 + 128]
                        last_t, ready = tp.run(src, dst, waits=[dtok, ld_eye])
                pan_rd_free[0] = last_t
            return ready

        for half, qts in enumerate([list(range(0, JH)), list(range(JH, J))]):
            if not qts:
                continue
            if not xq_built[0]:
                own_ready = None
                for pnl in range(OWN_PAN):
                    own_ready = load_panel(pnl, True)
                xq_built[0] = own_ready
            for pnl in range(NPAN):
                if pnl < OWN_PAN:
                    rhs = [xq_fm[:, dc * T + pnl * 512: dc * T + pnl * 512 + 512]
                           for dc in range(DC)]
                    pan_ready = xq_built[0]
                else:
                    pan_ready = load_panel(pnl, False)
                    rhs = [pan_fm[:, dc * 512: dc * 512 + 512] for dc in range(DC)]
                for qi, qt in enumerate(qts):
                    bank = mmb[qi % 4]
                    mms = [(xq_fm[:, dc * T + qt * 128: dc * T + qt * 128 + 128], rhs[dc])
                           for dc in range(DC)]
                    mt = mm_group(kb, bank, bank.ap, mms, waits=[pan_ready])
                    dst = e_buf[:, qi * S2 + pnl * 512: qi * S2 + pnl * 512 + 512]
                    ev = kb.emit("S", lambda e, d=dst, s=bank.ap: e.copy(d, s), waits=[mt])
                    bank.free = ev
                    rt = kb.emit("V", lambda e, d=dst, j=qt: e.tensor_reduce(
                        st(j, C_T0), d, AXX, AOT.min), waits=[ev, min_tok[qt]])
                    if min_tok[qt] is None:
                        min_tok[qt] = kb.emit("V", lambda e, j=qt: e.tensor_copy(
                            st(j, C_MIN), st(j, C_T0)), waits=[rt])
                    else:
                        min_tok[qt] = kb.emit("V", lambda e, j=qt: e.tensor_tensor(
                            st(j, C_MIN), st(j, C_MIN), st(j, C_T0), AOT.min),
                            waits=[rt])
            pbuf_free = None
            for qi, qt in enumerate(qts):
                exp_tok = None
                add_tok = None
                for kk in range(S2 // 512):
                    src = e_buf[:, qi * S2 + kk * 512: qi * S2 + kk * 512 + 512]
                    dst = p_buf[:, kk * 512: kk * 512 + 512]
                    acc = st(qt, C_ESUM) if kk == 0 else st(qt, C_T1)
                    exp_tok = kb.emit("S", lambda e, s=src, d=dst, a=acc, j=qt: e.activation(
                        d, s, AFT.Exp, bias=st(j, C_MIN), scale=-1.0, accum_out=a),
                        waits=[min_tok[qt], pbuf_free, add_tok])
                    if kk > 0:
                        add_tok = kb.emit("V", lambda e, j=qt: e.tensor_add(
                            st(j, C_ESUM), st(j, C_ESUM), st(j, C_T1)),
                            waits=[exp_tok] + ([add_tok] if add_tok else []))
                rc = kb.emit("V", lambda e, j=qt: e.reciprocal(st(j, C_PVS), st(j, C_ESUM)),
                             waits=[exp_tok, add_tok])
                pvs_tok[qt] = kb.emit("V", lambda e, j=qt: e.tensor_scalar(
                    st(j, C_PVS), st(j, C_PVS), gam_s[:, 0:1], 0.5, AOT.mult, AOT.mult),
                    waits=[rc, ld_gam])
                ttok = None
                for kt in range(KT):
                    src = p_buf[:, kt * 128: kt * 128 + 128]
                    dst = p_T[:, kt * T + qt * 128: kt * T + qt * 128 + 128]
                    ttok, pT_last[0] = tp.run(src, dst, waits=[exp_tok], evac="S")
                pbuf_free = ttok
        pT_done = pT_last[0]

        if dbg == "E":
            kb.dma("Y", dbgs, stat[:, 0:32 * J], waits=[kb.tok("V"), kb.tok("S")],
                   sbuf_dest=False, ch="out")
            kb.dma("Y", dbge, e_buf[:, 0:4096], waits=[kb.tok("S")], sbuf_dest=False, ch="out")
            kb.finish()
            with nc.Block() as block:
                kb.install(block)
            return nc

        ab_stack.close()
        kb.close_regions()

        c_stack = stack.enter_context(ExitStack())
        xcol = [sb(c_stack, f"xcol{i}", [128, KT * 512], BF16) for i in range(2)]
        xown = [sb(c_stack, f"xown{i}", [128, J * 512]) for i in range(2)]
        vstg = [sb(c_stack, f"vstg{i}", [128, J * 512]) for i in range(2)]
        sq_dump = sb(c_stack, "sq_dump", [128, 512])
        xc_free = [None, None]
        xo_free = [None, None]
        v_free = [None, None]
        sum_tok = [None] * J
        sq_last = [None] * J
        sqd_tok = None
        for ds in range(c.DSL):
            bi = ds % 2
            ctok = kb.dma("G", xcol[bi][:].rearrange("p (t k) -> p t k", t=KT),
                          xbv[:, :, ds * 512:(ds + 1) * 512].rearrange("t p k -> p t k"),
                          waits=[xc_free[bi]], ch=f"xc{bi}")
            otok = kb.dma("Y", xown[bi][:].rearrange("p (t k) -> p t k", t=J),
                          xbv[0:J, :, ds * 512:(ds + 1) * 512].rearrange("t p k -> p t k"),
                          waits=[xo_free[bi]], ch=f"xo{bi}")
            last_mm = None
            for qt in range(J):
                bank = mmb[qt % 6]
                mms = [(p_T[:, kt * T + qt * 128: kt * T + qt * 128 + 128],
                        xcol[bi][:, kt * 512: kt * 512 + 512]) for kt in range(KT)]
                last_mm = mm_group(kb, bank, bank.ap, mms, waits=[ctok, pT_done])
                dst = vstg[bi][:, qt * 512: qt * 512 + 512]
                xo = xown[bi][:, qt * 512: qt * 512 + 512]
                ev = kb.emit("V", lambda e, d=dst, s=bank.ap, x=xo, j=qt: e.scalar_tensor_tensor(
                    d, s, st(j, C_PVS), x, AOT.mult, AOT.add),
                    waits=[last_mm, otok, v_free[bi], pvs_tok[qt]])
                bank.free = ev
                rt = kb.emit("V", lambda e, d=dst, j=qt: e.reduce_sum(st(j, C_T0), d, AXX),
                             waits=[ev] + ([sum_tok[qt]] if sum_tok[qt] else []))
                if ds == 0:
                    sum_tok[qt] = kb.emit("V", lambda e, j=qt: e.tensor_copy(
                        st(j, C_SUM), st(j, C_T0)), waits=[rt])
                else:
                    sum_tok[qt] = kb.emit("V", lambda e, j=qt: e.tensor_add(
                        st(j, C_SUM), st(j, C_SUM), st(j, C_T0)), waits=[rt])
                sq_last[qt] = kb.emit("S", lambda e, d=dst, j=qt, k=ds: e.activation(
                    sq_dump[:], d, AFT.Square,
                    accum_out=sqp[:, j * c.DSL + k: j * c.DSL + k + 1]),
                    waits=[ev] + ([sqd_tok] if sqd_tok else []))
                sqd_tok = sq_last[qt]
            ptok = kb.dma("S", v_park[:, :, ds * 512:(ds + 1) * 512].rearrange("t p k -> p t k"),
                          vstg[bi][:].rearrange("p (t k) -> p t k", t=J),
                          waits=[sq_last[J - 1]], sbuf_dest=False, ch="vpk")
            v_free[bi] = ptok
            xc_free[bi] = last_mm
            xo_free[bi] = sum_tok[J - 1]

        if dbg == "v1":
            kb.dma("Y", dbgs, stat[:, 0:32 * J], waits=[kb.tok("V"), kb.tok("S")],
                   sbuf_dest=False, ch="out")
            for qt in range(J):
                kb.dma("Y", outv[qt], v_park[qt], waits=[kb.tok("dS_vpk")],
                       sbuf_dest=False, ch="out")
            kb.finish()
            with nc.Block() as block:
                kb.install(block)
            return nc

        c_stack.close()
        pt_stack.close()
        kb.close_regions()

        d_outer = stack.enter_context(ExitStack())
        M_up = sb(d_outer, "M_up", [128, T])
        Vs2 = sb(d_outer, "Vs2", [128, T])
        s2c = sb(d_outer, "s2c", [128, 8])
        s2row = sb(d_outer, "s2row", [1, T])
        up_stack = stack.enter_context(ExitStack())
        xiT = sb(up_stack, "xiT", [128, DC * T], BF16)
        ln1_stack = stack.enter_context(ExitStack())
        v_in = [sb(ln1_stack, f"v_in{i}", [128, D]) for i in range(2)]
        x1t = sb(ln1_stack, "x1t", [128, D])
        xit = sb(ln1_stack, "xit", [128, D], BF16)
        vin_free = [None, None]
        x1t_free = None
        xit_free = None
        x1_done = [None] * J
        c1_tok = [None] * J
        xiT_ev = None
        parks_done = kb.tok("dS_vpk")
        for qt in range(J):
            bi = qt % 2
            dtok = kb.dma("Y", v_in[bi][:], v_park[qt], waits=[vin_free[bi], parks_done],
                          ch=f"vi{bi}")
            a1 = kb.emit("V", lambda e, j=qt: e.tensor_scalar(
                st(j, C_MU), st(j, C_SUM), 1.0 / D, None, AOT.mult),
                waits=[sum_tok[qt]])
            a2 = kb.emit("V", lambda e, j=qt: e.tensor_reduce(
                st(j, C_VAR), sqp[:, j * c.DSL:(j + 1) * c.DSL], AXX, AOT.add),
                waits=[sq_last[qt], sqd_tok])
            a3 = kb.emit("V", lambda e, j=qt: e.tensor_scalar(
                st(j, C_VAR), st(j, C_VAR), 1.0 / D, None, AOT.mult), waits=[a2])
            a4 = kb.emit("V", lambda e, j=qt: e.tensor_tensor(
                st(j, C_T0), st(j, C_MU), st(j, C_MU), AOT.mult), waits=[a1])
            var_tok = kb.emit("V", lambda e, j=qt: e.tensor_tensor(
                st(j, C_VAR), st(j, C_VAR), st(j, C_T0), AOT.subtract), waits=[a3, a4])
            sd_tok = kb.emit("S", lambda e, j=qt: e.activation(
                st(j, C_RST), st(j, C_VAR), AFT.Sqrt, bias=cv_s[:, 0:1]),
                waits=[var_tok, ld_cv])
            r1 = kb.emit("V", lambda e, j=qt: e.reciprocal(st(j, C_RST), st(j, C_RST)),
                         waits=[sd_tok])
            r2_ = kb.emit("V", lambda e, j=qt: e.tensor_tensor(
                st(j, C_NMU), st(j, C_MU), st(j, C_RST), AOT.mult), waits=[r1, a1])
            nmu_tok = kb.emit("V", lambda e, j=qt: e.tensor_scalar(
                st(j, C_NMU), st(j, C_NMU), -1.0, None, AOT.mult), waits=[r2_])
            ln_tok = kb.emit("S", lambda e, j=qt, b=bi: e.activation(
                x1t[:], v_in[b][:], AFT.Identity, bias=st(j, C_NMU), scale=st(j, C_RST)),
                waits=[dtok, nmu_tok, x1t_free])
            vin_free[bi] = ln_tok
            x1_done[qt] = kb.dma("S", x1_park[qt], x1t[:], waits=[ln_tok],
                                 sbuf_dest=False, ch="x1p")
            m1 = kb.emit("V", lambda e, j=qt: e.reduce_max(
                st(j, C_AM1), x1t[:], AXX, apply_absolute_value=True), waits=[ln_tok])
            m2 = kb.emit("V", lambda e, j=qt: e.tensor_scalar(
                st(j, C_AM1), st(j, C_AM1), 1e-5, None, AOT.max), waits=[m1])
            m3 = kb.emit("V", lambda e, j=qt: e.reciprocal(st(j, C_S1), st(j, C_AM1)),
                         waits=[m2])
            m4 = kb.emit("V", lambda e, j=qt: e.tensor_scalar(
                st(j, C_S1), st(j, C_S1), 127.0, None, AOT.mult), waits=[m3])
            s1_tok = kb.emit("V", lambda e, j=qt: e.tensor_scalar(
                st(j, C_IS1), st(j, C_AM1), 1.0 / 127, None, AOT.mult), waits=[m2])
            c1_tok[qt] = kb.emit("V", lambda e, j=qt: e.tensor_tensor(
                st(j, C_C1), st(j, C_IS1), scb_s[:, 2:3], AOT.mult),
                waits=[s1_tok, scb_done])
            q1 = kb.emit("S", lambda e, j=qt: e.activation(
                x1t[:], x1t[:], AFT.Identity, bias=cv_s[:, 1:2], scale=st(j, C_S1)),
                waits=[m4, x1_done[qt], m1, ln_tok])
            q2 = kb.emit("V", lambda e: e.tensor_scalar(
                x1t[:], x1t[:], MAGIC - 128.0, MAGIC + 127.0, AOT.max, AOT.min),
                waits=[q1])
            q3 = kb.emit("V", lambda e: e.tensor_scalar(
                xit[:], x1t[:], MAGIC, None, AOT.subtract), waits=[q2, xit_free])
            x1t_free = q3
            ttok = None
            for dc in range(DC):
                src = xit[:, dc * 128: dc * 128 + 128]
                dst = xiT[:, dc * T + qt * 128: dc * T + qt * 128 + 128]
                ttok, xiT_ev = tp.run(src, dst, waits=[q3], evac="S")
            xit_free = ttok

        if dbg == "x1":
            kb.dma("Y", dbgs, stat[:, 0:32 * J], waits=[kb.tok("V"), kb.tok("S")],
                   sbuf_dest=False, ch="out")
            for qt in range(J):
                kb.dma("Y", outv[qt], x1_park[qt], waits=[x1_done[qt]],
                       sbuf_dest=False, ch="out")
            kb.finish()
            with nc.Block() as block:
                kb.install(block)
            return nc

        ln1_stack.close()
        kb.close_regions()

        C_SUM2, C_MU2, C_VAR2, C_RST2, C_NMU2, C_T4 = 22, 23, 24, 25, 26, 27
        C_SMX, C_SNEG, C_SSUM, C_SRC = 28, 29, 30, 31
        sq_free = [None]
        stat2_free = [None] * J

        def ln_apply(j, src, dst, sqbuf, deps):
            a_s = kb.emit("V", lambda e: e.reduce_sum(st(j, C_SUM2), src, AXX),
                          waits=list(deps) + _toks(stat2_free[j]))
            a_q = kb.emit("S", lambda e: e.activation(
                sqbuf, src, AFT.Square, accum_out=st(j, C_VAR2)),
                waits=list(deps) + _toks(sq_free[0]) + _toks(stat2_free[j]))
            sq_free[0] = a_q
            m1 = kb.emit("V", lambda e: e.tensor_scalar(
                st(j, C_MU2), st(j, C_SUM2), 1.0 / D, None, AOT.mult), waits=[a_s])
            m2 = kb.emit("V", lambda e: e.tensor_scalar(
                st(j, C_VAR2), st(j, C_VAR2), 1.0 / D, None, AOT.mult), waits=[a_q])
            m3 = kb.emit("V", lambda e: e.tensor_tensor(
                st(j, C_T4), st(j, C_MU2), st(j, C_MU2), AOT.mult), waits=[m1])
            m4 = kb.emit("V", lambda e: e.tensor_tensor(
                st(j, C_VAR2), st(j, C_VAR2), st(j, C_T4), AOT.subtract),
                waits=[m2, m3])
            m5 = kb.emit("S", lambda e: e.activation(
                st(j, C_RST2), st(j, C_VAR2), AFT.Sqrt, bias=cv_s[:, 0:1]),
                waits=[m4, ld_cv])
            m6 = kb.emit("V", lambda e: e.reciprocal(st(j, C_RST2), st(j, C_RST2)),
                         waits=[m5])
            m7 = kb.emit("V", lambda e: e.tensor_tensor(
                st(j, C_NMU2), st(j, C_MU2), st(j, C_RST2), AOT.mult),
                waits=[m6, m1])
            m8 = kb.emit("V", lambda e: e.tensor_scalar(
                st(j, C_NMU2), st(j, C_NMU2), -1.0, None, AOT.mult), waits=[m7])
            ap_ = kb.emit("S", lambda e: e.activation(
                dst, src, AFT.Identity, bias=st(j, C_NMU2), scale=st(j, C_RST2)),
                waits=list(deps) + [m8])
            stat2_free[j] = ap_
            return ap_, a_s

        bank_i = [0]

        def nb():
            b = mmb[bank_i[0] % 6]
            bank_i[0] += 1
            return b

        NTC = T // 512
        FTB = 2048 // 128
        wup_stg = sb(up_stack, "wup_stg", [128, DC * 128])
        wup_tmp = sb(up_stack, "wup_tmp", [128, DC * 128])
        wuq = [sb(up_stack, f"wuq{i}", [128, DC * 128], BF16) for i in range(2)]
        rstg = [sb(up_stack, f"rstg{i}", [128, T], F16) for i in range(2)]
        zM = kb.emit("V", lambda e: e.memset(M_up[:], 0.0))
        wstg_free = None
        wtmp_free = None
        wuq_free = [None, None]
        rstg_free = [None, None]
        mx_tok = [zM] * NTC
        xiT_done = xiT_ev
        up2 = dbg is not None and dbg.startswith("up2")
        FT_eff = 2 if up2 else c.FT
        for ot in range(FT_eff):
            bi = ot % 2
            dtok = kb.dma("Y", wup_stg[:].rearrange("p (dc o) -> p dc o", dc=DC),
                          w_upT[ot].rearrange("(dc p) o -> p dc o", p=128),
                          waits=[wstg_free], ch="wup")
            uq1 = kb.emit("S", lambda e: e.activation(
                wup_tmp[:], wup_stg[:], AFT.Identity, bias=cv_s[:, 1:2],
                scale=scb_s[:, 3:4]),
                waits=[dtok, scb_done, ld_cv, wtmp_free])
            wstg_free = uq1
            uq2 = kb.emit("V", lambda e: e.tensor_scalar(
                wup_tmp[:], wup_tmp[:], MAGIC - 1.0, MAGIC + 1.0, AOT.max, AOT.min),
                waits=[uq1])
            uq3 = kb.emit("V", lambda e, b=bi: e.tensor_scalar(
                wuq[b][:], wup_tmp[:], MAGIC, None, AOT.subtract),
                waits=[uq2] + _toks(wuq_free[bi]))
            wtmp_free = uq3
            rel_toks = []
            mt = None
            for tc in range(NTC):
                bank = nb()
                mms = [(wuq[bi][:, dc * 128: dc * 128 + 128],
                        xiT[:, dc * T + tc * 512: dc * T + tc * 512 + 512])
                       for dc in range(DC)]
                mt = mm_group(kb, bank, bank.ap, mms, waits=[uq3, xiT_done])
                rel = kb.emit("S", lambda e, b=bi, t=tc, a=bank.ap:
                              e.activation(
                                  rstg[b][:, t * 512: t * 512 + 512], a,
                                  AFT.Relu, scale=cv_s[:, 2:3]),
                              waits=[mt, ld_cv] + _toks(rstg_free[bi]))
                rel_toks.append(rel)
                bank.free = [rel]
                mx = kb.emit("V", lambda e, b=bi, t=tc: e.tensor_max(
                    M_up[:, t * 512: t * 512 + 512],
                    rstg[b][:, t * 512: t * 512 + 512],
                    M_up[:, t * 512: t * 512 + 512]),
                    waits=[rel, mx_tok[tc]])
                mx_tok[tc] = mx
                rel_toks.append(mx)
            ptok = kb.dma("S", rh_pk[ot], rstg[bi][:], waits=rel_toks,
                          sbuf_dest=False, ch=f"rhp{bi}")
            rstg_free[bi] = [ptok]
            wuq_free[bi] = [mt]

        if dbg == "upq" or up2:
            kb.dma("Y", dbgs, stat[:, 0:32 * J], waits=[kb.tok("V"), kb.tok("S")],
                   sbuf_dest=False, ch="out")
            for qt in range(J):
                kb.dma("Y", outv[qt], x1_park[qt], waits=[x1_done[qt]],
                       sbuf_dest=False, ch="out")
            kb.finish()
            with nc.Block() as block:
                kb.install(block)
            return nc

        c2_tok = [None] * J
        s2c_toks = []
        for j in range(J):
            bank = nb()
            mmt = kb.emit("T", lambda e, j_=j, a=bank.ap: e.matmul(
                a[:, 0:128], M_up[:, j_ * 128: j_ * 128 + 128], eye32_s[:],
                start=True, stop=True),
                waits=_toks(bank.free) + [mx_tok[(j * 128) // 512], ld_eye32])
            hv = kb.emit("V", lambda e, j_=j, a=bank.ap: e.tensor_reduce(
                st(j_, C_HMX), a[:, 0:128], AXX, AOT.max), waits=[mmt])
            bank.free = hv
            e1 = kb.emit("V", lambda e, j_=j: e.tensor_tensor(
                st(j_, C_T2), st(j_, C_HMX), st(j_, C_HMX), AOT.mult), waits=[hv])
            e2 = kb.emit("V", lambda e, j_=j: e.tensor_scalar(
                st(j_, C_T0), st(j_, C_T2), 1e-30, None, AOT.max), waits=[e1])
            e3 = kb.emit("V", lambda e, j_=j: e.reciprocal(
                st(j_, C_T1), st(j_, C_T0)), waits=[e2])
            e4 = kb.emit("V", lambda e, j_=j: e.tensor_scalar(
                st(j_, C_S2P), st(j_, C_T1), 127.0, None, AOT.mult), waits=[e3])
            e5 = kb.emit("V", lambda e, j_=j: e.tensor_copy(
                s2c[:, j_:j_ + 1], st(j_, C_S2P)), waits=[e4])
            s2c_toks.append(e5)
            e6 = kb.emit("V", lambda e, j_=j: e.tensor_tensor(
                st(j_, C_T3), st(j_, C_C1), st(j_, C_C1), AOT.mult),
                waits=[c1_tok[j]])
            e7 = kb.emit("V", lambda e, j_=j: e.scalar_tensor_tensor(
                st(j_, C_R2M), st(j_, C_T2), 256.0, st(j_, C_T3),
                AOT.mult, AOT.mult),
                waits=[e6, e1])
            e8 = kb.emit("V", lambda e, j_=j: e.tensor_scalar(
                st(j_, C_R2M), st(j_, C_R2M), 1e-5, None, AOT.max), waits=[e7])
            c2_tok[j] = kb.emit("V", lambda e, j_=j: e.tensor_scalar(
                st(j_, C_C2), st(j_, C_R2M), scb_s[:, 4:5], 1.0 / 127,
                AOT.mult, AOT.mult), waits=[e8, scb_done])
        s2r8 = sb(up_stack, "s2r8", [J, 128])
        bank = nb()
        mmtr = kb.emit("T", lambda e, a=bank.ap: e.matmul(
            a[0:J, 0:128], s2c[:, 0:J], eye32_s[:], start=True, stop=True),
            waits=_toks(bank.free) + s2c_toks + [ld_eye32])
        cp8 = kb.emit("V", lambda e, a=bank.ap: e.tensor_copy(
            s2r8[:], a[0:J, 0:128]), waits=[mmtr])
        bank.free = cp8
        ds2w = kb.dma("S", s2_d.rearrange("(j ccc) -> j ccc", j=J), s2r8[:],
                      waits=[cp8], sbuf_dest=False, ch="s2d")
        rc_last = kb.dma("Y", s2row[0:1, 0:T],
                         s2_d.rearrange("(o t) -> o t", o=1),
                         waits=[ds2w], ch="s2d2")
        vs2_done = None
        for tc in range(NTC):
            bank2 = nb()
            bmt = kb.emit("T", lambda e, t=tc, a=bank2.ap: e.matmul(
                a, oner_s[:], s2row[0:1, t * 512: t * 512 + 512],
                start=True, stop=True),
                waits=_toks(bank2.free) + [rc_last, ld_oner])
            vs2_done = kb.emit("V", lambda e, t=tc, a=bank2.ap: e.tensor_copy(
                Vs2[:, t * 512: t * 512 + 512], a), waits=[bmt])
            bank2.free = vs2_done

        if dbg == "up":
            kb.dma("Y", dbgs, stat[:, 0:32 * J], waits=[kb.tok("V"), kb.tok("S")],
                   sbuf_dest=False, ch="out")
            for qt in range(J):
                kb.dma("Y", outv[qt], x1_park[qt], waits=[x1_done[qt]],
                       sbuf_dest=False, ch="out")
            kb.finish()
            with nc.Block() as block:
                kb.install(block)
            return nc

        up_stack.close()
        kb.close_regions()

        down_stack = stack.enter_context(ExitStack())
        rhrd = [sb(down_stack, f"rhrd{i}", [128, T], F16) for i in range(2)]
        dtmp = sb(down_stack, "dtmp", [128, T])
        xi2T = [sb(down_stack, f"xi2T{i}", [128, FTB * T], BF16) for i in range(2)]
        wdn_stg = sb(down_stack, "wdn_stg", [128, FTB * 512])
        wdq = [sb(down_stack, f"wdq{i}", [128, FTB * 512], BF16) for i in range(2)]
        pst = [sb(down_stack, f"pst{i}", [128, 512]) for i in range(4)]
        rh_parked = [kb.tok("dS_rhp0"), kb.tok("dS_rhp1")]
        rhrd_free = [None, None]
        dtmp_free = None
        xi2T_busy = [None, None]
        wstg_free2 = None
        wdq_free = [None, None]
        pst_free = [None] * 4
        pi = 0
        prep_done = [None, None]
        for fb in range(c.NFB):
            xb_ = fb % 2
            pd = None
            for ft in range(FTB):
                bi = ft % 2
                dr = kb.dma("G", rhrd[bi][:], rh_pk[fb * FTB + ft],
                            waits=_toks(rhrd_free[bi]) + rh_parked, ch=f"rhr{bi}")
                v1 = kb.emit("V", lambda e, b=bi: e.tensor_tensor(
                    dtmp[:], rhrd[b][:], rhrd[b][:], AOT.mult),
                    waits=[dr, dtmp_free])
                rhrd_free[bi] = v1
                v2 = kb.emit("V", lambda e: e.tensor_tensor(
                    dtmp[:], dtmp[:], Vs2[:], AOT.mult), waits=[v1, vs2_done])
                v3 = kb.emit("V", lambda e, x=xb_, f=ft: e.tensor_scalar(
                    xi2T[x][:, f * T: (f + 1) * T], dtmp[:], MAGIC, MAGIC,
                    AOT.add, AOT.subtract),
                    waits=[v2] + _toks(xi2T_busy[xb_]))
                dtmp_free = v3
                pd = v3
            prep_done[xb_] = pd
            mm_last = None
            for os_ in range(c.DSL):
                wi = (fb * c.DSL + os_) % 2
                dw = kb.dma("Y", wdn_stg[:].rearrange("p (kt o) -> p kt o", kt=FTB),
                            w_downT[fb * 2048:(fb + 1) * 2048,
                                    os_ * 512:(os_ + 1) * 512]
                            .rearrange("(kt p) o -> p kt o", p=128),
                            waits=[wstg_free2], ch="wdn")
                dq1 = kb.emit("S", lambda e: e.activation(
                    wdn_stg[:], wdn_stg[:], AFT.Identity, bias=cv_s[:, 1:2],
                    scale=scb_s[:, 5:6]), waits=[dw, scb_done, ld_cv])
                dq2 = kb.emit("V", lambda e: e.tensor_scalar(
                    wdn_stg[:], wdn_stg[:], MAGIC - 1.0, MAGIC + 1.0,
                    AOT.max, AOT.min), waits=[dq1])
                dq3 = kb.emit("V", lambda e, w=wi: e.tensor_scalar(
                    wdq[w][:], wdn_stg[:], MAGIC, None, AOT.subtract),
                    waits=[dq2] + _toks(wdq_free[wi]))
                wstg_free2 = dq3
                for t in range(J):
                    bank = nb()
                    mms = [(xi2T[xb_][:, kt * T + t * 128: kt * T + t * 128 + 128],
                            wdq[wi][:, kt * 512:(kt + 1) * 512])
                           for kt in range(FTB)]
                    mt = mm_group(kb, bank, bank.ap, mms,
                                  waits=[dq3, prep_done[xb_]])
                    mm_last = mt
                    p_ = pi % 4
                    ev = kb.emit("V", lambda e, a=bank.ap, q=p_: e.tensor_copy(
                        pst[q][:], a), waits=[mt] + _toks(pst_free[p_]))
                    bank.free = ev
                    pk = kb.dma("S", pdn[fb][t][:, os_ * 512:(os_ + 1) * 512],
                                pst[p_][:], waits=[ev], sbuf_dest=False,
                                ch=f"pdw{p_}")
                    pst_free[p_] = pk
                    pi += 1
                wdq_free[wi] = mm_last
            xi2T_busy[xb_] = mm_last

        down_stack.close()
        d_outer.close()
        kb.close_regions()

        x2_stack = stack.enter_context(ExitStack())
        x2T = sb(x2_stack, "x2T", [128, DC * T], BF16)
        ln2_stack = stack.enter_context(ExitStack())
        acc2 = sb(ln2_stack, "acc2", [128, D])
        pstg = [sb(ln2_stack, f"pstg{i}", [128, D]) for i in range(2)]
        x1rd = sb(ln2_stack, "x1rd", [128, D])
        x2t = sb(ln2_stack, "x2t", [128, D])
        x2b = sb(ln2_stack, "x2b", [128, D], BF16)
        sqd2 = sb(ln2_stack, "sqd2", [128, D])
        pdn_done = [kb.tok(f"dS_pdw{i}") for i in range(4)]
        x1p_done = kb.tok("dS_x1p")
        acc_free = None
        pstg_free = [None, None]
        x1rd_free = None
        x2t_free = None
        x2b_free = None
        x2_done = [None] * J
        x2T_ev = None
        for t in range(J):
            d0 = kb.dma("Y", acc2[:], pdn[0][t],
                        waits=_toks(acc_free) + pdn_done, ch="pr0")
            ad = d0
            for fb in range(1, c.NFB):
                bi = fb % 2
                dfb = kb.dma("G", pstg[bi][:], pdn[fb][t],
                             waits=_toks(pstg_free[bi]) + pdn_done,
                             ch=f"pr{bi + 1}")
                ad = kb.emit("V", lambda e, b=bi: e.tensor_add(
                    acc2[:], acc2[:], pstg[b][:]), waits=[dfb, ad])
                pstg_free[bi] = ad
            dx1 = kb.dma("Y", x1rd[:], x1_park[t],
                         waits=_toks(x1rd_free) + [x1p_done], ch="x1r")
            vv = kb.emit("V", lambda e, j=t: e.scalar_tensor_tensor(
                acc2[:], acc2[:], st(j, C_C2), x1rd[:], AOT.mult, AOT.add),
                waits=[ad, dx1, c2_tok[t]])
            x1rd_free = vv
            apt, asr = ln_apply(t, acc2[:], x2t[:], sqd2[:],
                                [vv] + _toks(x2t_free))
            acc_free = [apt, asr]
            x2_done[t] = kb.dma("S", x2_park[t], x2t[:], waits=[apt],
                                sbuf_dest=False, ch="x2p")
            cb = kb.emit("V", lambda e: e.tensor_copy(x2b[:], x2t[:]),
                         waits=[apt] + _toks(x2b_free))
            x2t_free = [cb, x2_done[t]]
            ttok = None
            for dc in range(DC):
                ttok, x2T_ev = tp.run(
                    x2b[:, dc * 128: dc * 128 + 128],
                    x2T[:, dc * T + t * 128: dc * T + t * 128 + 128],
                    waits=[cb], evac="S")
            x2b_free = ttok

        if dbg == "x2":
            kb.dma("Y", dbgs, stat[:, 0:32 * J], waits=[kb.tok("V"), kb.tok("S")],
                   sbuf_dest=False, ch="out")
            for qt in range(J):
                kb.dma("Y", outv[qt], x2_park[qt], waits=[x2_done[qt]],
                       sbuf_dest=False, ch="out")
            kb.finish()
            with nc.Block() as block:
                kb.install(block)
            return nc

        ln2_stack.close()
        kb.close_regions()

        qkv_stack = stack.enter_context(ExitStack())
        KVW = c.KVH * 128
        wqb = [sb(qkv_stack, f"wqb{i}", [128, DC * 128], BF16) for i in range(2)]
        qstg = [sb(qkv_stack, f"qstg{i}", [128, T], BF16) for i in range(2)]
        wkvb = sb(qkv_stack, "wkvb", [128, DC * 512], BF16)
        kvst = [sb(qkv_stack, f"kvst{i}", [128, 512], BF16) for i in range(4)]
        SC = 0.08838834764831845
        x2T_done = x2T_ev
        wqb_free = [None, None]
        qstg_free = [None, None]
        for h in range(c.NH):
            bi = h % 2
            dq = kb.dma("G", wqb[bi][:].rearrange("p (dc o) -> p dc o", dc=DC),
                        wq[h].rearrange("(dc p) o -> p dc o", p=128),
                        waits=_toks(wqb_free[bi]), ch=f"wq{bi}")
            evs = []
            mt = None
            for tc in range(NTC):
                bank = nb()
                mms = [(wqb[bi][:, dc * 128: dc * 128 + 128],
                        x2T[:, dc * T + tc * 512: dc * T + tc * 512 + 512])
                       for dc in range(DC)]
                mt = mm_group(kb, bank, bank.ap, mms, waits=[dq, x2T_done])
                ev = kb.emit("S", lambda e, b=bi, t=tc, a=bank.ap: e.activation(
                    qstg[b][:, t * 512: t * 512 + 512], a, AFT.Identity,
                    scale=SC), waits=[mt] + _toks(qstg_free[bi]))
                evs.append(ev)
                bank.free = ev
            pk = kb.dma("S", qT_pk[h], qstg[bi][:], waits=evs,
                        sbuf_dest=False, ch=f"qtp{bi}")
            qstg_free[bi] = pk
            wqb_free[bi] = mt
        wkv_free = None
        kvst_free = [None] * 4
        ki = 0
        kv_srcK_v = kv_srcK.rearrange("p (kvh tok) -> p kvh tok", kvh=c.KVH)
        kv_srcV_v = kv_srcV.rearrange("p (kvh j dh) -> p kvh j dh",
                                      kvh=c.KVH, j=J)
        for s0 in range(0, KVW, 512):
            sw = min(512, KVW - s0)
            dk = kb.dma("G", wkvb[:, 0:DC * sw].rearrange(
                "p (dc o) -> p dc o", dc=DC),
                wk[:, s0:s0 + sw].rearrange("(dc p) o -> p dc o", p=128),
                waits=[wkv_free], ch="wkv")
            last = None
            for m0 in range(0, sw, 128):
                kvh = (s0 + m0) // 128
                for tc in range(NTC):
                    bank = nb()
                    mms = [(wkvb[:, dc * sw + m0: dc * sw + m0 + 128],
                            x2T[:, dc * T + tc * 512: dc * T + tc * 512 + 512])
                           for dc in range(DC)]
                    mt = mm_group(kb, bank, bank.ap, mms, waits=[dk, x2T_done])
                    last = mt
                    kq = ki % 4
                    ev = kb.emit("V", lambda e, a=bank.ap, q=kq: e.tensor_copy(
                        kvst[q][:], a), waits=[mt] + _toks(kvst_free[kq]))
                    bank.free = ev
                    dko = kb.dma("S", kv_srcK_v[:, kvh,
                                               tc * 512: tc * 512 + 512],
                                 kvst[kq][:], waits=[ev], sbuf_dest=False,
                                 ch=f"kvo{kq}")
                    kvst_free[kq] = dko
                    ki += 1
            wkv_free = last
        for s0 in range(0, KVW, 512):
            sw = min(512, KVW - s0)
            nk = sw // 128
            dv = kb.dma("G", wkvb[:, 0:DC * sw].rearrange(
                "p (dc o) -> p dc o", dc=DC),
                wv[:, s0:s0 + sw].rearrange("(dc p) o -> p dc o", p=128),
                waits=[wkv_free], ch="wkv")
            last = None
            for t in range(J):
                bank = nb()
                mms = [(x2T[:, dc * T + t * 128: dc * T + t * 128 + 128],
                        wkvb[:, dc * sw: dc * sw + sw]) for dc in range(DC)]
                mt = mm_group(kb, bank, bank.ap[:, 0:sw], mms,
                              waits=[dv, x2T_done])
                last = mt
                kq = ki % 4
                ev = kb.emit("V", lambda e, a=bank.ap, q=kq, w=sw: e.tensor_copy(
                    kvst[q][:, 0:w], a[:, 0:w]), waits=[mt] + _toks(kvst_free[kq]))
                bank.free = ev
                dvo = kb.dma("S", kv_srcV_v[:, s0 // 128: s0 // 128 + nk,
                                            t, :],
                             kvst[kq][:, 0:sw].rearrange(
                                 "p (k dh) -> p k dh", k=nk),
                             waits=[ev], sbuf_dest=False, ch=f"kvo{kq}")
                kvst_free[kq] = dvo
                ki += 1
            wkv_free = last
        kv_out_done = [kb.tok(f"dS_kvo{i}") for i in range(4)]
        ccK = kb.emit("G", lambda e: e.collective_compute(
            "AllGather", AOT.bypass,
            replica_groups=[[0, 1], [2, 3], [4, 5], [6, 7]],
            ins=[kv_srcK], outs=[kv_dstK]), waits=kv_out_done)
        ccV = kb.emit("G", lambda e: e.collective_compute(
            "AllGather", AOT.bypass,
            replica_groups=[[0, 1], [2, 3], [4, 5], [6, 7]],
            ins=[kv_srcV], outs=[kv_dstV]), waits=[ccK])

        qkv_stack.close()
        x2_stack.close()
        kb.close_regions()

        sc_stack = stack.enter_context(ExitStack())
        kT_f = sb(sc_stack, "kT_f", [128, c.KVH * S2], BF16)
        V_f = sb(sc_stack, "V_f", [128, c.KVH * KT * 128], BF16)
        P_T = sb(sc_stack, "P_T", [128, KT * T], BF16)
        sc_buf = sb(sc_stack, "sc_buf", [128, S2])
        p_sm = sb(sc_stack, "p_sm", [128, S2], BF16)
        qTh = [sb(sc_stack, f"qTh{i}", [128, T], BF16) for i in range(2)]
        attn_h = [sb(sc_stack, f"attn_h{i}", [128, T], BF16) for i in range(2)]
        kT_f4 = kT_f[:].rearrange("p (kvh loc ccc) -> p kvh loc ccc",
                                  kvh=c.KVH, loc=KT)
        V_f4 = V_f[:].rearrange("p (kvh loc dh) -> p kvh loc dh",
                                kvh=c.KVH, loc=KT)
        kv_rd = []
        for q in range(2):
            dk = kb.dma("G", kT_f4[:, :, q * J:(q + 1) * J, :],
                        kv_dstK[q * 128:(q + 1) * 128, :].rearrange(
                            "p (kvh j ccc) -> p kvh j ccc", kvh=c.KVH, j=J),
                        waits=[ccK], ch="agk")
            dv = kb.dma("G", V_f4[:, :, q * J:(q + 1) * J, :],
                        kv_dstV[q * 128:(q + 1) * 128, :].rearrange(
                            "p (kvh j dh) -> p kvh j dh", kvh=c.KVH, j=J),
                        waits=[ccV], ch="agv")
            kv_rd += [dk, dv]
        kv_rd = [kb.tok("dG_agk"), kb.tok("dG_agv")]
        zp = kb.emit("V", lambda e: e.memset(P_T[:], 0.0))
        qT_parked = [kb.tok("dS_qtp0"), kb.tok("dS_qtp1")]
        qTh_free = [None, None]
        attn_free = [None, None]
        sm_free = [None] * J
        psm_free = None
        scb_free = None
        pt_readlast = zp
        for h in range(c.NH):
            kvh = h // c.G
            bi = h % 2
            dqh = kb.dma("Y", qTh[bi][:], qT_pk[h],
                         waits=_toks(qTh_free[bi]) + qT_parked,
                         ch=f"qtr{bi}")
            last_ev = {"V": None, "S": None}
            last_tp = None
            for qt in range(J):
                W_own = (qt + 1) * 128
                sevs = []
                for reg in range(2):
                    base = kvh * S2 + reg * T
                    dstb = reg * W_own
                    for b0 in range(0, W_own, 512):
                        bw = min(512, W_own - b0)
                        bank = nb()
                        mtS = kb.emit(
                            "T",
                            lambda e, a=bank.ap, b_=bi, q_=qt, ba=base, b0_=b0,
                            w=bw: e.matmul(
                                a[:, 0:w], qTh[b_][:, q_ * 128: q_ * 128 + 128],
                                kT_f[:, ba + b0_: ba + b0_ + w],
                                start=True, stop=True),
                            waits=_toks(bank.free) + [dqh] + kv_rd)
                        ev = kb.emit(
                            "S",
                            lambda e, a=bank.ap, d0=dstb + b0, w=bw: e.copy(
                                sc_buf[:, d0: d0 + w], a[:, 0:w]),
                            waits=[mtS, scb_free])
                        bank.free = ev
                        sevs.append(ev)
                mk0 = kb.emit("V", lambda e, q_=qt: e.tensor_tensor(
                    sc_buf[:, q_ * 128: q_ * 128 + 128],
                    sc_buf[:, q_ * 128: q_ * 128 + 128], mk0_s[:], AOT.add),
                    waits=sevs + [ld_mk0])
                mk1 = kb.emit("V", lambda e, q_=qt, w=W_own: e.tensor_tensor(
                    sc_buf[:, w + q_ * 128: w + q_ * 128 + 128],
                    sc_buf[:, w + q_ * 128: w + q_ * 128 + 128], mk1_s[:],
                    AOT.add), waits=sevs + [ld_mk1])
                W2 = 2 * W_own
                rmx = kb.emit("V", lambda e, q_=qt, w=W2: e.tensor_reduce(
                    st(q_, C_SMX), sc_buf[:, 0:w], AXX, AOT.max),
                    waits=[mk0, mk1] + _toks(sm_free[qt]))
                ng = kb.emit("V", lambda e, q_=qt: e.tensor_scalar(
                    st(q_, C_SNEG), st(q_, C_SMX), -1.0, None, AOT.mult),
                    waits=[rmx])
                ex = kb.emit("S", lambda e, q_=qt, w=W2: e.activation(
                    sc_buf[:, 0:w], sc_buf[:, 0:w], AFT.Exp,
                    bias=st(q_, C_SNEG), accum_out=st(q_, C_SSUM)),
                    waits=[ng])
                scb_free = ex
                rc = kb.emit("V", lambda e, q_=qt: e.reciprocal(
                    st(q_, C_SRC), st(q_, C_SSUM)), waits=[ex])
                nm = kb.emit("V", lambda e, q_=qt, w=W2: e.tensor_scalar(
                    p_sm[:, 0:w], sc_buf[:, 0:w], st(q_, C_SRC), None,
                    AOT.mult), waits=[rc, psm_free])
                sm_free[qt] = nm
                for i in range(2 * (qt + 1)):
                    loc = i if i <= qt else J + (i - qt - 1)
                    ttok, evt = tp.run(
                        p_sm[:, i * 128: i * 128 + 128],
                        P_T[:, loc * T + qt * 128: loc * T + qt * 128 + 128],
                        waits=[nm, pt_readlast],
                        evac=("V" if i % 2 else "S"))
                    last_ev["V" if i % 2 else "S"] = evt
                    last_tp = ttok
                psm_free = last_tp
            aevs = []
            for qc in range(NTC):
                qtmax = 4 * qc + 3
                bank = nb()
                mms = []
                for loc0 in (0, J):
                    for jj in range(qtmax + 1):
                        loc = loc0 + jj
                        mms.append((
                            V_f[:, kvh * KT * 128 + loc * 128:
                                kvh * KT * 128 + loc * 128 + 128],
                            P_T[:, loc * T + qc * 512: loc * T + qc * 512 + 512]))
                mt = mm_group(kb, bank, bank.ap, mms,
                              waits=_toks(last_ev["V"]) + _toks(last_ev["S"])
                              + kv_rd)
                pt_readlast = mt
                ev = kb.emit("V", lambda e, a=bank.ap, b_=bi, q_=qc:
                             e.tensor_copy(
                                 attn_h[b_][:, q_ * 512: q_ * 512 + 512], a),
                             waits=[mt] + _toks(attn_free[bi]))
                bank.free = ev
                aevs.append(ev)
            apk = kb.dma("S", att_pk[h], attn_h[bi][:], waits=aevs,
                         sbuf_dest=False, ch=f"atw{bi}")
            attn_free[bi] = apk
            qTh_free[bi] = pt_readlast

        sc_stack.close()
        kb.close_regions()

        dn_stack = stack.enter_context(ExitStack())
        wdb = [sb(dn_stack, f"wdb{i}", [128, c.NH * 512], BF16) for i in range(2)]
        attn_t = [sb(dn_stack, f"attn_t{i}", [128, c.NH * 128], BF16)
                  for i in range(2)]
        pst2 = [sb(dn_stack, f"pst2{i}", [128, 512]) for i in range(4)]
        att_done = [kb.tok("dS_atw0"), kb.tok("dS_atw1")]
        wdb_free = [None, None]
        attn_t_free = [None, None]
        pst2_free = [None] * 4
        p2i = 0
        for ds in range(c.DSL):
            wi = ds % 2
            dwd = kb.dma("G", wdb[wi][:].rearrange("p (h o) -> p h o", h=c.NH),
                         w_denseT[ds].rearrange("(h p) o -> p h o", p=128),
                         waits=_toks(wdb_free[wi]), ch=f"wdd{wi}")
            mt = None
            for qt in range(J):
                ai = qt % 2
                da = kb.dma("Y", attn_t[ai][:].rearrange(
                    "p (h q) -> p h q", h=c.NH),
                    att_pk[:, :, qt * 128:(qt + 1) * 128].rearrange(
                        "h p q -> p h q"),
                    waits=_toks(attn_t_free[ai]) + att_done, ch=f"atr{ai}")
                bank = nb()
                mms = [(attn_t[ai][:, hh * 128: hh * 128 + 128],
                        wdb[wi][:, hh * 512: hh * 512 + 512])
                       for hh in range(c.NH)]
                mt = mm_group(kb, bank, bank.ap, mms, waits=[da, dwd])
                attn_t_free[ai] = mt
                q2 = p2i % 4
                ev = kb.emit("V", lambda e, a=bank.ap, q_=q2: e.tensor_copy(
                    pst2[q_][:], a), waits=[mt] + _toks(pst2_free[q2]))
                bank.free = ev
                pk = kb.dma("S", v3_park[qt][:, ds * 512:(ds + 1) * 512],
                            pst2[q2][:], waits=[ev], sbuf_dest=False,
                            ch=f"v3p{q2}")
                pst2_free[q2] = pk
                p2i += 1
            wdb_free[wi] = mt

        dn_stack.close()
        kb.close_regions()

        f_outer = stack.enter_context(ExitStack())
        x3T = sb(f_outer, "x3T", [128, DC * T], BF16)
        rwinb = sb(f_outer, "rwinb", [128, T])
        qfm = sb(f_outer, "qfm", [128, DC])
        yT = sb(f_outer, "yT", [128, T], BF16)
        BsT = sb(f_outer, "BsT", [128, D], BF16)
        ln3_stack = stack.enter_context(ExitStack())
        v3rd = sb(ln3_stack, "v3rd", [128, D])
        x2rd = sb(ln3_stack, "x2rd", [128, D])
        x3t = sb(ln3_stack, "x3t", [128, D])
        x3b = sb(ln3_stack, "x3b", [128, D], BF16)
        sqd3 = sb(ln3_stack, "sqd3", [128, D])
        v3_done = [kb.tok(f"dS_v3p{i}") for i in range(4)]
        x2pk_done = kb.tok("dS_x2p")
        v3rd_free = None
        x2rd_free = None
        x3t_free = None
        x3b_free = None
        x3_done = [None] * J
        x3T_ev = None
        for t in range(J):
            d1 = kb.dma("Y", v3rd[:], v3_park[t],
                        waits=_toks(v3rd_free) + v3_done, ch="v3r")
            d2 = kb.dma("G", x2rd[:], x2_park[t],
                        waits=_toks(x2rd_free) + [x2pk_done], ch="x2r")
            vv = kb.emit("V", lambda e: e.tensor_add(v3rd[:], v3rd[:], x2rd[:]),
                         waits=[d1, d2])
            x2rd_free = vv
            apt, asr = ln_apply(t, v3rd[:], x3t[:], sqd3[:],
                                [vv] + _toks(x3t_free))
            v3rd_free = [apt, asr]
            x3_done[t] = kb.dma("S", x3_park[t], x3t[:], waits=[apt],
                                sbuf_dest=False, ch="x3p")
            cb = kb.emit("V", lambda e: e.tensor_copy(x3b[:], x3t[:]),
                         waits=[apt] + _toks(x3b_free))
            x3t_free = [cb, x3_done[t]]
            ttok = None
            for dc in range(DC):
                ttok, x3T_ev = tp.run(
                    x3b[:, dc * 128: dc * 128 + 128],
                    x3T[:, dc * T + t * 128: dc * T + t * 128 + 128],
                    waits=[cb], evac="S")
            x3b_free = ttok

        if dbg == "x3":
            kb.dma("Y", dbgs, stat[:, 0:32 * J], waits=[kb.tok("V"), kb.tok("S")],
                   sbuf_dest=False, ch="out")
            for qt in range(J):
                kb.dma("Y", outv[qt], x3_park[qt], waits=[x3_done[qt]],
                       sbuf_dest=False, ch="out")
            kb.finish()
            with nc.Block() as block:
                kb.install(block)
            return nc

        ln3_stack.close()
        kb.close_regions()

        rt_stack = stack.enter_context(ExitStack())
        qtmp = sb(rt_stack, "qtmp", [128, T])
        rwt_s = sb(rt_stack, "rwt_s", [128, DC * 8])
        lg = sb(rt_stack, "lg", [1, 32])
        rbs = sb(rt_stack, "rbs", [1, 8])
        rtc = sb(rt_stack, "rtc", [8, 1])
        rsc_col = sb(rt_stack, "rsc_col", [128, 1])
        bt_s = sb(rt_stack, "bt_s", [128, D])
        atb = sb(rt_stack, "atb", [128, DC * 128], BF16)
        rb_ev = None
        for tc in range(NTC):
            bank = nb()
            bmt = kb.emit("T", lambda e, a=bank.ap, t=tc: e.matmul(
                a, oner_s[:], rwr_s[0:1, t * 512: t * 512 + 512],
                start=True, stop=True),
                waits=_toks(bank.free) + [ld_oner, ld_rwr])
            rb_ev = kb.emit("V", lambda e, a=bank.ap, t=tc: e.tensor_copy(
                rwinb[:, t * 512: t * 512 + 512], a), waits=[bmt])
            bank.free = rb_ev
        qtmp_free = None
        qs_last = None
        for dc in range(DC):
            v1 = kb.emit("V", lambda e, d_=dc: e.tensor_tensor(
                qtmp[:], x3T[:, d_ * T: (d_ + 1) * T], rwinb[:], AOT.mult),
                waits=[x3T_ev, rb_ev, qtmp_free])
            qs_last = kb.emit("V", lambda e, d_=dc: e.reduce_sum(
                qfm[:, d_:d_ + 1], qtmp[:], AXX), waits=[v1])
            qtmp_free = qs_last
        dq1 = kb.dma("G", q_src.rearrange("(p dc) -> p dc", p=128), qfm[:],
                     waits=[qs_last], sbuf_dest=False, ch="qsr")
        ccq = kb.emit("G", lambda e: e.collective_compute(
            "AllReduce", AOT.add,
            replica_groups=[[0, 1], [2, 3], [4, 5], [6, 7]],
            ins=[q_src], outs=[q_dst]), waits=[dq1])
        dq2 = kb.dma("G", qfm[:], q_dst.rearrange("(p dc) -> p dc", p=128),
                     waits=[ccq], ch="qsr")
        ld_rwt = kb.dma("Y", rwt_s[:].rearrange("p (dc e) -> p dc e", dc=DC),
                        rW_T.rearrange("(dc p) e -> p dc e", p=128), ch="rwl")
        bank = nb()
        lg_mms = [(qfm[:, dc:dc + 1], rwt_s[:, dc * 8: dc * 8 + 8])
                  for dc in range(DC)]
        lmt = mm_group(kb, bank, bank.ap[0:1, 0:8], lg_mms,
                       waits=[dq2, ld_rwt])
        ld_rb = kb.dma("Y", rbs[:], rb_row, ch="rbl")
        l0 = kb.emit("V", lambda e, a=bank.ap: e.tensor_copy(
            lg[0:1, 0:8], a[0:1, 0:8]), waits=[lmt])
        bank.free = l0
        l1 = kb.emit("V", lambda e: e.tensor_scalar(
            lg[0:1, 0:8], lg[0:1, 0:8], 1.0 / c.WCNT, None, AOT.mult),
            waits=[l0])
        l2 = kb.emit("V", lambda e: e.tensor_tensor(
            lg[0:1, 0:8], lg[0:1, 0:8], rbs[0:1, :], AOT.add),
            waits=[l1, ld_rb])
        l3 = kb.emit("V", lambda e: e.tensor_reduce(
            lg[0:1, 8:9], lg[0:1, 0:8], AXX, AOT.max), waits=[l2])
        l4 = kb.emit("V", lambda e: e.tensor_scalar(
            lg[0:1, 9:10], lg[0:1, 8:9], -1.0, None, AOT.mult), waits=[l3])
        l5 = kb.emit("S", lambda e: e.activation(
            lg[0:1, 16:24], lg[0:1, 0:8], AFT.Exp, bias=lg[0:1, 9:10],
            accum_out=lg[0:1, 10:11]), waits=[l4])
        l6 = kb.emit("V", lambda e: e.reciprocal(
            lg[0:1, 11:12], lg[0:1, 10:11]), waits=[l5])
        l7 = kb.emit("V", lambda e: e.tensor_scalar(
            lg[0:1, 24:32], lg[0:1, 16:24], lg[0:1, 11:12], None, AOT.mult),
            waits=[l6])
        d3 = kb.dma("S", rt_d.rearrange("(o e) -> o e", o=1), lg[0:1, 24:32],
                    waits=[l7], sbuf_dest=False, ch="rtd")
        d4 = kb.dma("Y", rtc[:], rt_d.rearrange("(e o) -> e o", o=1),
                    waits=[d3], ch="rtd2")
        bank = nb()
        rmt = kb.emit("T", lambda e, a=bank.ap: e.matmul(
            a[:, 0:1], repm_s[:], rtc[:], start=True, stop=True),
            waits=_toks(bank.free) + [d4, ld_repm])
        rsc_ev = kb.emit("V", lambda e, a=bank.ap: e.tensor_copy(
            rsc_col[:], a[:, 0:1]), waits=[rmt])
        bank.free = rsc_ev
        ld_bt = kb.dma("Y", bt_s[:], B_T, ch="btl")
        bse = kb.emit("V", lambda e: e.tensor_scalar(
            BsT[:], bt_s[:], rsc_col[:, 0:1], None, AOT.mult),
            waits=[ld_bt, rsc_ev])
        da_ = kb.dma("G", atb[:].rearrange("p (dc r) -> p dc r", dc=DC),
                     A_T.rearrange("(dc p) r -> p dc r", p=128), ch="atl")
        y_ev = None
        for tc in range(NTC):
            bank = nb()
            mms = [(atb[:, dc * 128: dc * 128 + 128],
                    x3T[:, dc * T + tc * 512: dc * T + tc * 512 + 512])
                   for dc in range(DC)]
            mt = mm_group(kb, bank, bank.ap, mms, waits=[da_, x3T_ev])
            y_ev = kb.emit("V", lambda e, a=bank.ap, t=tc: e.tensor_copy(
                yT[:, t * 512: t * 512 + 512], a), waits=[mt, y_ev])
            bank.free = y_ev

        rt_stack.close()
        kb.close_regions()

        fb_stack = stack.enter_context(ExitStack())
        wfb = [sb(fb_stack, f"wfb{i}", [128, DC * 512], BF16) for i in range(2)]
        fst = [sb(fb_stack, f"fst{i}", [128, 512]) for i in range(4)]
        wfb_free = [None, None]
        fst_free = [None] * 4
        fi = 0
        for ds in range(c.DSL):
            wi = ds % 2
            dwf = kb.dma("G", wfb[wi][:].rearrange("p (dc o) -> p dc o", dc=DC),
                         WT[ds].rearrange("(dc p) o -> p dc o", p=128),
                         waits=_toks(wfb_free[wi]), ch=f"wff{wi}")
            mt = None
            for t in range(J):
                bank = nb()
                mms = [(x3T[:, dc * T + t * 128: dc * T + t * 128 + 128],
                        wfb[wi][:, dc * 512:(dc + 1) * 512])
                       for dc in range(DC)]
                mms.append((yT[:, t * 128: t * 128 + 128],
                            BsT[:, ds * 512:(ds + 1) * 512]))
                mt = mm_group(kb, bank, bank.ap, mms,
                              waits=[dwf, x3T_ev, y_ev, bse])
                q3_ = fi % 4
                ev = kb.emit("V", lambda e, a=bank.ap, q_=q3_: e.tensor_copy(
                    fst[q_][:], a), waits=[mt] + _toks(fst_free[q3_]))
                bank.free = ev
                pk = kb.dma("S", b4_park[t][:, ds * 512:(ds + 1) * 512],
                            fst[q3_][:], waits=[ev], sbuf_dest=False,
                            ch=f"b4p{q3_}")
                fst_free[q3_] = pk
                fi += 1
            wfb_free[wi] = mt

        fb_stack.close()
        f_outer.close()
        kb.close_regions()

        ln4_stack = stack.enter_context(ExitStack())
        bb = sb(ln4_stack, "bb", [128, D])
        brs = sb(ln4_stack, "brs", [1, D])
        b4rd = [sb(ln4_stack, f"b4rd{i}", [128, D]) for i in range(2)]
        x3rd = [sb(ln4_stack, f"x3rd{i}", [128, D]) for i in range(2)]
        outt = [sb(ln4_stack, f"outt{i}", [128, D]) for i in range(2)]
        sqd4 = sb(ln4_stack, "sqd4", [128, D])
        ld_br = kb.dma("Y", brs[:], b_row, ch="brl")
        bb_ev = None
        for ds2 in range(c.DSL):
            bank = nb()
            bmt = kb.emit("T", lambda e, a=bank.ap, d_=ds2: e.matmul(
                a, oner_s[:], brs[0:1, d_ * 512: d_ * 512 + 512],
                start=True, stop=True),
                waits=_toks(bank.free) + [ld_br, ld_oner])
            bb_ev = kb.emit("V", lambda e, a=bank.ap, d_=ds2: e.tensor_copy(
                bb[:, d_ * 512: d_ * 512 + 512], a), waits=[bmt, bb_ev])
            bank.free = bb_ev
        b4_done = [kb.tok(f"dS_b4p{i}") for i in range(4)]
        x3pk_done = kb.tok("dS_x3p")
        b4rd_free = [None, None]
        x3rd_free = [None, None]
        outt_free = [None, None]
        for t in range(J):
            bi = t % 2
            d1 = kb.dma("Y", b4rd[bi][:], b4_park[t],
                        waits=_toks(b4rd_free[bi]) + b4_done, ch=f"b4r{bi}")
            d2 = kb.dma("G", x3rd[bi][:], x3_park[t],
                        waits=_toks(x3rd_free[bi]) + [x3pk_done], ch=f"x3r{bi}")
            v1 = kb.emit("V", lambda e, b=bi: e.tensor_add(
                b4rd[b][:], b4rd[b][:], bb[:]), waits=[d1, bb_ev])
            v2 = kb.emit("V", lambda e, b=bi: e.tensor_add(
                b4rd[b][:], b4rd[b][:], x3rd[b][:]), waits=[v1, d2])
            x3rd_free[bi] = v2
            apt, asr = ln_apply(t, b4rd[bi][:], outt[bi][:], sqd4[:],
                                [v2] + _toks(outt_free[bi]))
            b4rd_free[bi] = [apt, asr]
            od = kb.dma("Y", outv[t], outt[bi][:], waits=[apt],
                        sbuf_dest=False, ch=f"out{bi}")
            outt_free[bi] = od

        kb.finish()
        with nc.Block() as block:
            kb.install(block)
    return nc


def kernel(**inputs):
    cfg = FULL
    in_maps, metas = _host_prepare(cfg, **inputs)
    nc = build_nc(cfg)
    res = run_bass_kernel_spmd(nc, in_maps, core_ids=list(range(8)))
    x = np.asarray(inputs["L0_attn_x"])
    outp = np.zeros_like(x)
    for r, m in zip(res.results, metas):
        outp[m["b"]][m["own"]] = r["out"]
    return outp


# version 05
# speedup vs baseline: 1.1536x; 1.1536x over previous
from contextlib import ExitStack

import numpy as np
import ml_dtypes

import concourse.bass as bass
import concourse.mybir as mybir
from concourse.bass_utils import run_bass_kernel_spmd

F32 = mybir.dt.float32
BF16 = mybir.dt.bfloat16
F16 = mybir.dt.float16
AOT = mybir.AluOpType
AFT = mybir.ActivationFunctionType
AXX = mybir.AxisListType.X

MAGIC = 12582912.0
NEG = -1.0e30
LN_EPS = 1e-5
Q_START, Q_END = 611, 2047


class Cfg:
    def __init__(self, D=4096, F=16384, S=2048, NH=32, KVH=8):
        self.D, self.F, self.S, self.NH, self.KVH = D, F, S, NH, KVH
        self.G = NH // KVH
        self.DC = D // 128
        self.FC = F // 128
        self.J = S // 256
        self.T = 128 * self.J
        self.DSL = D // 512
        self.UPSL = F // 512
        self.FT = F // 128
        self.NFB = F // 2048
        self.WCNT = S - 1 - 611


FULL = Cfg()
TINY = Cfg(D=512, F=2048, S=1024, NH=8, KVH=2)



def _host_prepare(cfg, L0_attn_x, L0_attn_gamma, L0_ffn_w_up, L0_ffn_w_down,
                  L1_attn_w_qkv, L1_attn_w_dense, L1_ffn_W, L1_ffn_b, L1_ffn_A,
                  L1_ffn_B, L1_ffn_router_W, L1_ffn_router_b):
    c = cfg
    f32 = np.float32

    def t(a):
        return np.ascontiguousarray(np.asarray(a).T)

    FTB = 16
    w_upT = np.ascontiguousarray(
        np.asarray(L0_ffn_w_up, f32).reshape(c.FT, 128, c.DC, 128)
        .transpose(0, 3, 2, 1).reshape(c.FT, 128, c.D))
    w_downT = np.ascontiguousarray(
        np.asarray(L0_ffn_w_down, f32).reshape(c.DSL, 512, c.NFB, FTB, 128)
        .transpose(2, 0, 4, 3, 1).reshape(c.NFB, c.DSL, 128, FTB * 512))
    w_qkvT = t(L1_attn_w_qkv)
    wq = np.ascontiguousarray(
        w_qkvT[:, : c.NH * 128].reshape(c.DC, 128, c.NH, 128)
        .transpose(2, 1, 0, 3).reshape(c.NH, 128, c.D))
    KVW = c.KVH * 128
    SW = min(512, KVW)
    NS = KVW // SW
    wk = np.ascontiguousarray(
        w_qkvT[:, c.NH * 128: (c.NH + c.KVH) * 128].reshape(c.DC, 128, NS, SW)
        .transpose(2, 1, 0, 3).reshape(NS, 128, c.DC * SW))
    wv = np.ascontiguousarray(
        w_qkvT[:, (c.NH + c.KVH) * 128:].reshape(c.DC, 128, NS, SW)
        .transpose(2, 1, 0, 3).reshape(NS, 128, c.DC * SW))
    w_denseT = np.ascontiguousarray(
        t(L1_attn_w_dense).reshape(c.NH, 128, c.DSL, 512)
        .transpose(2, 1, 0, 3).reshape(c.DSL, 128, c.NH * 512))
    WT = np.ascontiguousarray(
        t(L1_ffn_W).reshape(c.DC, 128, c.DSL, 512)
        .transpose(2, 1, 0, 3).reshape(c.DSL, 128, c.DC * 512))
    b_row = np.asarray(L1_ffn_b, f32).reshape(1, c.D)
    ER = np.asarray(L1_ffn_A).shape[0] * np.asarray(L1_ffn_A).shape[1]
    A_T = np.ascontiguousarray(
        t(np.asarray(L1_ffn_A).reshape(ER, c.D)).reshape(c.DC, 128, ER)
        .transpose(1, 0, 2).reshape(128, c.DC * ER))
    B_T = np.ascontiguousarray(
        np.transpose(np.asarray(L1_ffn_B), (0, 2, 1)).reshape(ER, c.D))
    rW_T = np.ascontiguousarray(
        t(L1_ffn_router_W).reshape(c.DC, 128, -1)
        .transpose(1, 0, 2).reshape(128, -1))
    rb_row = np.asarray(L1_ffn_router_b, f32).reshape(1, -1)
    gamma_b = np.ascontiguousarray(
        np.broadcast_to(np.asarray(L0_attn_gamma).reshape(1, 1), (128, 1)).astype(f32))
    eye = np.eye(128, dtype=ml_dtypes.bfloat16)
    eye32 = np.eye(128, dtype=f32)
    ones_col = np.ones((128, 1), f32)
    ones_row = np.ones((1, 128), f32)
    consts_v = np.ascontiguousarray(np.broadcast_to(
        np.array([[LN_EPS, MAGIC, 0.0625]], f32), (128, 3)))
    tri = np.where(np.tril(np.ones((128, 128), bool)), 0.0, NEG).astype(f32)
    negf = np.full((128, 128), NEG, f32)
    zf = np.zeros((128, 128), f32)
    repm = np.zeros((8, 128), f32)
    for e in range(8):
        repm[e, e * 16:(e + 1) * 16] = 2.0

    x = np.asarray(L0_attn_x)
    in_maps, metas = [], []
    for core in range(8):
        b, p = core // 2, core % 2
        own = np.concatenate([np.arange(128 * (2 * j + p), 128 * (2 * j + p) + 128)
                              for j in range(c.J)])
        par = np.concatenate([np.arange(128 * (2 * j + 1 - p), 128 * (2 * j + 1 - p) + 128)
                              for j in range(c.J)])
        xbc = np.ascontiguousarray(x[b][np.concatenate([own, par])])
        mask0 = tri if p == 0 else zf
        mask1 = negf if p == 0 else tri
        rwin_row = np.zeros((1, c.T), f32)
        for j in range(c.J):
            g = 128 * (2 * j + p) + np.arange(128)
            rwin_row[0, j * 128:(j + 1) * 128] = (
                (g >= Q_START) & (g < c.S - 1)).astype(f32)
        nsh = c.D * c.UPSL // 8
        in_maps.append(dict(
            xb=xbc, gamma_b=gamma_b, eye=eye, eye32=eye32, ones_col=ones_col,
            ones_row=ones_row, consts_v=consts_v,
            mask0=mask0, mask1=mask1, rwin_row=rwin_row, repm=repm,
            w_upT=w_upT, w_downT=w_downT, wq=wq, wk=wk, wv=wv, w_denseT=w_denseT,
            WT=WT, b_row=b_row, A_T=A_T, B_T=B_T, rW_T=rW_T, rb_row=rb_row,
            wup_shard=np.ascontiguousarray(w_upT.reshape(-1, 512)[core * nsh:(core + 1) * nsh]),
            wdn_shard=np.ascontiguousarray(
                w_downT.reshape(-1, c.D)[core * (c.F // 8):(core + 1) * (c.F // 8)]),
        ))
        metas.append(dict(b=b, own=own))
    return in_maps, metas



class KB:
    ENGS = ("T", "V", "S", "G", "Y")

    def __init__(self, nc, stack):
        self.nc = nc
        self.stack = stack
        self.ops = {k: [] for k in self.ENGS}
        self.cnt = {}
        self.sems = {}
        self._waited = {k: {} for k in self.ENGS}
        self.region_toks = []

    def _sem(self, name):
        if name not in self.sems:
            self.sems[name] = self.stack.enter_context(self.nc.semaphore(name))
            self.cnt[name] = 0
        return self.sems[name]

    def wait(self, eng, tok):
        if tok is None:
            return
        name, val = tok
        if val <= 0:
            return
        sem = self._sem(name)
        assert val <= self.cnt[name], (eng, name, val, self.cnt[name])
        if self._waited[eng].get(name, 0) >= val:
            return
        self._waited[eng][name] = val
        self.ops[eng].append(lambda e, s=sem, v=val: e.wait_ge(s, v))

    def emit(self, eng, fn, waits=(), inc=True):
        for tk in waits:
            self.wait(eng, tk)
        if inc:
            sem = self._sem(eng)
            self.cnt[eng] += 1
            tok = (eng, self.cnt[eng])
            self.ops[eng].append(lambda e, f=fn, s=sem: f(e).then_inc(s, 1))
            return tok
        self.ops[eng].append(fn)
        return None

    def dma(self, q, out, in_, waits=(), sbuf_dest=True, ch="g", **kw):
        assert q in ("Y", "S", "G")
        w = list(waits) + (self.region_toks if sbuf_dest else [])
        for tk in w:
            self.wait(q, tk)
        name = "d" + q + "_" + ch
        sem = self._sem(name)
        self.cnt[name] += 16
        tok = (name, self.cnt[name])
        self.ops[q].append(lambda e, o=out, i=in_, k=dict(kw), s=sem:
                           e.dma_start(out=o, in_=i, **k).then_inc(s, 16))
        return tok

    def tok(self, name):
        v = self.cnt.get(name, 0)
        return (name, v) if v else None

    def close_regions(self):
        toks = [(k, self.cnt.get(k, 0)) for k in ("T", "V", "S", "G")]
        toks = [t for t in toks if t[1] > 0]
        dtoks = [(k, v) for k, v in self.cnt.items()
                 if k.startswith("d") and v > 0]
        for k in ("T", "V", "S", "G"):
            for t in toks:
                if t[0] != k:
                    self.wait(k, t)
            for t in dtoks:
                self.wait(k, t)
        self.region_toks = toks + dtoks

    def finish(self):
        for q, v in sorted(self.cnt.items()):
            if q.startswith("d") and v:
                self.wait("Y", (q, v))

    def install(self, block):
        attr = {"T": "tensor", "V": "vector", "S": "scalar", "G": "gpsimd", "Y": "sync"}
        for k, a in attr.items():
            if not self.ops[k]:
                continue
            def body(e, ops=self.ops[k]):
                for f in ops:
                    f(e)
            getattr(block, a)(body)


class Bank:
    def __init__(self, ap):
        self.ap = ap
        self.free = None


def _toks(x):
    if x is None:
        return []
    if isinstance(x, list):
        return [t for t in x if t is not None]
    return [x]


def mm_group(kb, bank, out_ap, mms, waits=()):
    n = len(mms)
    first = _toks(bank.free) + list(waits)
    tok = None
    for i, (l, r) in enumerate(mms):
        tok = kb.emit(
            "T",
            lambda e, o=out_ap, l=l, r=r, st=(i == 0), sp=(i == n - 1):
                e.matmul(o, l, r, start=st, stop=sp),
            waits=first if i == 0 else (), inc=(i == n - 1))
    return tok


class TP:

    def __init__(self, kb, banks, eye_ap):
        self.kb = kb
        self.banks = banks
        self.eye = eye_ap
        self.i = 0

    def run(self, src_ap, dst_ap, waits=(), evac="V"):
        bank = self.banks[self.i % len(self.banks)]
        self.i += 1
        t = self.kb.emit("T", lambda e, o=bank.ap, s=src_ap: e.transpose(o, s, self.eye),
                         waits=_toks(bank.free) + list(waits))
        if evac == "V":
            fn = lambda e, d=dst_ap, s=bank.ap: e.tensor_copy(d, s)
        else:
            fn = lambda e, d=dst_ap, s=bank.ap: e.copy(d, s)
        ev = self.kb.emit(evac, fn, waits=[t])
        bank.free = ev
        return t, ev



def build_nc(cfg, dbg=None):
    c = cfg
    J, DC, FC, D, F, T = c.J, c.DC, c.FC, c.D, c.F, c.T
    S2 = 2 * T
    KT = 2 * J
    nc = bass.Bass("TRN2", target_bir_lowering=False, debug=False)

    dt = lambda n, sh, ty=F32, kind="ExternalInput": nc.dram_tensor(n, sh, ty, kind=kind).ap()
    di = lambda n, sh, ty=F32: nc.dram_tensor(n, sh, ty).ap()

    xb = dt("xb", [S2, D])
    gamma_b = dt("gamma_b", [128, 1])
    eye = dt("eye", [128, 128], BF16)
    ones_col = dt("ones_col", [128, 1])
    ones_row = dt("ones_row", [1, 128])
    consts_v = dt("consts_v", [128, 3])
    mask0 = dt("mask0", [128, 128])
    mask1 = dt("mask1", [128, 128])
    rwin_row = dt("rwin_row", [1, T])
    eye32 = dt("eye32", [128, 128])
    repm = dt("repm", [8, 128])
    FTB0 = 16
    KVW0 = c.KVH * 128
    SW0 = min(512, KVW0)
    NS0 = KVW0 // SW0
    w_upT = dt("w_upT", [c.FT, 128, D])
    w_downT = dt("w_downT", [c.NFB, c.DSL, 128, FTB0 * 512])
    wq = dt("wq", [c.NH, 128, D])
    wk = dt("wk", [NS0, 128, DC * SW0])
    wv = dt("wv", [NS0, 128, DC * SW0])
    w_denseT = dt("w_denseT", [c.DSL, 128, c.NH * 512])
    WT = dt("WT", [c.DSL, 128, DC * 512])
    b_row = dt("b_row", [1, D])
    A_T = dt("A_T", [128, DC * 128])
    B_T = dt("B_T", [128, D])
    rW_T = dt("rW_T", [128, DC * 8])
    rb_row = dt("rb_row", [1, 8])
    nsh = c.D * c.UPSL // 8
    wup_shard = dt("wup_shard", [nsh, 512])
    wdn_shard = dt("wdn_shard", [F // 8, D])
    out = dt("out", [T, D], F32, "ExternalOutput")
    outv = out.rearrange("(t p) d -> t p d", p=128)

    dbgs = dt("dbgs", [128, 32 * J], F32, "ExternalOutput") if dbg else None
    dbge = dt("dbge", [128, 4096], F32, "ExternalOutput") if dbg else None
    v_park = di("v_park", [J, 128, D])
    x1_park = di("x1_park", [J, 128, D])
    sc_src = di("sc_src", [1, 16])
    sc_dst = di("sc_dst", [1, 16])
    rh_pk = di("rh_pk", [c.FT, 128, T], F16)
    pdn = di("pdn", [c.NFB, J, 128, D])
    x2_park = di("x2_park", [J, 128, D])
    v3_park = di("v3_park", [J, 128, D])
    x3_park = di("x3_park", [J, 128, D])
    b4_park = di("b4_park", [J, 128, D])
    qT_pk = di("qT_pk", [c.NH, 128, T], BF16)
    att_pk = di("att_pk", [J, 128, c.NH * 128], BF16)
    kv_srcK = di("kv_srcK", [128, c.KVH * T], BF16)
    kv_srcV = di("kv_srcV", [128, c.KVH * T], BF16)
    kv_dstK = di("kv_dstK", [256, c.KVH * T], BF16)
    kv_dstV = di("kv_dstV", [256, c.KVH * T], BF16)
    q_src = di("q_src", [D])
    q_dst = di("q_dst", [D])
    rt_d = di("rt_d", [8])
    s2_d = di("s2_d", [T])

    xbv = xb.rearrange("(n p) d -> n p d", p=128)

    stack = ExitStack()
    kb = KB(nc, stack)

    def sb(st_, n, sh, ty=F32, side="left"):
        return st_.enter_context(nc.sbuf_tensor(n, sh, ty, side=side))

    def psb(n, sh, ty=F32):
        return Bank(stack.enter_context(nc.psum_tensor(n, sh, ty))[:])

    with stack:
        eye_s = sb(stack, "eye_s", [128, 128], BF16, side="right")
        mk0_s = sb(stack, "mk0_s", [128, 128], side="right")
        mk1_s = sb(stack, "mk1_s", [128, 128], side="right")
        gam_s = sb(stack, "gam_s", [128, 1], side="right")
        onec_s = sb(stack, "onec_s", [128, 1], side="right")
        oner_s = sb(stack, "oner_s", [1, 128], side="right")
        cv_s = sb(stack, "cv_s", [128, 3], side="right")
        eye32_s = sb(stack, "eye32_s", [128, 128], side="right")
        repm_s = sb(stack, "repm_s", [8, 128], side="right")
        rwr_s = sb(stack, "rwr_s", [1, T], side="right")
        stat = sb(stack, "stat", [128, 32 * J], side="right")
        sqp = sb(stack, "sqp", [128, J * c.DSL], side="right")
        sc_s = sb(stack, "sc_s", [1, 16], side="right")
        scb_s = sb(stack, "scb_s", [128, 8], side="right")

        ld_eye = kb.dma("Y", eye_s[:], eye, ch="c0")
        ld_mk0 = kb.dma("Y", mk0_s[:], mask0, ch="c1")
        ld_mk1 = kb.dma("Y", mk1_s[:], mask1, ch="c2")
        ld_gam = kb.dma("Y", gam_s[:], gamma_b, ch="c3")
        ld_one = kb.dma("Y", onec_s[:], ones_col, ch="c4")
        ld_oner = kb.dma("Y", oner_s[:], ones_row, ch="c7")
        ld_cv = kb.dma("Y", cv_s[:], consts_v, ch="c5")
        ld_eye32 = kb.dma("Y", eye32_s[:], eye32, ch="c6")
        ld_repm = kb.dma("Y", repm_s[:], repm, ch="c8")
        ld_rwr = kb.dma("Y", rwr_s[:], rwin_row, ch="c9")
        z_stat = kb.emit("V", lambda e: e.memset(stat[:], 0.0))
        z_sqp = kb.emit("V", lambda e: e.memset(sqp[:], 0.0))

        C_MIN, C_ESUM, C_PVS = 0, 1, 2
        C_SUM, C_MU, C_VAR, C_RST, C_NMU = 3, 4, 5, 6, 7
        C_AM1, C_S1, C_IS1 = 8, 9, 10
        C_AM2, C_S2, C_C2 = 11, 12, 13
        C_T0, C_T1, C_T2, C_T3 = 14, 15, 16, 17
        C_C1, C_HMX, C_S2P, C_R2M = 18, 19, 20, 21

        def st(j, col, n=1):
            return stat[:, 32 * j + col:32 * j + col + n]

        mmb = [psb(f"mm{i}", [128, 512]) for i in range(6)]
        tpb = [psb("tp0", [128, 1024], BF16), psb("tp1", [128, 1024], BF16)]
        tp = TP(kb, [Bank(b.ap[:, 0:128]) for b in tpb], eye_s[:])

        w1_stack = stack.enter_context(ExitStack())
        accw = sb(w1_stack, "accw", [128, 2])
        shard_b = [sb(w1_stack, f"shard_b{i}", [128, 4096], BF16) for i in range(2)]
        z0 = kb.emit("V", lambda e: e.memset(accw[:], 0.0))
        upv = wup_shard.rearrange("(n p) k -> n p k", p=128)
        dnv = wdn_shard.rearrange("(n p) k -> n p k", p=128)
        jobs = []
        k_up = 4096 // 512
        n_up_tiles = nsh // 128
        for i in range(0, n_up_tiles, k_up):
            n = min(k_up, n_up_tiles - i)
            jobs.append((0, upv[i:i + n], n, 512))
        k_dn = max(1, 4096 // D)
        n_dn_tiles = F // 8 // 128
        for i in range(0, n_dn_tiles, k_dn):
            n = min(k_dn, n_dn_tiles - i)
            jobs.append((1, dnv[i:i + n], n, D))
        sh_free = [z0, z0]
        vtok = z0
        for ji, (which, ap, n, kw_) in enumerate(jobs):
            bi = ji % 2
            width = n * kw_
            dsb = shard_b[bi][:, 0:width]
            dtok = kb.dma("G", dsb.rearrange("p (n k) -> p n k", n=n),
                          ap.rearrange("n p k -> p n k"),
                          waits=[sh_free[bi]], ch=f"sh{bi}")
            rtok = kb.emit("V", lambda e, d=dsb: e.reduce_sum(
                st(0, C_T3), d, AXX, apply_absolute_value=True), waits=[dtok, vtok])
            vtok = kb.emit("V", lambda e, w=which: e.tensor_add(
                accw[:, w:w + 1], accw[:, w:w + 1], st(0, C_T3)), waits=[rtok])
            sh_free[bi] = vtok
        tsc = kb.emit("T", lambda e: e.matmul(mmb[4].ap[0:1, 0:2], onec_s[:, 0:1],
                                              accw[:], start=True, stop=True),
                      waits=[vtok, ld_one, mmb[4].free])
        zs = kb.emit("V", lambda e: e.memset(sc_s[:], 0.0))
        cpy = kb.emit("V", lambda e: e.tensor_copy(sc_s[:, 0:2], mmb[4].ap[0:1, 0:2]),
                      waits=[tsc, zs])
        mmb[4].free = cpy
        dcc = kb.dma("G", sc_src, sc_s[:], waits=[cpy], sbuf_dest=False, ch="sc")
        cc1 = kb.emit("G", lambda e: e.collective_compute(
            "AllReduce", AOT.add, replica_groups=[[0, 1, 2, 3, 4, 5, 6, 7]],
            ins=[sc_src], outs=[sc_dst]), waits=[dcc])
        dcb = kb.dma("G", sc_s[:], sc_dst, waits=[cc1], ch="sc")
        t1 = kb.emit("V", lambda e: e.tensor_scalar(sc_s[:, 2:3], sc_s[:, 0:1],
                                                    1.0 / (D * F), LN_EPS, AOT.mult, AOT.add),
                     waits=[dcb, cpy])
        t2 = kb.emit("V", lambda e: e.tensor_scalar(sc_s[:, 4:5], sc_s[:, 1:2],
                                                    1.0 / (D * F), LN_EPS, AOT.mult, AOT.add),
                     waits=[dcb])
        t3 = kb.emit("V", lambda e: e.reciprocal(sc_s[:, 3:4], sc_s[:, 2:3]), waits=[t1])
        sc_done = kb.emit("V", lambda e: e.reciprocal(sc_s[:, 5:6], sc_s[:, 4:5]),
                          waits=[t2, t3])
        tb2 = kb.emit("T", lambda e: e.matmul(mmb[5].ap[:, 0:8], oner_s[0:1, :],
                                              sc_s[0:1, 0:8], start=True, stop=True),
                      waits=[sc_done, ld_oner, mmb[5].free])
        scb_done = kb.emit("V", lambda e: e.tensor_copy(scb_s[:], mmb[5].ap[:, 0:8]),
                           waits=[tb2])
        mmb[5].free = scb_done
        w1_stack.close()
        kb.close_regions()

        pt_stack = stack.enter_context(ExitStack())
        ab_stack = stack.enter_context(ExitStack())
        JH = max(1, J // 2)
        xq_fm = sb(ab_stack, "xq_fm", [128, DC * T], BF16)
        pan_tm = sb(ab_stack, "pan_tm", [128, 2 * D], BF16)
        pan_fm = sb(ab_stack, "pan_fm", [128, DC * 512], BF16)
        e_buf = sb(ab_stack, "e_buf", [128, JH * S2])
        p_buf = sb(ab_stack, "p_buf", [128, S2], BF16)
        p_T = sb(pt_stack, "p_T", [128, KT * T], BF16, side="right")

        NPAN = S2 // 512
        OWN_PAN = T // 512
        pan_rd_free = [None]
        xq_built = [False]
        pvs_tok = [None] * J
        pT_last = [None]
        min_tok = [None] * J

        def load_panel(pnl, into_own):
            ready = None
            for h in range(2):
                dtok = kb.dma("G", pan_tm[:].rearrange("p (t d) -> p t d", t=2),
                              xbv[4 * pnl + 2 * h: 4 * pnl + 2 * h + 2]
                              .rearrange("t p d -> p t d"),
                              waits=[pan_rd_free[0]], ch=f"pan{h}")
                for tt in range(2):
                    for dc in range(DC):
                        src = pan_tm[:, tt * D + dc * 128: tt * D + dc * 128 + 128]
                        if into_own:
                            col = (4 * pnl + 2 * h + tt) * 128
                            dst = xq_fm[:, dc * T + col: dc * T + col + 128]
                        else:
                            dst = pan_fm[:, dc * 512 + (2 * h + tt) * 128:
                                         dc * 512 + (2 * h + tt) * 128 + 128]
                        last_t, ready = tp.run(src, dst, waits=[dtok, ld_eye])
                pan_rd_free[0] = last_t
            return ready

        for half, qts in enumerate([list(range(0, JH)), list(range(JH, J))]):
            if not qts:
                continue
            if not xq_built[0]:
                own_ready = None
                for pnl in range(OWN_PAN):
                    own_ready = load_panel(pnl, True)
                xq_built[0] = own_ready
            for pnl in range(NPAN):
                if pnl < OWN_PAN:
                    rhs = [xq_fm[:, dc * T + pnl * 512: dc * T + pnl * 512 + 512]
                           for dc in range(DC)]
                    pan_ready = xq_built[0]
                else:
                    pan_ready = load_panel(pnl, False)
                    rhs = [pan_fm[:, dc * 512: dc * 512 + 512] for dc in range(DC)]
                for qi, qt in enumerate(qts):
                    bank = mmb[qi % 4]
                    mms = [(xq_fm[:, dc * T + qt * 128: dc * T + qt * 128 + 128], rhs[dc])
                           for dc in range(DC)]
                    mt = mm_group(kb, bank, bank.ap, mms, waits=[pan_ready])
                    dst = e_buf[:, qi * S2 + pnl * 512: qi * S2 + pnl * 512 + 512]
                    ev = kb.emit("S", lambda e, d=dst, s=bank.ap: e.copy(d, s), waits=[mt])
                    bank.free = ev
                    rt = kb.emit("V", lambda e, d=dst, j=qt: e.tensor_reduce(
                        st(j, C_T0), d, AXX, AOT.min), waits=[ev, min_tok[qt]])
                    if min_tok[qt] is None:
                        min_tok[qt] = kb.emit("V", lambda e, j=qt: e.tensor_copy(
                            st(j, C_MIN), st(j, C_T0)), waits=[rt])
                    else:
                        min_tok[qt] = kb.emit("V", lambda e, j=qt: e.tensor_tensor(
                            st(j, C_MIN), st(j, C_MIN), st(j, C_T0), AOT.min),
                            waits=[rt])
            pbuf_free = None
            for qi, qt in enumerate(qts):
                exp_tok = None
                add_tok = None
                for kk in range(S2 // 512):
                    src = e_buf[:, qi * S2 + kk * 512: qi * S2 + kk * 512 + 512]
                    dst = p_buf[:, kk * 512: kk * 512 + 512]
                    acc = st(qt, C_ESUM) if kk == 0 else st(qt, C_T1)
                    exp_tok = kb.emit("S", lambda e, s=src, d=dst, a=acc, j=qt: e.activation(
                        d, s, AFT.Exp, bias=st(j, C_MIN), scale=-1.0, accum_out=a),
                        waits=[min_tok[qt], pbuf_free, add_tok])
                    if kk > 0:
                        add_tok = kb.emit("V", lambda e, j=qt: e.tensor_add(
                            st(j, C_ESUM), st(j, C_ESUM), st(j, C_T1)),
                            waits=[exp_tok] + ([add_tok] if add_tok else []))
                rc = kb.emit("V", lambda e, j=qt: e.reciprocal(st(j, C_PVS), st(j, C_ESUM)),
                             waits=[exp_tok, add_tok])
                pvs_tok[qt] = kb.emit("V", lambda e, j=qt: e.tensor_scalar(
                    st(j, C_PVS), st(j, C_PVS), gam_s[:, 0:1], 0.5, AOT.mult, AOT.mult),
                    waits=[rc, ld_gam])
                ttok = None
                for kt in range(KT):
                    src = p_buf[:, kt * 128: kt * 128 + 128]
                    dst = p_T[:, kt * T + qt * 128: kt * T + qt * 128 + 128]
                    ttok, pT_last[0] = tp.run(src, dst, waits=[exp_tok], evac="S")
                pbuf_free = ttok
        pT_done = pT_last[0]

        if dbg == "E":
            kb.dma("Y", dbgs, stat[:, 0:32 * J], waits=[kb.tok("V"), kb.tok("S")],
                   sbuf_dest=False, ch="out")
            kb.dma("Y", dbge, e_buf[:, 0:4096], waits=[kb.tok("S")], sbuf_dest=False, ch="out")
            kb.finish()
            with nc.Block() as block:
                kb.install(block)
            return nc

        ab_stack.close()
        kb.close_regions()

        c_stack = stack.enter_context(ExitStack())
        xcol = [sb(c_stack, f"xcol{i}", [128, KT * 512], BF16) for i in range(2)]
        xown = [sb(c_stack, f"xown{i}", [128, J * 512]) for i in range(2)]
        vstg = [sb(c_stack, f"vstg{i}", [128, J * 512]) for i in range(2)]
        sq_dump = sb(c_stack, "sq_dump", [128, 512])
        xc_free = [None, None]
        xo_free = [None, None]
        v_free = [None, None]
        sum_tok = [None] * J
        sq_last = [None] * J
        sqd_tok = None
        for ds in range(c.DSL):
            bi = ds % 2
            ctok = kb.dma("G", xcol[bi][:].rearrange("p (t k) -> p t k", t=KT),
                          xbv[:, :, ds * 512:(ds + 1) * 512].rearrange("t p k -> p t k"),
                          waits=[xc_free[bi]], ch=f"xc{bi}")
            otok = kb.dma("Y", xown[bi][:].rearrange("p (t k) -> p t k", t=J),
                          xbv[0:J, :, ds * 512:(ds + 1) * 512].rearrange("t p k -> p t k"),
                          waits=[xo_free[bi]], ch=f"xo{bi}")
            last_mm = None
            for qt in range(J):
                bank = mmb[qt % 6]
                mms = [(p_T[:, kt * T + qt * 128: kt * T + qt * 128 + 128],
                        xcol[bi][:, kt * 512: kt * 512 + 512]) for kt in range(KT)]
                last_mm = mm_group(kb, bank, bank.ap, mms, waits=[ctok, pT_done])
                dst = vstg[bi][:, qt * 512: qt * 512 + 512]
                xo = xown[bi][:, qt * 512: qt * 512 + 512]
                ev = kb.emit("V", lambda e, d=dst, s=bank.ap, x=xo, j=qt: e.scalar_tensor_tensor(
                    d, s, st(j, C_PVS), x, AOT.mult, AOT.add),
                    waits=[last_mm, otok, v_free[bi], pvs_tok[qt]])
                bank.free = ev
                rt = kb.emit("V", lambda e, d=dst, j=qt: e.reduce_sum(st(j, C_T0), d, AXX),
                             waits=[ev] + ([sum_tok[qt]] if sum_tok[qt] else []))
                if ds == 0:
                    sum_tok[qt] = kb.emit("V", lambda e, j=qt: e.tensor_copy(
                        st(j, C_SUM), st(j, C_T0)), waits=[rt])
                else:
                    sum_tok[qt] = kb.emit("V", lambda e, j=qt: e.tensor_add(
                        st(j, C_SUM), st(j, C_SUM), st(j, C_T0)), waits=[rt])
                sq_last[qt] = kb.emit("S", lambda e, d=dst, j=qt, k=ds: e.activation(
                    sq_dump[:], d, AFT.Square,
                    accum_out=sqp[:, j * c.DSL + k: j * c.DSL + k + 1]),
                    waits=[ev] + ([sqd_tok] if sqd_tok else []))
                sqd_tok = sq_last[qt]
            ptok = kb.dma("S", v_park[:, :, ds * 512:(ds + 1) * 512].rearrange("t p k -> p t k"),
                          vstg[bi][:].rearrange("p (t k) -> p t k", t=J),
                          waits=[sq_last[J - 1]], sbuf_dest=False, ch="vpk")
            v_free[bi] = ptok
            xc_free[bi] = last_mm
            xo_free[bi] = sum_tok[J - 1]

        if dbg == "v1":
            kb.dma("Y", dbgs, stat[:, 0:32 * J], waits=[kb.tok("V"), kb.tok("S")],
                   sbuf_dest=False, ch="out")
            for qt in range(J):
                kb.dma("Y", outv[qt], v_park[qt], waits=[kb.tok("dS_vpk")],
                       sbuf_dest=False, ch="out")
            kb.finish()
            with nc.Block() as block:
                kb.install(block)
            return nc

        c_stack.close()
        pt_stack.close()
        kb.close_regions()

        d_outer = stack.enter_context(ExitStack())
        M_up = sb(d_outer, "M_up", [128, T])
        Vs2 = sb(d_outer, "Vs2", [128, T])
        s2c = sb(d_outer, "s2c", [128, 8])
        s2row = sb(d_outer, "s2row", [1, T])
        up_stack = stack.enter_context(ExitStack())
        xiT = sb(up_stack, "xiT", [128, DC * T], BF16)
        ln1_stack = stack.enter_context(ExitStack())
        v_in = [sb(ln1_stack, f"v_in{i}", [128, D]) for i in range(2)]
        x1t = sb(ln1_stack, "x1t", [128, D])
        xit = sb(ln1_stack, "xit", [128, D], BF16)
        vin_free = [None, None]
        x1t_free = None
        xit_free = None
        x1_done = [None] * J
        c1_tok = [None] * J
        xiT_ev = None
        parks_done = kb.tok("dS_vpk")
        for qt in range(J):
            bi = qt % 2
            dtok = kb.dma("Y", v_in[bi][:], v_park[qt], waits=[vin_free[bi], parks_done],
                          ch=f"vi{bi}")
            a1 = kb.emit("V", lambda e, j=qt: e.tensor_scalar(
                st(j, C_MU), st(j, C_SUM), 1.0 / D, None, AOT.mult),
                waits=[sum_tok[qt]])
            a2 = kb.emit("V", lambda e, j=qt: e.tensor_reduce(
                st(j, C_VAR), sqp[:, j * c.DSL:(j + 1) * c.DSL], AXX, AOT.add),
                waits=[sq_last[qt], sqd_tok])
            a3 = kb.emit("V", lambda e, j=qt: e.tensor_scalar(
                st(j, C_VAR), st(j, C_VAR), 1.0 / D, None, AOT.mult), waits=[a2])
            a4 = kb.emit("V", lambda e, j=qt: e.tensor_tensor(
                st(j, C_T0), st(j, C_MU), st(j, C_MU), AOT.mult), waits=[a1])
            var_tok = kb.emit("V", lambda e, j=qt: e.tensor_tensor(
                st(j, C_VAR), st(j, C_VAR), st(j, C_T0), AOT.subtract), waits=[a3, a4])
            sd_tok = kb.emit("S", lambda e, j=qt: e.activation(
                st(j, C_RST), st(j, C_VAR), AFT.Sqrt, bias=cv_s[:, 0:1]),
                waits=[var_tok, ld_cv])
            r1 = kb.emit("V", lambda e, j=qt: e.reciprocal(st(j, C_RST), st(j, C_RST)),
                         waits=[sd_tok])
            r2_ = kb.emit("V", lambda e, j=qt: e.tensor_tensor(
                st(j, C_NMU), st(j, C_MU), st(j, C_RST), AOT.mult), waits=[r1, a1])
            nmu_tok = kb.emit("V", lambda e, j=qt: e.tensor_scalar(
                st(j, C_NMU), st(j, C_NMU), -1.0, None, AOT.mult), waits=[r2_])
            ln_tok = kb.emit("S", lambda e, j=qt, b=bi: e.activation(
                x1t[:], v_in[b][:], AFT.Identity, bias=st(j, C_NMU), scale=st(j, C_RST)),
                waits=[dtok, nmu_tok, x1t_free])
            vin_free[bi] = ln_tok
            x1_done[qt] = kb.dma("S", x1_park[qt], x1t[:], waits=[ln_tok],
                                 sbuf_dest=False, ch="x1p")
            m1 = kb.emit("V", lambda e, j=qt: e.reduce_max(
                st(j, C_AM1), x1t[:], AXX, apply_absolute_value=True), waits=[ln_tok])
            m2 = kb.emit("V", lambda e, j=qt: e.tensor_scalar(
                st(j, C_AM1), st(j, C_AM1), 1e-5, None, AOT.max), waits=[m1])
            m3 = kb.emit("V", lambda e, j=qt: e.reciprocal(st(j, C_S1), st(j, C_AM1)),
                         waits=[m2])
            m4 = kb.emit("V", lambda e, j=qt: e.tensor_scalar(
                st(j, C_S1), st(j, C_S1), 127.0, None, AOT.mult), waits=[m3])
            s1_tok = kb.emit("V", lambda e, j=qt: e.tensor_scalar(
                st(j, C_IS1), st(j, C_AM1), 1.0 / 127, None, AOT.mult), waits=[m2])
            c1_tok[qt] = kb.emit("V", lambda e, j=qt: e.tensor_tensor(
                st(j, C_C1), st(j, C_IS1), scb_s[:, 2:3], AOT.mult),
                waits=[s1_tok, scb_done])
            q1 = kb.emit("S", lambda e, j=qt: e.activation(
                x1t[:], x1t[:], AFT.Identity, bias=cv_s[:, 1:2], scale=st(j, C_S1)),
                waits=[m4, x1_done[qt], m1, ln_tok])
            q2 = kb.emit("V", lambda e: e.tensor_scalar(
                x1t[:], x1t[:], MAGIC - 128.0, MAGIC + 127.0, AOT.max, AOT.min),
                waits=[q1])
            q3 = kb.emit("V", lambda e: e.tensor_scalar(
                xit[:], x1t[:], MAGIC, None, AOT.subtract), waits=[q2, xit_free])
            x1t_free = q3
            ttok = None
            for dc in range(DC):
                src = xit[:, dc * 128: dc * 128 + 128]
                dst = xiT[:, dc * T + qt * 128: dc * T + qt * 128 + 128]
                ttok, xiT_ev = tp.run(src, dst, waits=[q3], evac="S")
            xit_free = ttok

        if dbg == "x1":
            kb.dma("Y", dbgs, stat[:, 0:32 * J], waits=[kb.tok("V"), kb.tok("S")],
                   sbuf_dest=False, ch="out")
            for qt in range(J):
                kb.dma("Y", outv[qt], x1_park[qt], waits=[x1_done[qt]],
                       sbuf_dest=False, ch="out")
            kb.finish()
            with nc.Block() as block:
                kb.install(block)
            return nc

        ln1_stack.close()
        kb.close_regions()

        C_SUM2, C_MU2, C_VAR2, C_RST2, C_NMU2, C_T4 = 22, 23, 24, 25, 26, 27
        C_SMX, C_SNEG, C_SSUM, C_SRC = 28, 29, 30, 31
        sq_free = [None]
        stat2_free = [None] * J

        def ln_apply(j, src, dst, sqbuf, deps):
            a_s = kb.emit("V", lambda e: e.reduce_sum(st(j, C_SUM2), src, AXX),
                          waits=list(deps) + _toks(stat2_free[j]))
            a_q = kb.emit("S", lambda e: e.activation(
                sqbuf, src, AFT.Square, accum_out=st(j, C_VAR2)),
                waits=list(deps) + _toks(sq_free[0]) + _toks(stat2_free[j]))
            sq_free[0] = a_q
            m1 = kb.emit("V", lambda e: e.tensor_scalar(
                st(j, C_MU2), st(j, C_SUM2), 1.0 / D, None, AOT.mult), waits=[a_s])
            m2 = kb.emit("V", lambda e: e.tensor_scalar(
                st(j, C_VAR2), st(j, C_VAR2), 1.0 / D, None, AOT.mult), waits=[a_q])
            m3 = kb.emit("V", lambda e: e.tensor_tensor(
                st(j, C_T4), st(j, C_MU2), st(j, C_MU2), AOT.mult), waits=[m1])
            m4 = kb.emit("V", lambda e: e.tensor_tensor(
                st(j, C_VAR2), st(j, C_VAR2), st(j, C_T4), AOT.subtract),
                waits=[m2, m3])
            m5 = kb.emit("S", lambda e: e.activation(
                st(j, C_RST2), st(j, C_VAR2), AFT.Sqrt, bias=cv_s[:, 0:1]),
                waits=[m4, ld_cv])
            m6 = kb.emit("V", lambda e: e.reciprocal(st(j, C_RST2), st(j, C_RST2)),
                         waits=[m5])
            m7 = kb.emit("V", lambda e: e.tensor_tensor(
                st(j, C_NMU2), st(j, C_MU2), st(j, C_RST2), AOT.mult),
                waits=[m6, m1])
            m8 = kb.emit("V", lambda e: e.tensor_scalar(
                st(j, C_NMU2), st(j, C_NMU2), -1.0, None, AOT.mult), waits=[m7])
            ap_ = kb.emit("S", lambda e: e.activation(
                dst, src, AFT.Identity, bias=st(j, C_NMU2), scale=st(j, C_RST2)),
                waits=list(deps) + [m8])
            stat2_free[j] = ap_
            return ap_, a_s

        bank_i = [0]

        def nb():
            b = mmb[bank_i[0] % 6]
            bank_i[0] += 1
            return b

        NTC = T // 512
        FTB = 2048 // 128
        wup_stg = sb(up_stack, "wup_stg", [128, DC * 128])
        wup_tmp = sb(up_stack, "wup_tmp", [128, DC * 128])
        wuq = [sb(up_stack, f"wuq{i}", [128, DC * 128], BF16) for i in range(2)]
        rstg = [sb(up_stack, f"rstg{i}", [128, T], F16) for i in range(2)]
        zM = kb.emit("V", lambda e: e.memset(M_up[:], 0.0))
        wstg_free = None
        wtmp_free = None
        wuq_free = [None, None]
        rstg_free = [None, None]
        mx_tok = [zM] * NTC
        xiT_done = xiT_ev
        up2 = dbg is not None and dbg.startswith("up2")
        FT_eff = 2 if up2 else c.FT
        for ot in range(FT_eff):
            bi = ot % 2
            dtok = kb.dma("Y", wup_stg[:], w_upT[ot],
                          waits=[wstg_free], ch="wup")
            uq1 = kb.emit("S", lambda e: e.activation(
                wup_tmp[:], wup_stg[:], AFT.Identity, bias=cv_s[:, 1:2],
                scale=scb_s[:, 3:4]),
                waits=[dtok, scb_done, ld_cv, wtmp_free])
            wstg_free = uq1
            uq2 = kb.emit("V", lambda e: e.tensor_scalar(
                wup_tmp[:], wup_tmp[:], MAGIC - 1.0, MAGIC + 1.0, AOT.max, AOT.min),
                waits=[uq1])
            uq3 = kb.emit("V", lambda e, b=bi: e.tensor_scalar(
                wuq[b][:], wup_tmp[:], MAGIC, None, AOT.subtract),
                waits=[uq2] + _toks(wuq_free[bi]))
            wtmp_free = uq3
            rel_toks = []
            mt = None
            for tc in range(NTC):
                bank = nb()
                mms = [(wuq[bi][:, dc * 128: dc * 128 + 128],
                        xiT[:, dc * T + tc * 512: dc * T + tc * 512 + 512])
                       for dc in range(DC)]
                mt = mm_group(kb, bank, bank.ap, mms, waits=[uq3, xiT_done])
                rel = kb.emit("S", lambda e, b=bi, t=tc, a=bank.ap:
                              e.activation(
                                  rstg[b][:, t * 512: t * 512 + 512], a,
                                  AFT.Relu, scale=cv_s[:, 2:3]),
                              waits=[mt, ld_cv] + _toks(rstg_free[bi]))
                rel_toks.append(rel)
                bank.free = [rel]
                mx = kb.emit("V", lambda e, b=bi, t=tc: e.tensor_max(
                    M_up[:, t * 512: t * 512 + 512],
                    rstg[b][:, t * 512: t * 512 + 512],
                    M_up[:, t * 512: t * 512 + 512]),
                    waits=[rel, mx_tok[tc]])
                mx_tok[tc] = mx
                rel_toks.append(mx)
            ptok = kb.dma("S", rh_pk[ot], rstg[bi][:], waits=rel_toks,
                          sbuf_dest=False, ch=f"rhp{bi}")
            rstg_free[bi] = [ptok]
            wuq_free[bi] = [mt]

        if dbg == "upq" or up2:
            kb.dma("Y", dbgs, stat[:, 0:32 * J], waits=[kb.tok("V"), kb.tok("S")],
                   sbuf_dest=False, ch="out")
            for qt in range(J):
                kb.dma("Y", outv[qt], x1_park[qt], waits=[x1_done[qt]],
                       sbuf_dest=False, ch="out")
            kb.finish()
            with nc.Block() as block:
                kb.install(block)
            return nc

        c2_tok = [None] * J
        s2c_toks = []
        for j in range(J):
            bank = nb()
            mmt = kb.emit("T", lambda e, j_=j, a=bank.ap: e.matmul(
                a[:, 0:128], M_up[:, j_ * 128: j_ * 128 + 128], eye32_s[:],
                start=True, stop=True),
                waits=_toks(bank.free) + [mx_tok[(j * 128) // 512], ld_eye32])
            hv = kb.emit("V", lambda e, j_=j, a=bank.ap: e.tensor_reduce(
                st(j_, C_HMX), a[:, 0:128], AXX, AOT.max), waits=[mmt])
            bank.free = hv
            e1 = kb.emit("V", lambda e, j_=j: e.tensor_tensor(
                st(j_, C_T2), st(j_, C_HMX), st(j_, C_HMX), AOT.mult), waits=[hv])
            e2 = kb.emit("V", lambda e, j_=j: e.tensor_scalar(
                st(j_, C_T0), st(j_, C_T2), 1e-30, None, AOT.max), waits=[e1])
            e3 = kb.emit("V", lambda e, j_=j: e.reciprocal(
                st(j_, C_T1), st(j_, C_T0)), waits=[e2])
            e4 = kb.emit("V", lambda e, j_=j: e.tensor_scalar(
                st(j_, C_S2P), st(j_, C_T1), 127.0, None, AOT.mult), waits=[e3])
            e5 = kb.emit("V", lambda e, j_=j: e.tensor_copy(
                s2c[:, j_:j_ + 1], st(j_, C_S2P)), waits=[e4])
            s2c_toks.append(e5)
            e6 = kb.emit("V", lambda e, j_=j: e.tensor_tensor(
                st(j_, C_T3), st(j_, C_C1), st(j_, C_C1), AOT.mult),
                waits=[c1_tok[j]])
            e7 = kb.emit("V", lambda e, j_=j: e.scalar_tensor_tensor(
                st(j_, C_R2M), st(j_, C_T2), 256.0, st(j_, C_T3),
                AOT.mult, AOT.mult),
                waits=[e6, e1])
            e8 = kb.emit("V", lambda e, j_=j: e.tensor_scalar(
                st(j_, C_R2M), st(j_, C_R2M), 1e-5, None, AOT.max), waits=[e7])
            c2_tok[j] = kb.emit("V", lambda e, j_=j: e.tensor_scalar(
                st(j_, C_C2), st(j_, C_R2M), scb_s[:, 4:5], 1.0 / 127,
                AOT.mult, AOT.mult), waits=[e8, scb_done])
        s2r8 = sb(up_stack, "s2r8", [J, 128])
        bank = nb()
        mmtr = kb.emit("T", lambda e, a=bank.ap: e.matmul(
            a[0:J, 0:128], s2c[:, 0:J], eye32_s[:], start=True, stop=True),
            waits=_toks(bank.free) + s2c_toks + [ld_eye32])
        cp8 = kb.emit("V", lambda e, a=bank.ap: e.tensor_copy(
            s2r8[:], a[0:J, 0:128]), waits=[mmtr])
        bank.free = cp8
        ds2w = kb.dma("S", s2_d.rearrange("(j ccc) -> j ccc", j=J), s2r8[:],
                      waits=[cp8], sbuf_dest=False, ch="s2d")
        rc_last = kb.dma("Y", s2row[0:1, 0:T],
                         s2_d.rearrange("(o t) -> o t", o=1),
                         waits=[ds2w], ch="s2d2")
        vs2_done = None
        for tc in range(NTC):
            bank2 = nb()
            bmt = kb.emit("T", lambda e, t=tc, a=bank2.ap: e.matmul(
                a, oner_s[:], s2row[0:1, t * 512: t * 512 + 512],
                start=True, stop=True),
                waits=_toks(bank2.free) + [rc_last, ld_oner])
            vs2_done = kb.emit("V", lambda e, t=tc, a=bank2.ap: e.tensor_copy(
                Vs2[:, t * 512: t * 512 + 512], a), waits=[bmt])
            bank2.free = vs2_done

        if dbg == "up":
            kb.dma("Y", dbgs, stat[:, 0:32 * J], waits=[kb.tok("V"), kb.tok("S")],
                   sbuf_dest=False, ch="out")
            for qt in range(J):
                kb.dma("Y", outv[qt], x1_park[qt], waits=[x1_done[qt]],
                       sbuf_dest=False, ch="out")
            kb.finish()
            with nc.Block() as block:
                kb.install(block)
            return nc

        up_stack.close()
        kb.close_regions()

        down_stack = stack.enter_context(ExitStack())
        rhrd = [sb(down_stack, f"rhrd{i}", [128, T], F16) for i in range(2)]
        dtmp = sb(down_stack, "dtmp", [128, T])
        xi2T = [sb(down_stack, f"xi2T{i}", [128, FTB * T], BF16) for i in range(2)]
        wdn_stg = sb(down_stack, "wdn_stg", [128, FTB * 512])
        wdq = [sb(down_stack, f"wdq{i}", [128, FTB * 512], BF16) for i in range(2)]
        pst = [sb(down_stack, f"pst{i}", [128, 512]) for i in range(4)]
        rh_parked = [kb.tok("dS_rhp0"), kb.tok("dS_rhp1")]
        rhrd_free = [None, None]
        dtmp_free = None
        xi2T_busy = [None, None]
        wstg_free2 = None
        wdq_free = [None, None]
        pst_free = [None] * 4
        pi = 0
        prep_done = [None, None]
        for fb in range(c.NFB):
            xb_ = fb % 2
            pd = None
            for ft in range(FTB):
                bi = ft % 2
                dr = kb.dma("G", rhrd[bi][:], rh_pk[fb * FTB + ft],
                            waits=_toks(rhrd_free[bi]) + rh_parked, ch=f"rhr{bi}")
                v1 = kb.emit("V", lambda e, b=bi: e.tensor_tensor(
                    dtmp[:], rhrd[b][:], rhrd[b][:], AOT.mult),
                    waits=[dr, dtmp_free])
                rhrd_free[bi] = v1
                v2 = kb.emit("V", lambda e: e.tensor_tensor(
                    dtmp[:], dtmp[:], Vs2[:], AOT.mult), waits=[v1, vs2_done])
                v3 = kb.emit("V", lambda e, x=xb_, f=ft: e.tensor_scalar(
                    xi2T[x][:, f * T: (f + 1) * T], dtmp[:], MAGIC, MAGIC,
                    AOT.add, AOT.subtract),
                    waits=[v2] + _toks(xi2T_busy[xb_]))
                dtmp_free = v3
                pd = v3
            prep_done[xb_] = pd
            mm_last = None
            for os_ in range(c.DSL):
                wi = (fb * c.DSL + os_) % 2
                dw = kb.dma("Y", wdn_stg[:], w_downT[fb, os_],
                            waits=[wstg_free2], ch="wdn")
                dq1 = kb.emit("S", lambda e: e.activation(
                    wdn_stg[:], wdn_stg[:], AFT.Identity, bias=cv_s[:, 1:2],
                    scale=scb_s[:, 5:6]), waits=[dw, scb_done, ld_cv])
                dq2 = kb.emit("V", lambda e: e.tensor_scalar(
                    wdn_stg[:], wdn_stg[:], MAGIC - 1.0, MAGIC + 1.0,
                    AOT.max, AOT.min), waits=[dq1])
                dq3 = kb.emit("V", lambda e, w=wi: e.tensor_scalar(
                    wdq[w][:], wdn_stg[:], MAGIC, None, AOT.subtract),
                    waits=[dq2] + _toks(wdq_free[wi]))
                wstg_free2 = dq3
                for t in range(J):
                    bank = nb()
                    mms = [(xi2T[xb_][:, kt * T + t * 128: kt * T + t * 128 + 128],
                            wdq[wi][:, kt * 512:(kt + 1) * 512])
                           for kt in range(FTB)]
                    mt = mm_group(kb, bank, bank.ap, mms,
                                  waits=[dq3, prep_done[xb_]])
                    mm_last = mt
                    p_ = pi % 4
                    ev = kb.emit("V", lambda e, a=bank.ap, q=p_: e.tensor_copy(
                        pst[q][:], a), waits=[mt] + _toks(pst_free[p_]))
                    bank.free = ev
                    pk = kb.dma("S", pdn[fb][t][:, os_ * 512:(os_ + 1) * 512],
                                pst[p_][:], waits=[ev], sbuf_dest=False,
                                ch=f"pdw{p_}")
                    pst_free[p_] = pk
                    pi += 1
                wdq_free[wi] = mm_last
            xi2T_busy[xb_] = mm_last

        down_stack.close()
        d_outer.close()
        kb.close_regions()

        x2_stack = stack.enter_context(ExitStack())
        x2T = sb(x2_stack, "x2T", [128, DC * T], BF16)
        ln2_stack = stack.enter_context(ExitStack())
        acc2 = sb(ln2_stack, "acc2", [128, D])
        pstg = [sb(ln2_stack, f"pstg{i}", [128, D]) for i in range(2)]
        x1rd = sb(ln2_stack, "x1rd", [128, D])
        x2t = sb(ln2_stack, "x2t", [128, D])
        x2b = sb(ln2_stack, "x2b", [128, D], BF16)
        sqd2 = sb(ln2_stack, "sqd2", [128, D])
        pdn_done = [kb.tok(f"dS_pdw{i}") for i in range(4)]
        x1p_done = kb.tok("dS_x1p")
        acc_free = None
        pstg_free = [None, None]
        x1rd_free = None
        x2t_free = None
        x2b_free = None
        x2_done = [None] * J
        x2T_ev = None
        for t in range(J):
            d0 = kb.dma("Y", acc2[:], pdn[0][t],
                        waits=_toks(acc_free) + pdn_done, ch="pr0")
            ad = d0
            for fb in range(1, c.NFB):
                bi = fb % 2
                dfb = kb.dma("G", pstg[bi][:], pdn[fb][t],
                             waits=_toks(pstg_free[bi]) + pdn_done,
                             ch=f"pr{bi + 1}")
                ad = kb.emit("V", lambda e, b=bi: e.tensor_add(
                    acc2[:], acc2[:], pstg[b][:]), waits=[dfb, ad])
                pstg_free[bi] = ad
            dx1 = kb.dma("Y", x1rd[:], x1_park[t],
                         waits=_toks(x1rd_free) + [x1p_done], ch="x1r")
            vv = kb.emit("V", lambda e, j=t: e.scalar_tensor_tensor(
                acc2[:], acc2[:], st(j, C_C2), x1rd[:], AOT.mult, AOT.add),
                waits=[ad, dx1, c2_tok[t]])
            x1rd_free = vv
            apt, asr = ln_apply(t, acc2[:], x2t[:], sqd2[:],
                                [vv] + _toks(x2t_free))
            acc_free = [apt, asr]
            x2_done[t] = kb.dma("S", x2_park[t], x2t[:], waits=[apt],
                                sbuf_dest=False, ch="x2p")
            cb = kb.emit("V", lambda e: e.tensor_copy(x2b[:], x2t[:]),
                         waits=[apt] + _toks(x2b_free))
            x2t_free = [cb, x2_done[t]]
            ttok = None
            for dc in range(DC):
                ttok, x2T_ev = tp.run(
                    x2b[:, dc * 128: dc * 128 + 128],
                    x2T[:, dc * T + t * 128: dc * T + t * 128 + 128],
                    waits=[cb], evac="S")
            x2b_free = ttok

        if dbg == "x2":
            kb.dma("Y", dbgs, stat[:, 0:32 * J], waits=[kb.tok("V"), kb.tok("S")],
                   sbuf_dest=False, ch="out")
            for qt in range(J):
                kb.dma("Y", outv[qt], x2_park[qt], waits=[x2_done[qt]],
                       sbuf_dest=False, ch="out")
            kb.finish()
            with nc.Block() as block:
                kb.install(block)
            return nc

        ln2_stack.close()
        kb.close_regions()

        qkv_stack = stack.enter_context(ExitStack())
        KVW = c.KVH * 128
        wqb = [sb(qkv_stack, f"wqb{i}", [128, DC * 128], BF16) for i in range(2)]
        qstg = [sb(qkv_stack, f"qstg{i}", [128, T], BF16) for i in range(2)]
        wkvb = sb(qkv_stack, "wkvb", [128, DC * 512], BF16)
        kvst = [sb(qkv_stack, f"kvst{i}", [128, 512], BF16) for i in range(4)]
        SC = 0.08838834764831845
        x2T_done = x2T_ev
        wqb_free = [None, None]
        qstg_free = [None, None]
        for h in range(c.NH):
            bi = h % 2
            dq = kb.dma("G", wqb[bi][:], wq[h],
                        waits=_toks(wqb_free[bi]), ch=f"wq{bi}")
            evs = []
            mt = None
            for tc in range(NTC):
                bank = nb()
                mms = [(wqb[bi][:, dc * 128: dc * 128 + 128],
                        x2T[:, dc * T + tc * 512: dc * T + tc * 512 + 512])
                       for dc in range(DC)]
                mt = mm_group(kb, bank, bank.ap, mms, waits=[dq, x2T_done])
                ev = kb.emit("S", lambda e, b=bi, t=tc, a=bank.ap: e.activation(
                    qstg[b][:, t * 512: t * 512 + 512], a, AFT.Identity,
                    scale=SC), waits=[mt] + _toks(qstg_free[bi]))
                evs.append(ev)
                bank.free = ev
            pk = kb.dma("S", qT_pk[h], qstg[bi][:], waits=evs,
                        sbuf_dest=False, ch=f"qtp{bi}")
            qstg_free[bi] = pk
            wqb_free[bi] = mt
        wkv_free = None
        kvst_free = [None] * 4
        ki = 0
        kv_srcK_v = kv_srcK.rearrange("p (kvh tok) -> p kvh tok", kvh=c.KVH)
        kv_srcV_v = kv_srcV.rearrange("p (kvh j dh) -> p kvh j dh",
                                      kvh=c.KVH, j=J)
        for s0 in range(0, KVW, 512):
            sw = min(512, KVW - s0)
            dk = kb.dma("G", wkvb[:, 0:DC * sw], wk[s0 // sw],
                        waits=[wkv_free], ch="wkv")
            last = None
            for m0 in range(0, sw, 128):
                kvh = (s0 + m0) // 128
                for tc in range(NTC):
                    bank = nb()
                    mms = [(wkvb[:, dc * sw + m0: dc * sw + m0 + 128],
                            x2T[:, dc * T + tc * 512: dc * T + tc * 512 + 512])
                           for dc in range(DC)]
                    mt = mm_group(kb, bank, bank.ap, mms, waits=[dk, x2T_done])
                    last = mt
                    kq = ki % 4
                    ev = kb.emit("V", lambda e, a=bank.ap, q=kq: e.tensor_copy(
                        kvst[q][:], a), waits=[mt] + _toks(kvst_free[kq]))
                    bank.free = ev
                    dko = kb.dma("S", kv_srcK_v[:, kvh,
                                               tc * 512: tc * 512 + 512],
                                 kvst[kq][:], waits=[ev], sbuf_dest=False,
                                 ch=f"kvo{kq}")
                    kvst_free[kq] = dko
                    ki += 1
            wkv_free = last
        for s0 in range(0, KVW, 512):
            sw = min(512, KVW - s0)
            nk = sw // 128
            dv = kb.dma("G", wkvb[:, 0:DC * sw], wv[s0 // sw],
                        waits=[wkv_free], ch="wkv")
            last = None
            for t in range(J):
                bank = nb()
                mms = [(x2T[:, dc * T + t * 128: dc * T + t * 128 + 128],
                        wkvb[:, dc * sw: dc * sw + sw]) for dc in range(DC)]
                mt = mm_group(kb, bank, bank.ap[:, 0:sw], mms,
                              waits=[dv, x2T_done])
                last = mt
                kq = ki % 4
                ev = kb.emit("V", lambda e, a=bank.ap, q=kq, w=sw: e.tensor_copy(
                    kvst[q][:, 0:w], a[:, 0:w]), waits=[mt] + _toks(kvst_free[kq]))
                bank.free = ev
                dvo = kb.dma("S", kv_srcV_v[:, s0 // 128: s0 // 128 + nk,
                                            t, :],
                             kvst[kq][:, 0:sw].rearrange(
                                 "p (k dh) -> p k dh", k=nk),
                             waits=[ev], sbuf_dest=False, ch=f"kvo{kq}")
                kvst_free[kq] = dvo
                ki += 1
            wkv_free = last
        kv_out_done = [kb.tok(f"dS_kvo{i}") for i in range(4)]
        ccK = kb.emit("G", lambda e: e.collective_compute(
            "AllGather", AOT.bypass,
            replica_groups=[[0, 1], [2, 3], [4, 5], [6, 7]],
            ins=[kv_srcK], outs=[kv_dstK]), waits=kv_out_done)
        ccV = kb.emit("G", lambda e: e.collective_compute(
            "AllGather", AOT.bypass,
            replica_groups=[[0, 1], [2, 3], [4, 5], [6, 7]],
            ins=[kv_srcV], outs=[kv_dstV]), waits=[ccK])

        qkv_stack.close()
        x2_stack.close()
        kb.close_regions()

        sc_stack = stack.enter_context(ExitStack())
        kT_f = sb(sc_stack, "kT_f", [128, c.KVH * S2], BF16)
        V_f = sb(sc_stack, "V_f", [128, c.KVH * KT * 128], BF16)
        P_T = sb(sc_stack, "P_T", [128, KT * T], BF16)
        sc_buf = sb(sc_stack, "sc_buf", [128, S2])
        p_sm = sb(sc_stack, "p_sm", [128, S2], BF16)
        qTh = [sb(sc_stack, f"qTh{i}", [128, T], BF16) for i in range(2)]
        attn_h = [sb(sc_stack, f"attn_h{i}", [128, T], BF16) for i in range(2)]
        kT_f4 = kT_f[:].rearrange("p (kvh loc ccc) -> p kvh loc ccc",
                                  kvh=c.KVH, loc=KT)
        V_f4 = V_f[:].rearrange("p (kvh loc dh) -> p kvh loc dh",
                                kvh=c.KVH, loc=KT)
        kv_rd = []
        for q in range(2):
            dk = kb.dma("G", kT_f4[:, :, q * J:(q + 1) * J, :],
                        kv_dstK[q * 128:(q + 1) * 128, :].rearrange(
                            "p (kvh j ccc) -> p kvh j ccc", kvh=c.KVH, j=J),
                        waits=[ccK], ch="agk")
            dv = kb.dma("G", V_f4[:, :, q * J:(q + 1) * J, :],
                        kv_dstV[q * 128:(q + 1) * 128, :].rearrange(
                            "p (kvh j dh) -> p kvh j dh", kvh=c.KVH, j=J),
                        waits=[ccV], ch="agv")
            kv_rd += [dk, dv]
        kv_rd = [kb.tok("dG_agk"), kb.tok("dG_agv")]
        zp = kb.emit("V", lambda e: e.memset(P_T[:], 0.0))
        qT_parked = [kb.tok("dS_qtp0"), kb.tok("dS_qtp1")]
        qTh_free = [None, None]
        attn_free = [None, None]
        sm_free = [None] * J
        psm_free = None
        scb_free = None
        pt_readlast = zp
        for h in range(c.NH):
            kvh = h // c.G
            bi = h % 2
            dqh = kb.dma("Y", qTh[bi][:], qT_pk[h],
                         waits=_toks(qTh_free[bi]) + qT_parked,
                         ch=f"qtr{bi}")
            last_ev = {"V": None, "S": None}
            last_tp = None
            for qt in range(J):
                W_own = (qt + 1) * 128
                sevs = []
                for reg in range(2):
                    base = kvh * S2 + reg * T
                    dstb = reg * W_own
                    for b0 in range(0, W_own, 512):
                        bw = min(512, W_own - b0)
                        bank = nb()
                        mtS = kb.emit(
                            "T",
                            lambda e, a=bank.ap, b_=bi, q_=qt, ba=base, b0_=b0,
                            w=bw: e.matmul(
                                a[:, 0:w], qTh[b_][:, q_ * 128: q_ * 128 + 128],
                                kT_f[:, ba + b0_: ba + b0_ + w],
                                start=True, stop=True),
                            waits=_toks(bank.free) + [dqh] + kv_rd)
                        ev = kb.emit(
                            "S",
                            lambda e, a=bank.ap, d0=dstb + b0, w=bw: e.copy(
                                sc_buf[:, d0: d0 + w], a[:, 0:w]),
                            waits=[mtS, scb_free])
                        bank.free = ev
                        sevs.append(ev)
                mk0 = kb.emit("V", lambda e, q_=qt: e.tensor_tensor(
                    sc_buf[:, q_ * 128: q_ * 128 + 128],
                    sc_buf[:, q_ * 128: q_ * 128 + 128], mk0_s[:], AOT.add),
                    waits=sevs + [ld_mk0])
                mk1 = kb.emit("V", lambda e, q_=qt, w=W_own: e.tensor_tensor(
                    sc_buf[:, w + q_ * 128: w + q_ * 128 + 128],
                    sc_buf[:, w + q_ * 128: w + q_ * 128 + 128], mk1_s[:],
                    AOT.add), waits=sevs + [ld_mk1])
                W2 = 2 * W_own
                rmx = kb.emit("V", lambda e, q_=qt, w=W2: e.tensor_reduce(
                    st(q_, C_SMX), sc_buf[:, 0:w], AXX, AOT.max),
                    waits=[mk0, mk1] + _toks(sm_free[qt]))
                ng = kb.emit("V", lambda e, q_=qt: e.tensor_scalar(
                    st(q_, C_SNEG), st(q_, C_SMX), -1.0, None, AOT.mult),
                    waits=[rmx])
                ex = kb.emit("S", lambda e, q_=qt, w=W2: e.activation(
                    sc_buf[:, 0:w], sc_buf[:, 0:w], AFT.Exp,
                    bias=st(q_, C_SNEG), accum_out=st(q_, C_SSUM)),
                    waits=[ng])
                scb_free = ex
                rc = kb.emit("V", lambda e, q_=qt: e.reciprocal(
                    st(q_, C_SRC), st(q_, C_SSUM)), waits=[ex])
                nm = kb.emit("V", lambda e, q_=qt, w=W2: e.tensor_scalar(
                    p_sm[:, 0:w], sc_buf[:, 0:w], st(q_, C_SRC), None,
                    AOT.mult), waits=[rc, psm_free])
                sm_free[qt] = nm
                for i in range(2 * (qt + 1)):
                    loc = i if i <= qt else J + (i - qt - 1)
                    ttok, evt = tp.run(
                        p_sm[:, i * 128: i * 128 + 128],
                        P_T[:, loc * T + qt * 128: loc * T + qt * 128 + 128],
                        waits=[nm, pt_readlast],
                        evac=("V" if i % 2 else "S"))
                    last_ev["V" if i % 2 else "S"] = evt
                    last_tp = ttok
                psm_free = last_tp
            aevs = []
            for qc in range(NTC):
                qtmax = 4 * qc + 3
                bank = nb()
                mms = []
                for loc0 in (0, J):
                    for jj in range(qtmax + 1):
                        loc = loc0 + jj
                        mms.append((
                            V_f[:, kvh * KT * 128 + loc * 128:
                                kvh * KT * 128 + loc * 128 + 128],
                            P_T[:, loc * T + qc * 512: loc * T + qc * 512 + 512]))
                mt = mm_group(kb, bank, bank.ap, mms,
                              waits=_toks(last_ev["V"]) + _toks(last_ev["S"])
                              + kv_rd)
                pt_readlast = mt
                ev = kb.emit("V", lambda e, a=bank.ap, b_=bi, q_=qc:
                             e.tensor_copy(
                                 attn_h[b_][:, q_ * 512: q_ * 512 + 512], a),
                             waits=[mt] + _toks(attn_free[bi]))
                bank.free = ev
                aevs.append(ev)
            apk = kb.dma("S", att_pk[:, :, h * 128:(h + 1) * 128]
                         .rearrange("j p q -> p j q"),
                         attn_h[bi][:].rearrange("p (j q) -> p j q", j=J),
                         waits=aevs, sbuf_dest=False, ch=f"atw{bi}")
            attn_free[bi] = apk
            qTh_free[bi] = pt_readlast

        sc_stack.close()
        kb.close_regions()

        dn_stack = stack.enter_context(ExitStack())
        wdb = [sb(dn_stack, f"wdb{i}", [128, c.NH * 512], BF16) for i in range(2)]
        attn_t = [sb(dn_stack, f"attn_t{i}", [128, c.NH * 128], BF16)
                  for i in range(2)]
        pst2 = [sb(dn_stack, f"pst2{i}", [128, 512]) for i in range(4)]
        att_done = [kb.tok("dS_atw0"), kb.tok("dS_atw1")]
        wdb_free = [None, None]
        attn_t_free = [None, None]
        pst2_free = [None] * 4
        p2i = 0
        for ds in range(c.DSL):
            wi = ds % 2
            dwd = kb.dma("G", wdb[wi][:], w_denseT[ds],
                         waits=_toks(wdb_free[wi]), ch=f"wdd{wi}")
            mt = None
            for qt in range(J):
                ai = qt % 2
                da = kb.dma("Y", attn_t[ai][:], att_pk[qt],
                            waits=_toks(attn_t_free[ai]) + att_done,
                            ch=f"atr{ai}")
                bank = nb()
                mms = [(attn_t[ai][:, hh * 128: hh * 128 + 128],
                        wdb[wi][:, hh * 512: hh * 512 + 512])
                       for hh in range(c.NH)]
                mt = mm_group(kb, bank, bank.ap, mms, waits=[da, dwd])
                attn_t_free[ai] = mt
                q2 = p2i % 4
                ev = kb.emit("V", lambda e, a=bank.ap, q_=q2: e.tensor_copy(
                    pst2[q_][:], a), waits=[mt] + _toks(pst2_free[q2]))
                bank.free = ev
                pk = kb.dma("S", v3_park[qt][:, ds * 512:(ds + 1) * 512],
                            pst2[q2][:], waits=[ev], sbuf_dest=False,
                            ch=f"v3p{q2}")
                pst2_free[q2] = pk
                p2i += 1
            wdb_free[wi] = mt

        dn_stack.close()
        kb.close_regions()

        f_outer = stack.enter_context(ExitStack())
        x3T = sb(f_outer, "x3T", [128, DC * T], BF16)
        rwinb = sb(f_outer, "rwinb", [128, T])
        qfm = sb(f_outer, "qfm", [128, DC])
        yT = sb(f_outer, "yT", [128, T], BF16)
        BsT = sb(f_outer, "BsT", [128, D], BF16)
        ln3_stack = stack.enter_context(ExitStack())
        v3rd = sb(ln3_stack, "v3rd", [128, D])
        x2rd = sb(ln3_stack, "x2rd", [128, D])
        x3t = sb(ln3_stack, "x3t", [128, D])
        x3b = sb(ln3_stack, "x3b", [128, D], BF16)
        sqd3 = sb(ln3_stack, "sqd3", [128, D])
        v3_done = [kb.tok(f"dS_v3p{i}") for i in range(4)]
        x2pk_done = kb.tok("dS_x2p")
        v3rd_free = None
        x2rd_free = None
        x3t_free = None
        x3b_free = None
        x3_done = [None] * J
        x3T_ev = None
        for t in range(J):
            d1 = kb.dma("Y", v3rd[:], v3_park[t],
                        waits=_toks(v3rd_free) + v3_done, ch="v3r")
            d2 = kb.dma("G", x2rd[:], x2_park[t],
                        waits=_toks(x2rd_free) + [x2pk_done], ch="x2r")
            vv = kb.emit("V", lambda e: e.tensor_add(v3rd[:], v3rd[:], x2rd[:]),
                         waits=[d1, d2])
            x2rd_free = vv
            apt, asr = ln_apply(t, v3rd[:], x3t[:], sqd3[:],
                                [vv] + _toks(x3t_free))
            v3rd_free = [apt, asr]
            x3_done[t] = kb.dma("S", x3_park[t], x3t[:], waits=[apt],
                                sbuf_dest=False, ch="x3p")
            cb = kb.emit("V", lambda e: e.tensor_copy(x3b[:], x3t[:]),
                         waits=[apt] + _toks(x3b_free))
            x3t_free = [cb, x3_done[t]]
            ttok = None
            for dc in range(DC):
                ttok, x3T_ev = tp.run(
                    x3b[:, dc * 128: dc * 128 + 128],
                    x3T[:, dc * T + t * 128: dc * T + t * 128 + 128],
                    waits=[cb], evac="S")
            x3b_free = ttok

        if dbg == "x3":
            kb.dma("Y", dbgs, stat[:, 0:32 * J], waits=[kb.tok("V"), kb.tok("S")],
                   sbuf_dest=False, ch="out")
            for qt in range(J):
                kb.dma("Y", outv[qt], x3_park[qt], waits=[x3_done[qt]],
                       sbuf_dest=False, ch="out")
            kb.finish()
            with nc.Block() as block:
                kb.install(block)
            return nc

        ln3_stack.close()
        kb.close_regions()

        rt_stack = stack.enter_context(ExitStack())
        qtmp = sb(rt_stack, "qtmp", [128, T])
        rwt_s = sb(rt_stack, "rwt_s", [128, DC * 8])
        lg = sb(rt_stack, "lg", [1, 32])
        rbs = sb(rt_stack, "rbs", [1, 8])
        rtc = sb(rt_stack, "rtc", [8, 1])
        rsc_col = sb(rt_stack, "rsc_col", [128, 1])
        bt_s = sb(rt_stack, "bt_s", [128, D])
        atb = sb(rt_stack, "atb", [128, DC * 128], BF16)
        rb_ev = None
        for tc in range(NTC):
            bank = nb()
            bmt = kb.emit("T", lambda e, a=bank.ap, t=tc: e.matmul(
                a, oner_s[:], rwr_s[0:1, t * 512: t * 512 + 512],
                start=True, stop=True),
                waits=_toks(bank.free) + [ld_oner, ld_rwr])
            rb_ev = kb.emit("V", lambda e, a=bank.ap, t=tc: e.tensor_copy(
                rwinb[:, t * 512: t * 512 + 512], a), waits=[bmt])
            bank.free = rb_ev
        qtmp_free = None
        qs_last = None
        for dc in range(DC):
            v1 = kb.emit("V", lambda e, d_=dc: e.tensor_tensor(
                qtmp[:], x3T[:, d_ * T: (d_ + 1) * T], rwinb[:], AOT.mult),
                waits=[x3T_ev, rb_ev, qtmp_free])
            qs_last = kb.emit("V", lambda e, d_=dc: e.reduce_sum(
                qfm[:, d_:d_ + 1], qtmp[:], AXX), waits=[v1])
            qtmp_free = qs_last
        dq1 = kb.dma("G", q_src.rearrange("(p dc) -> p dc", p=128), qfm[:],
                     waits=[qs_last], sbuf_dest=False, ch="qsr")
        ccq = kb.emit("G", lambda e: e.collective_compute(
            "AllReduce", AOT.add,
            replica_groups=[[0, 1], [2, 3], [4, 5], [6, 7]],
            ins=[q_src], outs=[q_dst]), waits=[dq1])
        dq2 = kb.dma("G", qfm[:], q_dst.rearrange("(p dc) -> p dc", p=128),
                     waits=[ccq], ch="qsr")
        ld_rwt = kb.dma("Y", rwt_s[:], rW_T, ch="rwl")
        bank = nb()
        lg_mms = [(qfm[:, dc:dc + 1], rwt_s[:, dc * 8: dc * 8 + 8])
                  for dc in range(DC)]
        lmt = mm_group(kb, bank, bank.ap[0:1, 0:8], lg_mms,
                       waits=[dq2, ld_rwt])
        ld_rb = kb.dma("Y", rbs[:], rb_row, ch="rbl")
        l0 = kb.emit("V", lambda e, a=bank.ap: e.tensor_copy(
            lg[0:1, 0:8], a[0:1, 0:8]), waits=[lmt])
        bank.free = l0
        l1 = kb.emit("V", lambda e: e.tensor_scalar(
            lg[0:1, 0:8], lg[0:1, 0:8], 1.0 / c.WCNT, None, AOT.mult),
            waits=[l0])
        l2 = kb.emit("V", lambda e: e.tensor_tensor(
            lg[0:1, 0:8], lg[0:1, 0:8], rbs[0:1, :], AOT.add),
            waits=[l1, ld_rb])
        l3 = kb.emit("V", lambda e: e.tensor_reduce(
            lg[0:1, 8:9], lg[0:1, 0:8], AXX, AOT.max), waits=[l2])
        l4 = kb.emit("V", lambda e: e.tensor_scalar(
            lg[0:1, 9:10], lg[0:1, 8:9], -1.0, None, AOT.mult), waits=[l3])
        l5 = kb.emit("S", lambda e: e.activation(
            lg[0:1, 16:24], lg[0:1, 0:8], AFT.Exp, bias=lg[0:1, 9:10],
            accum_out=lg[0:1, 10:11]), waits=[l4])
        l6 = kb.emit("V", lambda e: e.reciprocal(
            lg[0:1, 11:12], lg[0:1, 10:11]), waits=[l5])
        l7 = kb.emit("V", lambda e: e.tensor_scalar(
            lg[0:1, 24:32], lg[0:1, 16:24], lg[0:1, 11:12], None, AOT.mult),
            waits=[l6])
        d3 = kb.dma("S", rt_d.rearrange("(o e) -> o e", o=1), lg[0:1, 24:32],
                    waits=[l7], sbuf_dest=False, ch="rtd")
        d4 = kb.dma("Y", rtc[:], rt_d.rearrange("(e o) -> e o", o=1),
                    waits=[d3], ch="rtd2")
        bank = nb()
        rmt = kb.emit("T", lambda e, a=bank.ap: e.matmul(
            a[:, 0:1], repm_s[:], rtc[:], start=True, stop=True),
            waits=_toks(bank.free) + [d4, ld_repm])
        rsc_ev = kb.emit("V", lambda e, a=bank.ap: e.tensor_copy(
            rsc_col[:], a[:, 0:1]), waits=[rmt])
        bank.free = rsc_ev
        ld_bt = kb.dma("Y", bt_s[:], B_T, ch="btl")
        bse = kb.emit("V", lambda e: e.tensor_scalar(
            BsT[:], bt_s[:], rsc_col[:, 0:1], None, AOT.mult),
            waits=[ld_bt, rsc_ev])
        da_ = kb.dma("G", atb[:], A_T, ch="atl")
        y_ev = None
        for tc in range(NTC):
            bank = nb()
            mms = [(atb[:, dc * 128: dc * 128 + 128],
                    x3T[:, dc * T + tc * 512: dc * T + tc * 512 + 512])
                   for dc in range(DC)]
            mt = mm_group(kb, bank, bank.ap, mms, waits=[da_, x3T_ev])
            y_ev = kb.emit("V", lambda e, a=bank.ap, t=tc: e.tensor_copy(
                yT[:, t * 512: t * 512 + 512], a), waits=[mt, y_ev])
            bank.free = y_ev

        rt_stack.close()
        kb.close_regions()

        fb_stack = stack.enter_context(ExitStack())
        wfb = [sb(fb_stack, f"wfb{i}", [128, DC * 512], BF16) for i in range(2)]
        fst = [sb(fb_stack, f"fst{i}", [128, 512]) for i in range(4)]
        wfb_free = [None, None]
        fst_free = [None] * 4
        fi = 0
        for ds in range(c.DSL):
            wi = ds % 2
            dwf = kb.dma("G", wfb[wi][:], WT[ds],
                         waits=_toks(wfb_free[wi]), ch=f"wff{wi}")
            mt = None
            for t in range(J):
                bank = nb()
                mms = [(x3T[:, dc * T + t * 128: dc * T + t * 128 + 128],
                        wfb[wi][:, dc * 512:(dc + 1) * 512])
                       for dc in range(DC)]
                mms.append((yT[:, t * 128: t * 128 + 128],
                            BsT[:, ds * 512:(ds + 1) * 512]))
                mt = mm_group(kb, bank, bank.ap, mms,
                              waits=[dwf, x3T_ev, y_ev, bse])
                q3_ = fi % 4
                ev = kb.emit("V", lambda e, a=bank.ap, q_=q3_: e.tensor_copy(
                    fst[q_][:], a), waits=[mt] + _toks(fst_free[q3_]))
                bank.free = ev
                pk = kb.dma("S", b4_park[t][:, ds * 512:(ds + 1) * 512],
                            fst[q3_][:], waits=[ev], sbuf_dest=False,
                            ch=f"b4p{q3_}")
                fst_free[q3_] = pk
                fi += 1
            wfb_free[wi] = mt

        fb_stack.close()
        f_outer.close()
        kb.close_regions()

        ln4_stack = stack.enter_context(ExitStack())
        bb = sb(ln4_stack, "bb", [128, D])
        brs = sb(ln4_stack, "brs", [1, D])
        b4rd = [sb(ln4_stack, f"b4rd{i}", [128, D]) for i in range(2)]
        x3rd = [sb(ln4_stack, f"x3rd{i}", [128, D]) for i in range(2)]
        outt = [sb(ln4_stack, f"outt{i}", [128, D]) for i in range(2)]
        sqd4 = sb(ln4_stack, "sqd4", [128, D])
        ld_br = kb.dma("Y", brs[:], b_row, ch="brl")
        bb_ev = None
        for ds2 in range(c.DSL):
            bank = nb()
            bmt = kb.emit("T", lambda e, a=bank.ap, d_=ds2: e.matmul(
                a, oner_s[:], brs[0:1, d_ * 512: d_ * 512 + 512],
                start=True, stop=True),
                waits=_toks(bank.free) + [ld_br, ld_oner])
            bb_ev = kb.emit("V", lambda e, a=bank.ap, d_=ds2: e.tensor_copy(
                bb[:, d_ * 512: d_ * 512 + 512], a), waits=[bmt, bb_ev])
            bank.free = bb_ev
        b4_done = [kb.tok(f"dS_b4p{i}") for i in range(4)]
        x3pk_done = kb.tok("dS_x3p")
        b4rd_free = [None, None]
        x3rd_free = [None, None]
        outt_free = [None, None]
        for t in range(J):
            bi = t % 2
            d1 = kb.dma("Y", b4rd[bi][:], b4_park[t],
                        waits=_toks(b4rd_free[bi]) + b4_done, ch=f"b4r{bi}")
            d2 = kb.dma("G", x3rd[bi][:], x3_park[t],
                        waits=_toks(x3rd_free[bi]) + [x3pk_done], ch=f"x3r{bi}")
            v1 = kb.emit("V", lambda e, b=bi: e.tensor_add(
                b4rd[b][:], b4rd[b][:], bb[:]), waits=[d1, bb_ev])
            v2 = kb.emit("V", lambda e, b=bi: e.tensor_add(
                b4rd[b][:], b4rd[b][:], x3rd[b][:]), waits=[v1, d2])
            x3rd_free[bi] = v2
            apt, asr = ln_apply(t, b4rd[bi][:], outt[bi][:], sqd4[:],
                                [v2] + _toks(outt_free[bi]))
            b4rd_free[bi] = [apt, asr]
            od = kb.dma("Y", outv[t], outt[bi][:], waits=[apt],
                        sbuf_dest=False, ch=f"out{bi}")
            outt_free[bi] = od

        kb.finish()
        with nc.Block() as block:
            kb.install(block)
    return nc


def kernel(**inputs):
    cfg = FULL
    in_maps, metas = _host_prepare(cfg, **inputs)
    nc = build_nc(cfg)
    res = run_bass_kernel_spmd(nc, in_maps, core_ids=list(range(8)))
    x = np.asarray(inputs["L0_attn_x"])
    outp = np.zeros_like(x)
    for r, m in zip(res.results, metas):
        outp[m["b"]][m["own"]] = r["out"]
    return outp


# version 07
# speedup vs baseline: 1.1562x; 1.0022x over previous
from contextlib import ExitStack

import numpy as np
import ml_dtypes

import concourse.bass as bass
import concourse.mybir as mybir
from concourse.bass_utils import run_bass_kernel_spmd

F32 = mybir.dt.float32
BF16 = mybir.dt.bfloat16
F16 = mybir.dt.float16
AOT = mybir.AluOpType
AFT = mybir.ActivationFunctionType
AXX = mybir.AxisListType.X

MAGIC = 12582912.0
NEG = -1.0e30
LN_EPS = 1e-5
Q_START, Q_END = 611, 2047


class Cfg:
    def __init__(self, D=4096, F=16384, S=2048, NH=32, KVH=8):
        self.D, self.F, self.S, self.NH, self.KVH = D, F, S, NH, KVH
        self.G = NH // KVH
        self.DC = D // 128
        self.FC = F // 128
        self.J = S // 256
        self.T = 128 * self.J
        self.DSL = D // 512
        self.UPSL = F // 512
        self.FT = F // 128
        self.NFB = F // 2048
        self.WCNT = S - 1 - 611


FULL = Cfg()
TINY = Cfg(D=512, F=2048, S=1024, NH=8, KVH=2)



def _host_prepare(cfg, L0_attn_x, L0_attn_gamma, L0_ffn_w_up, L0_ffn_w_down,
                  L1_attn_w_qkv, L1_attn_w_dense, L1_ffn_W, L1_ffn_b, L1_ffn_A,
                  L1_ffn_B, L1_ffn_router_W, L1_ffn_router_b):
    c = cfg
    f32 = np.float32

    def t(a):
        return np.ascontiguousarray(np.asarray(a).T)

    FTB = 16
    w_upT = np.ascontiguousarray(
        np.asarray(L0_ffn_w_up, f32).reshape(c.FT, 128, c.DC, 128)
        .transpose(0, 3, 2, 1).reshape(c.FT, 128, c.D))
    w_downT = np.ascontiguousarray(
        np.asarray(L0_ffn_w_down, f32).reshape(c.DSL, 512, c.NFB, FTB, 128)
        .transpose(2, 0, 4, 3, 1).reshape(c.NFB, c.DSL, 128, FTB * 512))
    w_qkvT = t(L1_attn_w_qkv)
    wq = np.ascontiguousarray(
        w_qkvT[:, : c.NH * 128].reshape(c.DC, 128, c.NH, 128)
        .transpose(2, 1, 0, 3).reshape(c.NH, 128, c.D))
    KVW = c.KVH * 128
    SW = min(512, KVW)
    NS = KVW // SW
    wk = np.ascontiguousarray(
        w_qkvT[:, c.NH * 128: (c.NH + c.KVH) * 128].reshape(c.DC, 128, NS, SW)
        .transpose(2, 1, 0, 3).reshape(NS, 128, c.DC * SW))
    wv = np.ascontiguousarray(
        w_qkvT[:, (c.NH + c.KVH) * 128:].reshape(c.DC, 128, NS, SW)
        .transpose(2, 1, 0, 3).reshape(NS, 128, c.DC * SW))
    w_denseT = np.ascontiguousarray(
        t(L1_attn_w_dense).reshape(c.NH, 128, c.DSL, 512)
        .transpose(2, 1, 0, 3).reshape(c.DSL, 128, c.NH * 512))
    WT = np.ascontiguousarray(
        t(L1_ffn_W).reshape(c.DC, 128, c.DSL, 512)
        .transpose(2, 1, 0, 3).reshape(c.DSL, 128, c.DC * 512))
    b_row = np.asarray(L1_ffn_b, f32).reshape(1, c.D)
    ER = np.asarray(L1_ffn_A).shape[0] * np.asarray(L1_ffn_A).shape[1]
    A_T = np.ascontiguousarray(
        t(np.asarray(L1_ffn_A).reshape(ER, c.D)).reshape(c.DC, 128, ER)
        .transpose(1, 0, 2).reshape(128, c.DC * ER))
    B_T = np.ascontiguousarray(
        np.transpose(np.asarray(L1_ffn_B), (0, 2, 1)).reshape(ER, c.D))
    rW_T = np.ascontiguousarray(
        t(L1_ffn_router_W).reshape(c.DC, 128, -1)
        .transpose(1, 0, 2).reshape(128, -1))
    rb_row = np.asarray(L1_ffn_router_b, f32).reshape(1, -1)
    gamma_b = np.ascontiguousarray(
        np.broadcast_to(np.asarray(L0_attn_gamma).reshape(1, 1), (128, 1)).astype(f32))
    eye = np.eye(128, dtype=ml_dtypes.bfloat16)
    eye32 = np.eye(128, dtype=f32)
    ones_col = np.ones((128, 1), f32)
    ones_row = np.ones((1, 128), f32)
    consts_v = np.ascontiguousarray(np.broadcast_to(
        np.array([[LN_EPS, MAGIC, 0.0625]], f32), (128, 3)))
    tri = np.where(np.tril(np.ones((128, 128), bool)), 0.0, NEG).astype(f32)
    negf = np.full((128, 128), NEG, f32)
    zf = np.zeros((128, 128), f32)
    repm = np.zeros((8, 128), f32)
    for e in range(8):
        repm[e, e * 16:(e + 1) * 16] = 2.0

    x = np.asarray(L0_attn_x)
    in_maps, metas = [], []
    for core in range(8):
        b, p = core // 2, core % 2
        own = np.concatenate([np.arange(128 * (2 * j + p), 128 * (2 * j + p) + 128)
                              for j in range(c.J)])
        par = np.concatenate([np.arange(128 * (2 * j + 1 - p), 128 * (2 * j + 1 - p) + 128)
                              for j in range(c.J)])
        xbc = np.ascontiguousarray(x[b][np.concatenate([own, par])])
        mask0 = tri if p == 0 else zf
        mask1 = negf if p == 0 else tri
        rwin_row = np.zeros((1, c.T), f32)
        for j in range(c.J):
            g = 128 * (2 * j + p) + np.arange(128)
            rwin_row[0, j * 128:(j + 1) * 128] = (
                (g >= Q_START) & (g < c.S - 1)).astype(f32)
        nsh = c.D * c.UPSL // 8
        in_maps.append(dict(
            xb=xbc, gamma_b=gamma_b, eye=eye, eye32=eye32, ones_col=ones_col,
            ones_row=ones_row, consts_v=consts_v,
            mask0=mask0, mask1=mask1, rwin_row=rwin_row, repm=repm,
            w_upT=w_upT, w_downT=w_downT, wq=wq, wk=wk, wv=wv, w_denseT=w_denseT,
            WT=WT, b_row=b_row, A_T=A_T, B_T=B_T, rW_T=rW_T, rb_row=rb_row,
            wup_shard=np.ascontiguousarray(w_upT.reshape(-1, 512)[core * nsh:(core + 1) * nsh]),
            wdn_shard=np.ascontiguousarray(
                w_downT.reshape(-1, c.D)[core * (c.F // 8):(core + 1) * (c.F // 8)]),
        ))
        metas.append(dict(b=b, own=own))
    return in_maps, metas



class KB:
    ENGS = ("T", "V", "S", "G", "Y")

    def __init__(self, nc, stack):
        self.nc = nc
        self.stack = stack
        self.ops = {k: [] for k in self.ENGS}
        self.cnt = {}
        self.sems = {}
        self._waited = {k: {} for k in self.ENGS}
        self.region_toks = []

    def _sem(self, name):
        if name not in self.sems:
            self.sems[name] = self.stack.enter_context(self.nc.semaphore(name))
            self.cnt[name] = 0
        return self.sems[name]

    def wait(self, eng, tok):
        if tok is None:
            return
        name, val = tok
        if val <= 0:
            return
        sem = self._sem(name)
        assert val <= self.cnt[name], (eng, name, val, self.cnt[name])
        if self._waited[eng].get(name, 0) >= val:
            return
        self._waited[eng][name] = val
        self.ops[eng].append(lambda e, s=sem, v=val: e.wait_ge(s, v))

    def emit(self, eng, fn, waits=(), inc=True):
        for tk in waits:
            self.wait(eng, tk)
        if inc:
            sem = self._sem(eng)
            self.cnt[eng] += 1
            tok = (eng, self.cnt[eng])
            self.ops[eng].append(lambda e, f=fn, s=sem: f(e).then_inc(s, 1))
            return tok
        self.ops[eng].append(fn)
        return None

    def dma(self, q, out, in_, waits=(), sbuf_dest=True, ch="g", **kw):
        assert q in ("Y", "S", "G")
        w = list(waits) + (self.region_toks if sbuf_dest else [])
        for tk in w:
            self.wait(q, tk)
        name = "d" + q + "_" + ch
        sem = self._sem(name)
        self.cnt[name] += 16
        tok = (name, self.cnt[name])
        self.ops[q].append(lambda e, o=out, i=in_, k=dict(kw), s=sem:
                           e.dma_start(out=o, in_=i, **k).then_inc(s, 16))
        return tok

    def tok(self, name):
        v = self.cnt.get(name, 0)
        return (name, v) if v else None

    def close_regions(self):
        toks = [(k, self.cnt.get(k, 0)) for k in ("T", "V", "S", "G")]
        toks = [t for t in toks if t[1] > 0]
        dtoks = [(k, v) for k, v in self.cnt.items()
                 if k.startswith("d") and v > 0]
        for k in ("T", "V", "S", "G"):
            for t in toks:
                if t[0] != k:
                    self.wait(k, t)
            for t in dtoks:
                self.wait(k, t)
        self.region_toks = toks + dtoks

    def finish(self):
        for q, v in sorted(self.cnt.items()):
            if q.startswith("d") and v:
                self.wait("Y", (q, v))

    def install(self, block):
        attr = {"T": "tensor", "V": "vector", "S": "scalar", "G": "gpsimd", "Y": "sync"}
        for k, a in attr.items():
            if not self.ops[k]:
                continue
            def body(e, ops=self.ops[k]):
                for f in ops:
                    f(e)
            getattr(block, a)(body)


class Bank:
    def __init__(self, ap):
        self.ap = ap
        self.free = None


def _toks(x):
    if x is None:
        return []
    if isinstance(x, list):
        return [t for t in x if t is not None]
    return [x]


def mm_group(kb, bank, out_ap, mms, waits=()):
    n = len(mms)
    first = _toks(bank.free) + list(waits)
    tok = None
    for i, (l, r) in enumerate(mms):
        tok = kb.emit(
            "T",
            lambda e, o=out_ap, l=l, r=r, st=(i == 0), sp=(i == n - 1):
                e.matmul(o, l, r, start=st, stop=sp),
            waits=first if i == 0 else (), inc=(i == n - 1))
    return tok


class TP:

    def __init__(self, kb, banks, eye_ap):
        self.kb = kb
        self.banks = banks
        self.eye = eye_ap
        self.i = 0

    def run(self, src_ap, dst_ap, waits=(), evac="V"):
        bank = self.banks[self.i % len(self.banks)]
        self.i += 1
        t = self.kb.emit("T", lambda e, o=bank.ap, s=src_ap: e.transpose(o, s, self.eye),
                         waits=_toks(bank.free) + list(waits))
        if evac == "V":
            fn = lambda e, d=dst_ap, s=bank.ap: e.tensor_copy(d, s)
        else:
            fn = lambda e, d=dst_ap, s=bank.ap: e.copy(d, s)
        ev = self.kb.emit(evac, fn, waits=[t])
        bank.free = ev
        return t, ev



def build_nc(cfg, dbg=None):
    c = cfg
    J, DC, FC, D, F, T = c.J, c.DC, c.FC, c.D, c.F, c.T
    S2 = 2 * T
    KT = 2 * J
    nc = bass.Bass("TRN2", target_bir_lowering=False, debug=False)

    dt = lambda n, sh, ty=F32, kind="ExternalInput": nc.dram_tensor(n, sh, ty, kind=kind).ap()
    di = lambda n, sh, ty=F32: nc.dram_tensor(n, sh, ty).ap()

    xb = dt("xb", [S2, D])
    gamma_b = dt("gamma_b", [128, 1])
    eye = dt("eye", [128, 128], BF16)
    ones_col = dt("ones_col", [128, 1])
    ones_row = dt("ones_row", [1, 128])
    consts_v = dt("consts_v", [128, 3])
    mask0 = dt("mask0", [128, 128])
    mask1 = dt("mask1", [128, 128])
    rwin_row = dt("rwin_row", [1, T])
    eye32 = dt("eye32", [128, 128])
    repm = dt("repm", [8, 128])
    FTB0 = 16
    KVW0 = c.KVH * 128
    SW0 = min(512, KVW0)
    NS0 = KVW0 // SW0
    w_upT = dt("w_upT", [c.FT, 128, D])
    w_downT = dt("w_downT", [c.NFB, c.DSL, 128, FTB0 * 512])
    wq = dt("wq", [c.NH, 128, D])
    wk = dt("wk", [NS0, 128, DC * SW0])
    wv = dt("wv", [NS0, 128, DC * SW0])
    w_denseT = dt("w_denseT", [c.DSL, 128, c.NH * 512])
    WT = dt("WT", [c.DSL, 128, DC * 512])
    b_row = dt("b_row", [1, D])
    A_T = dt("A_T", [128, DC * 128])
    B_T = dt("B_T", [128, D])
    rW_T = dt("rW_T", [128, DC * 8])
    rb_row = dt("rb_row", [1, 8])
    nsh = c.D * c.UPSL // 8
    wup_shard = dt("wup_shard", [nsh, 512])
    wdn_shard = dt("wdn_shard", [F // 8, D])
    out = dt("out", [T, D], F32, "ExternalOutput")
    outv = out.rearrange("(t p) d -> t p d", p=128)

    dbgs = dt("dbgs", [128, 32 * J], F32, "ExternalOutput") if dbg else None
    dbge = dt("dbge", [128, 4096], F32, "ExternalOutput") if dbg else None
    v_park = di("v_park", [J, 128, D])
    x1_park = di("x1_park", [J, 128, D])
    sc_src = di("sc_src", [1, 16])
    sc_dst = di("sc_dst", [1, 16])
    rh_pk = di("rh_pk", [c.FT, 128, T], F16)
    pdn = di("pdn", [c.NFB, J, 128, D])
    x2_park = di("x2_park", [J, 128, D])
    v3_park = di("v3_park", [J, 128, D])
    x3_park = di("x3_park", [J, 128, D])
    b4_park = di("b4_park", [J, 128, D])
    qT_pk = di("qT_pk", [c.NH, 128, T], BF16)
    att_pk = di("att_pk", [J, 128, c.NH * 128], BF16)
    kv_srcK = di("kv_srcK", [128, c.KVH * T], BF16)
    kv_srcV = di("kv_srcV", [128, c.KVH * T], BF16)
    kv_dstK = di("kv_dstK", [256, c.KVH * T], BF16)
    kv_dstV = di("kv_dstV", [256, c.KVH * T], BF16)
    q_src = di("q_src", [D])
    q_dst = di("q_dst", [D])
    rt_d = di("rt_d", [8])
    s2_d = di("s2_d", [T])

    xbv = xb.rearrange("(n p) d -> n p d", p=128)

    stack = ExitStack()
    kb = KB(nc, stack)

    def sb(st_, n, sh, ty=F32, side="left"):
        return st_.enter_context(nc.sbuf_tensor(n, sh, ty, side=side))

    def psb(n, sh, ty=F32):
        return Bank(stack.enter_context(nc.psum_tensor(n, sh, ty))[:])

    with stack:
        eye_s = sb(stack, "eye_s", [128, 128], BF16, side="right")
        mk0_s = sb(stack, "mk0_s", [128, 128], side="right")
        mk1_s = sb(stack, "mk1_s", [128, 128], side="right")
        gam_s = sb(stack, "gam_s", [128, 1], side="right")
        onec_s = sb(stack, "onec_s", [128, 1], side="right")
        oner_s = sb(stack, "oner_s", [1, 128], side="right")
        cv_s = sb(stack, "cv_s", [128, 3], side="right")
        eye32_s = sb(stack, "eye32_s", [128, 128], side="right")
        repm_s = sb(stack, "repm_s", [8, 128], side="right")
        rwr_s = sb(stack, "rwr_s", [1, T], side="right")
        stat = sb(stack, "stat", [128, 32 * J], side="right")
        sqp = sb(stack, "sqp", [128, J * c.DSL], side="right")
        sc_s = sb(stack, "sc_s", [1, 16], side="right")
        scb_s = sb(stack, "scb_s", [128, 8], side="right")

        ld_eye = kb.dma("Y", eye_s[:], eye, ch="c0")
        ld_mk0 = kb.dma("Y", mk0_s[:], mask0, ch="c1")
        ld_mk1 = kb.dma("Y", mk1_s[:], mask1, ch="c2")
        ld_gam = kb.dma("Y", gam_s[:], gamma_b, ch="c3")
        ld_one = kb.dma("Y", onec_s[:], ones_col, ch="c4")
        ld_oner = kb.dma("Y", oner_s[:], ones_row, ch="c7")
        ld_cv = kb.dma("Y", cv_s[:], consts_v, ch="c5")
        ld_eye32 = kb.dma("Y", eye32_s[:], eye32, ch="c6")
        ld_repm = kb.dma("Y", repm_s[:], repm, ch="c8")
        ld_rwr = kb.dma("Y", rwr_s[:], rwin_row, ch="c9")
        z_stat = kb.emit("V", lambda e: e.memset(stat[:], 0.0))
        z_sqp = kb.emit("V", lambda e: e.memset(sqp[:], 0.0))

        C_MIN, C_ESUM, C_PVS = 0, 1, 2
        C_SUM, C_MU, C_VAR, C_RST, C_NMU = 3, 4, 5, 6, 7
        C_AM1, C_S1, C_IS1 = 8, 9, 10
        C_AM2, C_S2, C_C2 = 11, 12, 13
        C_T0, C_T1, C_T2, C_T3 = 14, 15, 16, 17
        C_C1, C_HMX, C_S2P, C_R2M = 18, 19, 20, 21

        def st(j, col, n=1):
            return stat[:, 32 * j + col:32 * j + col + n]

        mmb = [psb(f"mm{i}", [128, 512]) for i in range(6)]
        tpb = [psb("tp0", [128, 1024], BF16), psb("tp1", [128, 1024], BF16)]
        tp = TP(kb, [Bank(b.ap[:, 0:128]) for b in tpb], eye_s[:])

        w1_stack = stack.enter_context(ExitStack())
        accw = sb(w1_stack, "accw", [128, 2])
        shard_b = [sb(w1_stack, f"shard_b{i}", [128, 4096], BF16) for i in range(2)]
        z0 = kb.emit("V", lambda e: e.memset(accw[:], 0.0))
        upv = wup_shard.rearrange("(n p) k -> n p k", p=128)
        dnv = wdn_shard.rearrange("(n p) k -> n p k", p=128)
        jobs = []
        k_up = 4096 // 512
        n_up_tiles = nsh // 128
        for i in range(0, n_up_tiles, k_up):
            n = min(k_up, n_up_tiles - i)
            jobs.append((0, upv[i:i + n], n, 512))
        k_dn = max(1, 4096 // D)
        n_dn_tiles = F // 8 // 128
        for i in range(0, n_dn_tiles, k_dn):
            n = min(k_dn, n_dn_tiles - i)
            jobs.append((1, dnv[i:i + n], n, D))
        sh_free = [z0, z0]
        vtok = z0
        for ji, (which, ap, n, kw_) in enumerate(jobs):
            bi = ji % 2
            width = n * kw_
            dsb = shard_b[bi][:, 0:width]
            dtok = kb.dma("G", dsb.rearrange("p (n k) -> p n k", n=n),
                          ap.rearrange("n p k -> p n k"),
                          waits=[sh_free[bi]], ch=f"sh{bi}")
            rtok = kb.emit("V", lambda e, d=dsb: e.reduce_sum(
                st(0, C_T3), d, AXX, apply_absolute_value=True), waits=[dtok, vtok])
            vtok = kb.emit("V", lambda e, w=which: e.tensor_add(
                accw[:, w:w + 1], accw[:, w:w + 1], st(0, C_T3)), waits=[rtok])
            sh_free[bi] = vtok
        tsc = kb.emit("T", lambda e: e.matmul(mmb[4].ap[0:1, 0:2], onec_s[:, 0:1],
                                              accw[:], start=True, stop=True),
                      waits=[vtok, ld_one, mmb[4].free])
        zs = kb.emit("V", lambda e: e.memset(sc_s[:], 0.0))
        cpy = kb.emit("V", lambda e: e.tensor_copy(sc_s[:, 0:2], mmb[4].ap[0:1, 0:2]),
                      waits=[tsc, zs])
        mmb[4].free = cpy
        dcc = kb.dma("G", sc_src, sc_s[:], waits=[cpy], sbuf_dest=False, ch="sc")
        cc1 = kb.emit("G", lambda e: e.collective_compute(
            "AllReduce", AOT.add, replica_groups=[[0, 1, 2, 3, 4, 5, 6, 7]],
            ins=[sc_src], outs=[sc_dst]), waits=[dcc])
        dcb = kb.dma("G", sc_s[:], sc_dst, waits=[cc1], ch="sc")
        t1 = kb.emit("V", lambda e: e.tensor_scalar(sc_s[:, 2:3], sc_s[:, 0:1],
                                                    1.0 / (D * F), LN_EPS, AOT.mult, AOT.add),
                     waits=[dcb, cpy])
        t2 = kb.emit("V", lambda e: e.tensor_scalar(sc_s[:, 4:5], sc_s[:, 1:2],
                                                    1.0 / (D * F), LN_EPS, AOT.mult, AOT.add),
                     waits=[dcb])
        t3 = kb.emit("V", lambda e: e.reciprocal(sc_s[:, 3:4], sc_s[:, 2:3]), waits=[t1])
        sc_done = kb.emit("V", lambda e: e.reciprocal(sc_s[:, 5:6], sc_s[:, 4:5]),
                          waits=[t2, t3])
        tb2 = kb.emit("T", lambda e: e.matmul(mmb[5].ap[:, 0:8], oner_s[0:1, :],
                                              sc_s[0:1, 0:8], start=True, stop=True),
                      waits=[sc_done, ld_oner, mmb[5].free])
        scb_done = kb.emit("V", lambda e: e.tensor_copy(scb_s[:], mmb[5].ap[:, 0:8]),
                           waits=[tb2])
        mmb[5].free = scb_done
        w1_stack.close()
        kb.close_regions()

        pt_stack = stack.enter_context(ExitStack())
        ab_stack = stack.enter_context(ExitStack())
        JH = max(1, J // 2)
        xq_fm = sb(ab_stack, "xq_fm", [128, DC * T], BF16)
        pan_tm = sb(ab_stack, "pan_tm", [128, 2 * D], BF16)
        pan_fm = sb(ab_stack, "pan_fm", [128, DC * 512], BF16)
        e_buf = sb(ab_stack, "e_buf", [128, JH * S2])
        p_buf = sb(ab_stack, "p_buf", [128, S2], BF16)
        p_T = sb(pt_stack, "p_T", [128, KT * T], BF16, side="right")

        NPAN = S2 // 512
        OWN_PAN = T // 512
        pan_rd_free = [None]
        xq_built = [False]
        pvs_tok = [None] * J
        pT_last = [None]
        min_tok = [None] * J

        def load_panel(pnl, into_own):
            ready = None
            for h in range(2):
                dtok = kb.dma("G", pan_tm[:].rearrange("p (t d) -> p t d", t=2),
                              xbv[4 * pnl + 2 * h: 4 * pnl + 2 * h + 2]
                              .rearrange("t p d -> p t d"),
                              waits=[pan_rd_free[0]], ch=f"pan{h}")
                for tt in range(2):
                    for dc in range(DC):
                        src = pan_tm[:, tt * D + dc * 128: tt * D + dc * 128 + 128]
                        if into_own:
                            col = (4 * pnl + 2 * h + tt) * 128
                            dst = xq_fm[:, dc * T + col: dc * T + col + 128]
                        else:
                            dst = pan_fm[:, dc * 512 + (2 * h + tt) * 128:
                                         dc * 512 + (2 * h + tt) * 128 + 128]
                        last_t, ready = tp.run(src, dst, waits=[dtok, ld_eye])
                pan_rd_free[0] = last_t
            return ready

        for half, qts in enumerate([list(range(0, JH)), list(range(JH, J))]):
            if not qts:
                continue
            if not xq_built[0]:
                own_ready = None
                for pnl in range(OWN_PAN):
                    own_ready = load_panel(pnl, True)
                xq_built[0] = own_ready
            for pnl in range(NPAN):
                if pnl < OWN_PAN:
                    rhs = [xq_fm[:, dc * T + pnl * 512: dc * T + pnl * 512 + 512]
                           for dc in range(DC)]
                    pan_ready = xq_built[0]
                else:
                    pan_ready = load_panel(pnl, False)
                    rhs = [pan_fm[:, dc * 512: dc * 512 + 512] for dc in range(DC)]
                for qi, qt in enumerate(qts):
                    bank = mmb[qi % 4]
                    mms = [(xq_fm[:, dc * T + qt * 128: dc * T + qt * 128 + 128], rhs[dc])
                           for dc in range(DC)]
                    mt = mm_group(kb, bank, bank.ap, mms, waits=[pan_ready])
                    dst = e_buf[:, qi * S2 + pnl * 512: qi * S2 + pnl * 512 + 512]
                    ev = kb.emit("S", lambda e, d=dst, s=bank.ap: e.copy(d, s), waits=[mt])
                    bank.free = ev
                    rt = kb.emit("V", lambda e, d=dst, j=qt: e.tensor_reduce(
                        st(j, C_T0), d, AXX, AOT.min), waits=[ev, min_tok[qt]])
                    if min_tok[qt] is None:
                        min_tok[qt] = kb.emit("V", lambda e, j=qt: e.tensor_copy(
                            st(j, C_MIN), st(j, C_T0)), waits=[rt])
                    else:
                        min_tok[qt] = kb.emit("V", lambda e, j=qt: e.tensor_tensor(
                            st(j, C_MIN), st(j, C_MIN), st(j, C_T0), AOT.min),
                            waits=[rt])
            pbuf_free = None
            for qi, qt in enumerate(qts):
                exp_tok = None
                add_tok = None
                for kk in range(S2 // 512):
                    src = e_buf[:, qi * S2 + kk * 512: qi * S2 + kk * 512 + 512]
                    dst = p_buf[:, kk * 512: kk * 512 + 512]
                    acc = st(qt, C_ESUM) if kk == 0 else st(qt, C_T1)
                    exp_tok = kb.emit("S", lambda e, s=src, d=dst, a=acc, j=qt: e.activation(
                        d, s, AFT.Exp, bias=st(j, C_MIN), scale=-1.0, accum_out=a),
                        waits=[min_tok[qt], pbuf_free, add_tok])
                    if kk > 0:
                        add_tok = kb.emit("V", lambda e, j=qt: e.tensor_add(
                            st(j, C_ESUM), st(j, C_ESUM), st(j, C_T1)),
                            waits=[exp_tok] + ([add_tok] if add_tok else []))
                rc = kb.emit("V", lambda e, j=qt: e.reciprocal(st(j, C_PVS), st(j, C_ESUM)),
                             waits=[exp_tok, add_tok])
                pvs_tok[qt] = kb.emit("V", lambda e, j=qt: e.tensor_scalar(
                    st(j, C_PVS), st(j, C_PVS), gam_s[:, 0:1], 0.5, AOT.mult, AOT.mult),
                    waits=[rc, ld_gam])
                ttok = None
                for kt in range(KT):
                    src = p_buf[:, kt * 128: kt * 128 + 128]
                    dst = p_T[:, kt * T + qt * 128: kt * T + qt * 128 + 128]
                    ttok, pT_last[0] = tp.run(src, dst, waits=[exp_tok], evac="S")
                pbuf_free = ttok
        pT_done = pT_last[0]

        if dbg == "E":
            kb.dma("Y", dbgs, stat[:, 0:32 * J], waits=[kb.tok("V"), kb.tok("S")],
                   sbuf_dest=False, ch="out")
            kb.dma("Y", dbge, e_buf[:, 0:4096], waits=[kb.tok("S")], sbuf_dest=False, ch="out")
            kb.finish()
            with nc.Block() as block:
                kb.install(block)
            return nc

        ab_stack.close()
        kb.close_regions()

        c_stack = stack.enter_context(ExitStack())
        xcol = [sb(c_stack, f"xcol{i}", [128, KT * 512], BF16) for i in range(2)]
        xown = [sb(c_stack, f"xown{i}", [128, J * 512]) for i in range(2)]
        vstg = [sb(c_stack, f"vstg{i}", [128, J * 512]) for i in range(2)]
        sq_dump = sb(c_stack, "sq_dump", [128, 512])
        xc_free = [None, None]
        xo_free = [None, None]
        v_free = [None, None]
        sum_tok = [None] * J
        sq_last = [None] * J
        sqd_tok = None
        for ds in range(c.DSL):
            bi = ds % 2
            ctok = kb.dma("G", xcol[bi][:].rearrange("p (t k) -> p t k", t=KT),
                          xbv[:, :, ds * 512:(ds + 1) * 512].rearrange("t p k -> p t k"),
                          waits=[xc_free[bi]], ch=f"xc{bi}")
            otok = kb.dma("Y", xown[bi][:].rearrange("p (t k) -> p t k", t=J),
                          xbv[0:J, :, ds * 512:(ds + 1) * 512].rearrange("t p k -> p t k"),
                          waits=[xo_free[bi]], ch=f"xo{bi}")
            last_mm = None
            for qt in range(J):
                bank = mmb[qt % 6]
                mms = [(p_T[:, kt * T + qt * 128: kt * T + qt * 128 + 128],
                        xcol[bi][:, kt * 512: kt * 512 + 512]) for kt in range(KT)]
                last_mm = mm_group(kb, bank, bank.ap, mms, waits=[ctok, pT_done])
                dst = vstg[bi][:, qt * 512: qt * 512 + 512]
                xo = xown[bi][:, qt * 512: qt * 512 + 512]
                ev = kb.emit("V", lambda e, d=dst, s=bank.ap, x=xo, j=qt: e.scalar_tensor_tensor(
                    d, s, st(j, C_PVS), x, AOT.mult, AOT.add),
                    waits=[last_mm, otok, v_free[bi], pvs_tok[qt]])
                bank.free = ev
                rt = kb.emit("V", lambda e, d=dst, j=qt: e.reduce_sum(st(j, C_T0), d, AXX),
                             waits=[ev] + ([sum_tok[qt]] if sum_tok[qt] else []))
                if ds == 0:
                    sum_tok[qt] = kb.emit("V", lambda e, j=qt: e.tensor_copy(
                        st(j, C_SUM), st(j, C_T0)), waits=[rt])
                else:
                    sum_tok[qt] = kb.emit("V", lambda e, j=qt: e.tensor_add(
                        st(j, C_SUM), st(j, C_SUM), st(j, C_T0)), waits=[rt])
                sq_last[qt] = kb.emit("S", lambda e, d=dst, j=qt, k=ds: e.activation(
                    sq_dump[:], d, AFT.Square,
                    accum_out=sqp[:, j * c.DSL + k: j * c.DSL + k + 1]),
                    waits=[ev] + ([sqd_tok] if sqd_tok else []))
                sqd_tok = sq_last[qt]
            ptok = kb.dma("S", v_park[:, :, ds * 512:(ds + 1) * 512].rearrange("t p k -> p t k"),
                          vstg[bi][:].rearrange("p (t k) -> p t k", t=J),
                          waits=[sq_last[J - 1]], sbuf_dest=False, ch="vpk")
            v_free[bi] = ptok
            xc_free[bi] = last_mm
            xo_free[bi] = sum_tok[J - 1]

        if dbg == "v1":
            kb.dma("Y", dbgs, stat[:, 0:32 * J], waits=[kb.tok("V"), kb.tok("S")],
                   sbuf_dest=False, ch="out")
            for qt in range(J):
                kb.dma("Y", outv[qt], v_park[qt], waits=[kb.tok("dS_vpk")],
                       sbuf_dest=False, ch="out")
            kb.finish()
            with nc.Block() as block:
                kb.install(block)
            return nc

        c_stack.close()
        pt_stack.close()
        kb.close_regions()

        d_outer = stack.enter_context(ExitStack())
        M_up = sb(d_outer, "M_up", [128, T])
        Vs2 = sb(d_outer, "Vs2", [128, T])
        s2c = sb(d_outer, "s2c", [128, 8])
        s2row = sb(d_outer, "s2row", [1, T])
        up_stack = stack.enter_context(ExitStack())
        xiT = sb(up_stack, "xiT", [128, DC * T], BF16)
        ln1_stack = stack.enter_context(ExitStack())
        v_in = [sb(ln1_stack, f"v_in{i}", [128, D]) for i in range(2)]
        x1t = sb(ln1_stack, "x1t", [128, D])
        xit = sb(ln1_stack, "xit", [128, D], BF16)
        vin_free = [None, None]
        x1t_free = None
        xit_free = None
        x1_done = [None] * J
        c1_tok = [None] * J
        xiT_ev = None
        parks_done = kb.tok("dS_vpk")
        for qt in range(J):
            bi = qt % 2
            dtok = kb.dma("Y", v_in[bi][:], v_park[qt], waits=[vin_free[bi], parks_done],
                          ch=f"vi{bi}")
            a1 = kb.emit("V", lambda e, j=qt: e.tensor_scalar(
                st(j, C_MU), st(j, C_SUM), 1.0 / D, None, AOT.mult),
                waits=[sum_tok[qt]])
            a2 = kb.emit("V", lambda e, j=qt: e.tensor_reduce(
                st(j, C_VAR), sqp[:, j * c.DSL:(j + 1) * c.DSL], AXX, AOT.add),
                waits=[sq_last[qt], sqd_tok])
            a3 = kb.emit("V", lambda e, j=qt: e.tensor_scalar(
                st(j, C_VAR), st(j, C_VAR), 1.0 / D, None, AOT.mult), waits=[a2])
            a4 = kb.emit("V", lambda e, j=qt: e.tensor_tensor(
                st(j, C_T0), st(j, C_MU), st(j, C_MU), AOT.mult), waits=[a1])
            var_tok = kb.emit("V", lambda e, j=qt: e.tensor_tensor(
                st(j, C_VAR), st(j, C_VAR), st(j, C_T0), AOT.subtract), waits=[a3, a4])
            sd_tok = kb.emit("S", lambda e, j=qt: e.activation(
                st(j, C_RST), st(j, C_VAR), AFT.Sqrt, bias=cv_s[:, 0:1]),
                waits=[var_tok, ld_cv])
            r1 = kb.emit("V", lambda e, j=qt: e.reciprocal(st(j, C_RST), st(j, C_RST)),
                         waits=[sd_tok])
            r2_ = kb.emit("V", lambda e, j=qt: e.tensor_tensor(
                st(j, C_NMU), st(j, C_MU), st(j, C_RST), AOT.mult), waits=[r1, a1])
            nmu_tok = kb.emit("V", lambda e, j=qt: e.tensor_scalar(
                st(j, C_NMU), st(j, C_NMU), -1.0, None, AOT.mult), waits=[r2_])
            ln_tok = kb.emit("S", lambda e, j=qt, b=bi: e.activation(
                x1t[:], v_in[b][:], AFT.Identity, bias=st(j, C_NMU), scale=st(j, C_RST)),
                waits=[dtok, nmu_tok, x1t_free])
            vin_free[bi] = ln_tok
            x1_done[qt] = kb.dma("S", x1_park[qt], x1t[:], waits=[ln_tok],
                                 sbuf_dest=False, ch="x1p")
            m1 = kb.emit("V", lambda e, j=qt: e.reduce_max(
                st(j, C_AM1), x1t[:], AXX, apply_absolute_value=True), waits=[ln_tok])
            m2 = kb.emit("V", lambda e, j=qt: e.tensor_scalar(
                st(j, C_AM1), st(j, C_AM1), 1e-5, None, AOT.max), waits=[m1])
            m3 = kb.emit("V", lambda e, j=qt: e.reciprocal(st(j, C_S1), st(j, C_AM1)),
                         waits=[m2])
            m4 = kb.emit("V", lambda e, j=qt: e.tensor_scalar(
                st(j, C_S1), st(j, C_S1), 127.0, None, AOT.mult), waits=[m3])
            s1_tok = kb.emit("V", lambda e, j=qt: e.tensor_scalar(
                st(j, C_IS1), st(j, C_AM1), 1.0 / 127, None, AOT.mult), waits=[m2])
            c1_tok[qt] = kb.emit("V", lambda e, j=qt: e.tensor_tensor(
                st(j, C_C1), st(j, C_IS1), scb_s[:, 2:3], AOT.mult),
                waits=[s1_tok, scb_done])
            q1 = kb.emit("S", lambda e, j=qt: e.activation(
                x1t[:], x1t[:], AFT.Identity, bias=cv_s[:, 1:2], scale=st(j, C_S1)),
                waits=[m4, x1_done[qt], m1, ln_tok])
            q2 = kb.emit("V", lambda e: e.tensor_scalar(
                x1t[:], x1t[:], MAGIC - 128.0, MAGIC + 127.0, AOT.max, AOT.min),
                waits=[q1])
            q3 = kb.emit("V", lambda e: e.tensor_scalar(
                xit[:], x1t[:], MAGIC, None, AOT.subtract), waits=[q2, xit_free])
            x1t_free = q3
            ttok = None
            for dc in range(DC):
                src = xit[:, dc * 128: dc * 128 + 128]
                dst = xiT[:, dc * T + qt * 128: dc * T + qt * 128 + 128]
                ttok, xiT_ev = tp.run(src, dst, waits=[q3], evac="S")
            xit_free = ttok

        if dbg == "x1":
            kb.dma("Y", dbgs, stat[:, 0:32 * J], waits=[kb.tok("V"), kb.tok("S")],
                   sbuf_dest=False, ch="out")
            for qt in range(J):
                kb.dma("Y", outv[qt], x1_park[qt], waits=[x1_done[qt]],
                       sbuf_dest=False, ch="out")
            kb.finish()
            with nc.Block() as block:
                kb.install(block)
            return nc

        ln1_stack.close()
        kb.close_regions()

        C_SUM2, C_MU2, C_VAR2, C_RST2, C_NMU2, C_T4 = 22, 23, 24, 25, 26, 27
        C_SMX, C_SNEG, C_SSUM, C_SRC = 28, 29, 30, 31
        sq_free = [None]
        stat2_free = [None] * J

        def ln_apply(j, src, dst, sqbuf, deps):
            a_s = kb.emit("V", lambda e: e.reduce_sum(st(j, C_SUM2), src, AXX),
                          waits=list(deps) + _toks(stat2_free[j]))
            a_q = kb.emit("S", lambda e: e.activation(
                sqbuf, src, AFT.Square, accum_out=st(j, C_VAR2)),
                waits=list(deps) + _toks(sq_free[0]) + _toks(stat2_free[j]))
            sq_free[0] = a_q
            m1 = kb.emit("V", lambda e: e.tensor_scalar(
                st(j, C_MU2), st(j, C_SUM2), 1.0 / D, None, AOT.mult), waits=[a_s])
            m2 = kb.emit("V", lambda e: e.tensor_scalar(
                st(j, C_VAR2), st(j, C_VAR2), 1.0 / D, None, AOT.mult), waits=[a_q])
            m3 = kb.emit("V", lambda e: e.tensor_tensor(
                st(j, C_T4), st(j, C_MU2), st(j, C_MU2), AOT.mult), waits=[m1])
            m4 = kb.emit("V", lambda e: e.tensor_tensor(
                st(j, C_VAR2), st(j, C_VAR2), st(j, C_T4), AOT.subtract),
                waits=[m2, m3])
            m5 = kb.emit("S", lambda e: e.activation(
                st(j, C_RST2), st(j, C_VAR2), AFT.Sqrt, bias=cv_s[:, 0:1]),
                waits=[m4, ld_cv])
            m6 = kb.emit("V", lambda e: e.reciprocal(st(j, C_RST2), st(j, C_RST2)),
                         waits=[m5])
            m7 = kb.emit("V", lambda e: e.tensor_tensor(
                st(j, C_NMU2), st(j, C_MU2), st(j, C_RST2), AOT.mult),
                waits=[m6, m1])
            m8 = kb.emit("V", lambda e: e.tensor_scalar(
                st(j, C_NMU2), st(j, C_NMU2), -1.0, None, AOT.mult), waits=[m7])
            ap_ = kb.emit("S", lambda e: e.activation(
                dst, src, AFT.Identity, bias=st(j, C_NMU2), scale=st(j, C_RST2)),
                waits=list(deps) + [m8])
            stat2_free[j] = ap_
            return ap_, a_s

        bank_i = [0]

        def nb():
            b = mmb[bank_i[0] % 6]
            bank_i[0] += 1
            return b

        NTC = T // 512
        FTB = 2048 // 128
        wup_stg = sb(up_stack, "wup_stg", [128, DC * 128])
        wup_tmp = sb(up_stack, "wup_tmp", [128, DC * 128])
        wuq = [sb(up_stack, f"wuq{i}", [128, DC * 128], BF16) for i in range(2)]
        rstg = [sb(up_stack, f"rstg{i}", [128, T], F16) for i in range(2)]
        zM = kb.emit("V", lambda e: e.memset(M_up[:], 0.0))
        wstg_free = None
        wtmp_free = None
        wuq_free = [None, None]
        rstg_free = [None, None]
        mx_tok = [zM] * NTC
        xiT_done = xiT_ev
        up2 = dbg is not None and dbg.startswith("up2")
        FT_eff = 2 if up2 else c.FT
        for ot in range(FT_eff):
            bi = ot % 2
            dtok = kb.dma("Y", wup_stg[:], w_upT[ot],
                          waits=[wstg_free], ch="wup")
            uq1 = kb.emit("S", lambda e: e.activation(
                wup_tmp[:], wup_stg[:], AFT.Identity, bias=cv_s[:, 1:2],
                scale=scb_s[:, 3:4]),
                waits=[dtok, scb_done, ld_cv, wtmp_free])
            wstg_free = uq1
            uq2 = kb.emit("V", lambda e: e.tensor_scalar(
                wup_tmp[:], wup_tmp[:], MAGIC - 1.0, MAGIC + 1.0, AOT.max, AOT.min),
                waits=[uq1])
            uq3 = kb.emit("V", lambda e, b=bi: e.tensor_scalar(
                wuq[b][:], wup_tmp[:], MAGIC, None, AOT.subtract),
                waits=[uq2] + _toks(wuq_free[bi]))
            wtmp_free = uq3
            rel_toks = []
            mt = None
            for tc in range(NTC):
                bank = nb()
                mms = [(wuq[bi][:, dc * 128: dc * 128 + 128],
                        xiT[:, dc * T + tc * 512: dc * T + tc * 512 + 512])
                       for dc in range(DC)]
                mt = mm_group(kb, bank, bank.ap, mms, waits=[uq3, xiT_done])
                rel = kb.emit("S", lambda e, b=bi, t=tc, a=bank.ap:
                              e.activation(
                                  rstg[b][:, t * 512: t * 512 + 512], a,
                                  AFT.Relu, scale=cv_s[:, 2:3]),
                              waits=[mt, ld_cv] + _toks(rstg_free[bi]))
                rel_toks.append(rel)
                bank.free = [rel]
                mx = kb.emit("V", lambda e, b=bi, t=tc: e.tensor_max(
                    M_up[:, t * 512: t * 512 + 512],
                    rstg[b][:, t * 512: t * 512 + 512],
                    M_up[:, t * 512: t * 512 + 512]),
                    waits=[rel, mx_tok[tc]])
                mx_tok[tc] = mx
                rel_toks.append(mx)
            ptok = kb.dma("S", rh_pk[ot], rstg[bi][:], waits=rel_toks,
                          sbuf_dest=False, ch=f"rhp{bi}")
            rstg_free[bi] = [ptok]
            wuq_free[bi] = [mt]

        if dbg == "upq" or up2:
            kb.dma("Y", dbgs, stat[:, 0:32 * J], waits=[kb.tok("V"), kb.tok("S")],
                   sbuf_dest=False, ch="out")
            for qt in range(J):
                kb.dma("Y", outv[qt], x1_park[qt], waits=[x1_done[qt]],
                       sbuf_dest=False, ch="out")
            kb.finish()
            with nc.Block() as block:
                kb.install(block)
            return nc

        c2_tok = [None] * J
        s2c_toks = []
        for j in range(J):
            bank = nb()
            mmt = kb.emit("T", lambda e, j_=j, a=bank.ap: e.matmul(
                a[:, 0:128], M_up[:, j_ * 128: j_ * 128 + 128], eye32_s[:],
                start=True, stop=True),
                waits=_toks(bank.free) + [mx_tok[(j * 128) // 512], ld_eye32])
            hv = kb.emit("V", lambda e, j_=j, a=bank.ap: e.tensor_reduce(
                st(j_, C_HMX), a[:, 0:128], AXX, AOT.max), waits=[mmt])
            bank.free = hv
            e1 = kb.emit("V", lambda e, j_=j: e.tensor_tensor(
                st(j_, C_T2), st(j_, C_HMX), st(j_, C_HMX), AOT.mult), waits=[hv])
            e2 = kb.emit("V", lambda e, j_=j: e.tensor_scalar(
                st(j_, C_T0), st(j_, C_T2), 1e-30, None, AOT.max), waits=[e1])
            e3 = kb.emit("V", lambda e, j_=j: e.reciprocal(
                st(j_, C_T1), st(j_, C_T0)), waits=[e2])
            e4 = kb.emit("V", lambda e, j_=j: e.tensor_scalar(
                st(j_, C_S2P), st(j_, C_T1), 127.0, None, AOT.mult), waits=[e3])
            e5 = kb.emit("V", lambda e, j_=j: e.tensor_copy(
                s2c[:, j_:j_ + 1], st(j_, C_S2P)), waits=[e4])
            s2c_toks.append(e5)
            e6 = kb.emit("V", lambda e, j_=j: e.tensor_tensor(
                st(j_, C_T3), st(j_, C_C1), st(j_, C_C1), AOT.mult),
                waits=[c1_tok[j]])
            e7 = kb.emit("V", lambda e, j_=j: e.scalar_tensor_tensor(
                st(j_, C_R2M), st(j_, C_T2), 256.0, st(j_, C_T3),
                AOT.mult, AOT.mult),
                waits=[e6, e1])
            e8 = kb.emit("V", lambda e, j_=j: e.tensor_scalar(
                st(j_, C_R2M), st(j_, C_R2M), 1e-5, None, AOT.max), waits=[e7])
            c2_tok[j] = kb.emit("V", lambda e, j_=j: e.tensor_scalar(
                st(j_, C_C2), st(j_, C_R2M), scb_s[:, 4:5], 1.0 / 127,
                AOT.mult, AOT.mult), waits=[e8, scb_done])
        s2r8 = sb(up_stack, "s2r8", [J, 128])
        bank = nb()
        mmtr = kb.emit("T", lambda e, a=bank.ap: e.matmul(
            a[0:J, 0:128], s2c[:, 0:J], eye32_s[:], start=True, stop=True),
            waits=_toks(bank.free) + s2c_toks + [ld_eye32])
        cp8 = kb.emit("V", lambda e, a=bank.ap: e.tensor_copy(
            s2r8[:], a[0:J, 0:128]), waits=[mmtr])
        bank.free = cp8
        ds2w = kb.dma("S", s2_d.rearrange("(j ccc) -> j ccc", j=J), s2r8[:],
                      waits=[cp8], sbuf_dest=False, ch="s2d")
        rc_last = kb.dma("Y", s2row[0:1, 0:T],
                         s2_d.rearrange("(o t) -> o t", o=1),
                         waits=[ds2w], ch="s2d2")
        vs2_done = None
        for tc in range(NTC):
            bank2 = nb()
            bmt = kb.emit("T", lambda e, t=tc, a=bank2.ap: e.matmul(
                a, oner_s[:], s2row[0:1, t * 512: t * 512 + 512],
                start=True, stop=True),
                waits=_toks(bank2.free) + [rc_last, ld_oner])
            vs2_done = kb.emit("V", lambda e, t=tc, a=bank2.ap: e.tensor_copy(
                Vs2[:, t * 512: t * 512 + 512], a), waits=[bmt])
            bank2.free = vs2_done

        if dbg == "up":
            kb.dma("Y", dbgs, stat[:, 0:32 * J], waits=[kb.tok("V"), kb.tok("S")],
                   sbuf_dest=False, ch="out")
            for qt in range(J):
                kb.dma("Y", outv[qt], x1_park[qt], waits=[x1_done[qt]],
                       sbuf_dest=False, ch="out")
            kb.finish()
            with nc.Block() as block:
                kb.install(block)
            return nc

        up_stack.close()
        kb.close_regions()

        down_stack = stack.enter_context(ExitStack())
        rhrd = [sb(down_stack, f"rhrd{i}", [128, T], F16) for i in range(2)]
        dtmp = sb(down_stack, "dtmp", [128, T])
        xi2T = [sb(down_stack, f"xi2T{i}", [128, FTB * T], BF16) for i in range(2)]
        wdn_stg = sb(down_stack, "wdn_stg", [128, FTB * 512])
        wdq = [sb(down_stack, f"wdq{i}", [128, FTB * 512], BF16) for i in range(2)]
        pst = [sb(down_stack, f"pst{i}", [128, 512]) for i in range(4)]
        rh_parked = [kb.tok("dS_rhp0"), kb.tok("dS_rhp1")]
        rhrd_free = [None, None]
        dtmp_free = None
        xi2T_busy = [None, None]
        wstg_free2 = None
        wdq_free = [None, None]
        pst_free = [None] * 4
        pi = 0
        prep_done = [None, None]
        for fb in range(c.NFB):
            xb_ = fb % 2
            pd = None
            for ft in range(FTB):
                bi = ft % 2
                dr = kb.dma("G", rhrd[bi][:], rh_pk[fb * FTB + ft],
                            waits=_toks(rhrd_free[bi]) + rh_parked, ch=f"rhr{bi}")
                v1 = kb.emit("V", lambda e, b=bi: e.tensor_tensor(
                    dtmp[:], rhrd[b][:], rhrd[b][:], AOT.mult),
                    waits=[dr, dtmp_free])
                rhrd_free[bi] = v1
                v2 = kb.emit("V", lambda e: e.tensor_tensor(
                    dtmp[:], dtmp[:], Vs2[:], AOT.mult), waits=[v1, vs2_done])
                v3 = kb.emit("V", lambda e, x=xb_, f=ft: e.tensor_scalar(
                    xi2T[x][:, f * T: (f + 1) * T], dtmp[:], MAGIC, MAGIC,
                    AOT.add, AOT.subtract),
                    waits=[v2] + _toks(xi2T_busy[xb_]))
                dtmp_free = v3
                pd = v3
            prep_done[xb_] = pd
            mm_last = None
            for os_ in range(c.DSL):
                wi = (fb * c.DSL + os_) % 2
                dw = kb.dma("Y", wdn_stg[:], w_downT[fb, os_],
                            waits=[wstg_free2], ch="wdn")
                dq1 = kb.emit("S", lambda e: e.activation(
                    wdn_stg[:], wdn_stg[:], AFT.Identity, bias=cv_s[:, 1:2],
                    scale=scb_s[:, 5:6]), waits=[dw, scb_done, ld_cv])
                dq2 = kb.emit("V", lambda e: e.tensor_scalar(
                    wdn_stg[:], wdn_stg[:], MAGIC - 1.0, MAGIC + 1.0,
                    AOT.max, AOT.min), waits=[dq1])
                dq3 = kb.emit("V", lambda e, w=wi: e.tensor_scalar(
                    wdq[w][:], wdn_stg[:], MAGIC, None, AOT.subtract),
                    waits=[dq2] + _toks(wdq_free[wi]))
                wstg_free2 = dq3
                for t in range(J):
                    bank = nb()
                    mms = [(xi2T[xb_][:, kt * T + t * 128: kt * T + t * 128 + 128],
                            wdq[wi][:, kt * 512:(kt + 1) * 512])
                           for kt in range(FTB)]
                    mt = mm_group(kb, bank, bank.ap, mms,
                                  waits=[dq3, prep_done[xb_]])
                    mm_last = mt
                    p_ = pi % 4
                    ev = kb.emit("V", lambda e, a=bank.ap, q=p_: e.tensor_copy(
                        pst[q][:], a), waits=[mt] + _toks(pst_free[p_]))
                    bank.free = ev
                    pk = kb.dma("S", pdn[fb][t][:, os_ * 512:(os_ + 1) * 512],
                                pst[p_][:], waits=[ev], sbuf_dest=False,
                                ch=f"pdw{p_}")
                    pst_free[p_] = pk
                    pi += 1
                wdq_free[wi] = mm_last
            xi2T_busy[xb_] = mm_last

        down_stack.close()
        d_outer.close()
        kb.close_regions()

        x2_stack = stack.enter_context(ExitStack())
        x2T = sb(x2_stack, "x2T", [128, DC * T], BF16)
        ln2_stack = stack.enter_context(ExitStack())
        acc2 = sb(ln2_stack, "acc2", [128, D])
        pstg = [sb(ln2_stack, f"pstg{i}", [128, D]) for i in range(2)]
        x1rd = sb(ln2_stack, "x1rd", [128, D])
        x2t = sb(ln2_stack, "x2t", [128, D])
        x2b = sb(ln2_stack, "x2b", [128, D], BF16)
        sqd2 = sb(ln2_stack, "sqd2", [128, D])
        pdn_done = [kb.tok(f"dS_pdw{i}") for i in range(4)]
        x1p_done = kb.tok("dS_x1p")
        acc_free = None
        pstg_free = [None, None]
        x1rd_free = None
        x2t_free = None
        x2b_free = None
        x2_done = [None] * J
        x2T_ev = None
        for t in range(J):
            d0 = kb.dma("Y", acc2[:], pdn[0][t],
                        waits=_toks(acc_free) + pdn_done, ch="pr0")
            ad = d0
            for fb in range(1, c.NFB):
                bi = fb % 2
                dfb = kb.dma("G", pstg[bi][:], pdn[fb][t],
                             waits=_toks(pstg_free[bi]) + pdn_done,
                             ch=f"pr{bi + 1}")
                ad = kb.emit("V", lambda e, b=bi: e.tensor_add(
                    acc2[:], acc2[:], pstg[b][:]), waits=[dfb, ad])
                pstg_free[bi] = ad
            dx1 = kb.dma("Y", x1rd[:], x1_park[t],
                         waits=_toks(x1rd_free) + [x1p_done], ch="x1r")
            vv = kb.emit("V", lambda e, j=t: e.scalar_tensor_tensor(
                acc2[:], acc2[:], st(j, C_C2), x1rd[:], AOT.mult, AOT.add),
                waits=[ad, dx1, c2_tok[t]])
            x1rd_free = vv
            apt, asr = ln_apply(t, acc2[:], x2t[:], sqd2[:],
                                [vv] + _toks(x2t_free))
            acc_free = [apt, asr]
            x2_done[t] = kb.dma("S", x2_park[t], x2t[:], waits=[apt],
                                sbuf_dest=False, ch="x2p")
            cb = kb.emit("V", lambda e: e.tensor_copy(x2b[:], x2t[:]),
                         waits=[apt] + _toks(x2b_free))
            x2t_free = [cb, x2_done[t]]
            ttok = None
            for dc in range(DC):
                ttok, x2T_ev = tp.run(
                    x2b[:, dc * 128: dc * 128 + 128],
                    x2T[:, dc * T + t * 128: dc * T + t * 128 + 128],
                    waits=[cb], evac="S")
            x2b_free = ttok

        if dbg == "x2":
            kb.dma("Y", dbgs, stat[:, 0:32 * J], waits=[kb.tok("V"), kb.tok("S")],
                   sbuf_dest=False, ch="out")
            for qt in range(J):
                kb.dma("Y", outv[qt], x2_park[qt], waits=[x2_done[qt]],
                       sbuf_dest=False, ch="out")
            kb.finish()
            with nc.Block() as block:
                kb.install(block)
            return nc

        ln2_stack.close()
        kb.close_regions()

        qkv_stack = stack.enter_context(ExitStack())
        KVW = c.KVH * 128
        wqb = [sb(qkv_stack, f"wqb{i}", [128, DC * 128], BF16) for i in range(2)]
        qstg = [sb(qkv_stack, f"qstg{i}", [128, T], BF16) for i in range(2)]
        wkvb = sb(qkv_stack, "wkvb", [128, DC * 512], BF16)
        kvst = [sb(qkv_stack, f"kvst{i}", [128, 512], BF16) for i in range(4)]
        SC = 0.08838834764831845
        x2T_done = x2T_ev
        wqb_free = [None, None]
        qstg_free = [None, None]
        for h in range(c.NH):
            bi = h % 2
            dq = kb.dma("G", wqb[bi][:], wq[h],
                        waits=_toks(wqb_free[bi]), ch=f"wq{bi}")
            evs = []
            mt = None
            for tc in range(NTC):
                bank = nb()
                mms = [(wqb[bi][:, dc * 128: dc * 128 + 128],
                        x2T[:, dc * T + tc * 512: dc * T + tc * 512 + 512])
                       for dc in range(DC)]
                mt = mm_group(kb, bank, bank.ap, mms, waits=[dq, x2T_done])
                ev = kb.emit("S", lambda e, b=bi, t=tc, a=bank.ap: e.activation(
                    qstg[b][:, t * 512: t * 512 + 512], a, AFT.Identity,
                    scale=SC), waits=[mt] + _toks(qstg_free[bi]))
                evs.append(ev)
                bank.free = ev
            pk = kb.dma("S", qT_pk[h], qstg[bi][:], waits=evs,
                        sbuf_dest=False, ch=f"qtp{bi}")
            qstg_free[bi] = pk
            wqb_free[bi] = mt
        wkv_free = None
        kvst_free = [None] * 4
        ki = 0
        kv_srcK_v = kv_srcK.rearrange("p (kvh tok) -> p kvh tok", kvh=c.KVH)
        kv_srcV_v = kv_srcV.rearrange("p (kvh j dh) -> p kvh j dh",
                                      kvh=c.KVH, j=J)
        for s0 in range(0, KVW, 512):
            sw = min(512, KVW - s0)
            dk = kb.dma("G", wkvb[:, 0:DC * sw], wk[s0 // sw],
                        waits=[wkv_free], ch="wkv")
            last = None
            for m0 in range(0, sw, 128):
                kvh = (s0 + m0) // 128
                for tc in range(NTC):
                    bank = nb()
                    mms = [(wkvb[:, dc * sw + m0: dc * sw + m0 + 128],
                            x2T[:, dc * T + tc * 512: dc * T + tc * 512 + 512])
                           for dc in range(DC)]
                    mt = mm_group(kb, bank, bank.ap, mms, waits=[dk, x2T_done])
                    last = mt
                    kq = ki % 4
                    ev = kb.emit("V", lambda e, a=bank.ap, q=kq: e.tensor_copy(
                        kvst[q][:], a), waits=[mt] + _toks(kvst_free[kq]))
                    bank.free = ev
                    dko = kb.dma("S", kv_srcK_v[:, kvh,
                                               tc * 512: tc * 512 + 512],
                                 kvst[kq][:], waits=[ev], sbuf_dest=False,
                                 ch=f"kvo{kq}")
                    kvst_free[kq] = dko
                    ki += 1
            wkv_free = last
        for s0 in range(0, KVW, 512):
            sw = min(512, KVW - s0)
            nk = sw // 128
            dv = kb.dma("G", wkvb[:, 0:DC * sw], wv[s0 // sw],
                        waits=[wkv_free], ch="wkv")
            last = None
            for t in range(J):
                bank = nb()
                mms = [(x2T[:, dc * T + t * 128: dc * T + t * 128 + 128],
                        wkvb[:, dc * sw: dc * sw + sw]) for dc in range(DC)]
                mt = mm_group(kb, bank, bank.ap[:, 0:sw], mms,
                              waits=[dv, x2T_done])
                last = mt
                kq = ki % 4
                ev = kb.emit("V", lambda e, a=bank.ap, q=kq, w=sw: e.tensor_copy(
                    kvst[q][:, 0:w], a[:, 0:w]), waits=[mt] + _toks(kvst_free[kq]))
                bank.free = ev
                dvo = kb.dma("S", kv_srcV_v[:, s0 // 128: s0 // 128 + nk,
                                            t, :],
                             kvst[kq][:, 0:sw].rearrange(
                                 "p (k dh) -> p k dh", k=nk),
                             waits=[ev], sbuf_dest=False, ch=f"kvo{kq}")
                kvst_free[kq] = dvo
                ki += 1
            wkv_free = last
        kv_out_done = [kb.tok(f"dS_kvo{i}") for i in range(4)]
        ccK = kb.emit("G", lambda e: e.collective_compute(
            "AllGather", AOT.bypass,
            replica_groups=[[0, 1], [2, 3], [4, 5], [6, 7]],
            ins=[kv_srcK], outs=[kv_dstK]), waits=kv_out_done)
        ccV = kb.emit("G", lambda e: e.collective_compute(
            "AllGather", AOT.bypass,
            replica_groups=[[0, 1], [2, 3], [4, 5], [6, 7]],
            ins=[kv_srcV], outs=[kv_dstV]), waits=[ccK])

        qkv_stack.close()
        x2_stack.close()
        kb.close_regions()

        sc_stack = stack.enter_context(ExitStack())
        kT_f = sb(sc_stack, "kT_f", [128, c.KVH * S2], BF16)
        V_f = sb(sc_stack, "V_f", [128, c.KVH * KT * 128], BF16)
        P_T = [sb(sc_stack, f"P_T{i}", [128, KT * T], BF16) for i in range(2)]
        sc_buf = [sb(sc_stack, f"sc_buf{i}", [128, S2]) for i in range(2)]
        p_sm = [sb(sc_stack, f"p_sm{i}", [128, S2], BF16) for i in range(2)]
        qTh = [sb(sc_stack, f"qTh{i}", [128, T], BF16) for i in range(2)]
        attn_h = [sb(sc_stack, f"attn_h{i}", [128, T], BF16) for i in range(2)]
        kT_f4 = kT_f[:].rearrange("p (kvh loc ccc) -> p kvh loc ccc",
                                  kvh=c.KVH, loc=KT)
        V_f4 = V_f[:].rearrange("p (kvh loc dh) -> p kvh loc dh",
                                kvh=c.KVH, loc=KT)
        kv_rd = []
        for q in range(2):
            dk = kb.dma("G", kT_f4[:, :, q * J:(q + 1) * J, :],
                        kv_dstK[q * 128:(q + 1) * 128, :].rearrange(
                            "p (kvh j ccc) -> p kvh j ccc", kvh=c.KVH, j=J),
                        waits=[ccK], ch="agk")
            dv = kb.dma("G", V_f4[:, :, q * J:(q + 1) * J, :],
                        kv_dstV[q * 128:(q + 1) * 128, :].rearrange(
                            "p (kvh j dh) -> p kvh j dh", kvh=c.KVH, j=J),
                        waits=[ccV], ch="agv")
            kv_rd += [dk, dv]
        kv_rd = [kb.tok("dG_agk"), kb.tok("dG_agv")]
        zp = [kb.emit("V", lambda e, i_=i: e.memset(P_T[i_][:], 0.0))
              for i in range(2)]
        qT_parked = [kb.tok("dS_qtp0"), kb.tok("dS_qtp1")]
        qTh_free = [None, None]
        attn_free = [None, None]
        sm_free = [[None] * J, [None] * J]
        psm_free = [None, None]
        scb_free = [None, None]
        pt_readlast = [zp[0], zp[1]]
        SCOL = [(C_SMX, C_SNEG, C_SSUM, C_SRC), (C_T0, C_T1, C_T2, C_T3)]
        for h in range(c.NH):
            kvh = h // c.G
            bi = h % 2
            dqh = kb.dma("Y", qTh[bi][:], qT_pk[h],
                         waits=_toks(qTh_free[bi]) + qT_parked,
                         ch=f"qtr{bi}")
            CX, CN, CS, CR = SCOL[bi]
            last_ev = {"V": None, "S": None}
            last_tp = None
            for qt in range(J):
                W_own = (qt + 1) * 128
                sevs = []
                for reg in range(2):
                    base = kvh * S2 + reg * T
                    dstb = reg * W_own
                    for b0 in range(0, W_own, 512):
                        bw = min(512, W_own - b0)
                        bank = nb()
                        mtS = kb.emit(
                            "T",
                            lambda e, a=bank.ap, b_=bi, q_=qt, ba=base, b0_=b0,
                            w=bw: e.matmul(
                                a[:, 0:w], qTh[b_][:, q_ * 128: q_ * 128 + 128],
                                kT_f[:, ba + b0_: ba + b0_ + w],
                                start=True, stop=True),
                            waits=_toks(bank.free) + [dqh] + kv_rd)
                        ev = kb.emit(
                            "S",
                            lambda e, a=bank.ap, b_=bi, d0=dstb + b0, w=bw:
                            e.copy(sc_buf[b_][:, d0: d0 + w], a[:, 0:w]),
                            waits=[mtS, scb_free[bi]])
                        bank.free = ev
                        sevs.append(ev)
                mk0 = kb.emit("V", lambda e, b_=bi, q_=qt: e.tensor_tensor(
                    sc_buf[b_][:, q_ * 128: q_ * 128 + 128],
                    sc_buf[b_][:, q_ * 128: q_ * 128 + 128], mk0_s[:], AOT.add),
                    waits=sevs + [ld_mk0])
                mk1 = kb.emit("V", lambda e, b_=bi, q_=qt, w=W_own:
                              e.tensor_tensor(
                    sc_buf[b_][:, w + q_ * 128: w + q_ * 128 + 128],
                    sc_buf[b_][:, w + q_ * 128: w + q_ * 128 + 128], mk1_s[:],
                    AOT.add), waits=sevs + [ld_mk1])
                W2 = 2 * W_own
                rmx = kb.emit("V", lambda e, b_=bi, q_=qt, w=W2, cx=CX:
                              e.tensor_reduce(
                    st(q_, cx), sc_buf[b_][:, 0:w], AXX, AOT.max),
                    waits=[mk0, mk1] + _toks(sm_free[bi][qt]))
                ng = kb.emit("V", lambda e, q_=qt, cx=CX, cn=CN:
                             e.tensor_scalar(
                    st(q_, cn), st(q_, cx), -1.0, None, AOT.mult),
                    waits=[rmx])
                ex = kb.emit("S", lambda e, b_=bi, q_=qt, w=W2, cn=CN, cs=CS:
                             e.activation(
                    sc_buf[b_][:, 0:w], sc_buf[b_][:, 0:w], AFT.Exp,
                    bias=st(q_, cn), accum_out=st(q_, cs)),
                    waits=[ng])
                scb_free[bi] = ex
                rc = kb.emit("V", lambda e, q_=qt, cr=CR, cs=CS: e.reciprocal(
                    st(q_, cr), st(q_, cs)), waits=[ex])
                nm = kb.emit("V", lambda e, b_=bi, q_=qt, w=W2, cr=CR:
                             e.tensor_scalar(
                    p_sm[b_][:, 0:w], sc_buf[b_][:, 0:w], st(q_, cr), None,
                    AOT.mult), waits=[rc, psm_free[bi]])
                sm_free[bi][qt] = nm
                for i in range(2 * (qt + 1)):
                    loc = i if i <= qt else J + (i - qt - 1)
                    ttok, evt = tp.run(
                        p_sm[bi][:, i * 128: i * 128 + 128],
                        P_T[bi][:, loc * T + qt * 128: loc * T + qt * 128 + 128],
                        waits=[nm, pt_readlast[bi]],
                        evac=("V" if i % 2 else "S"))
                    last_ev["V" if i % 2 else "S"] = evt
                    last_tp = ttok
                psm_free[bi] = last_tp
            aevs = []
            for qc in range(NTC):
                qtmax = 4 * qc + 3
                bank = nb()
                mms = []
                for loc0 in (0, J):
                    for jj in range(qtmax + 1):
                        loc = loc0 + jj
                        mms.append((
                            V_f[:, kvh * KT * 128 + loc * 128:
                                kvh * KT * 128 + loc * 128 + 128],
                            P_T[bi][:, loc * T + qc * 512:
                                     loc * T + qc * 512 + 512]))
                mt = mm_group(kb, bank, bank.ap, mms,
                              waits=_toks(last_ev["V"]) + _toks(last_ev["S"])
                              + kv_rd)
                pt_readlast[bi] = mt
                ev = kb.emit("V", lambda e, a=bank.ap, b_=bi, q_=qc:
                             e.tensor_copy(
                                 attn_h[b_][:, q_ * 512: q_ * 512 + 512], a),
                             waits=[mt] + _toks(attn_free[bi]))
                bank.free = ev
                aevs.append(ev)
            apk = kb.dma("S", att_pk[:, :, h * 128:(h + 1) * 128]
                         .rearrange("j p q -> p j q"),
                         attn_h[bi][:].rearrange("p (j q) -> p j q", j=J),
                         waits=aevs, sbuf_dest=False, ch=f"atw{bi}")
            attn_free[bi] = apk
            qTh_free[bi] = pt_readlast[bi]

        sc_stack.close()
        kb.close_regions()

        dn_stack = stack.enter_context(ExitStack())
        wdb = [sb(dn_stack, f"wdb{i}", [128, c.NH * 512], BF16) for i in range(2)]
        attn_t = [sb(dn_stack, f"attn_t{i}", [128, c.NH * 128], BF16)
                  for i in range(2)]
        pst2 = [sb(dn_stack, f"pst2{i}", [128, 512]) for i in range(4)]
        att_done = [kb.tok("dS_atw0"), kb.tok("dS_atw1")]
        wdb_free = [None, None]
        attn_t_free = [None, None]
        pst2_free = [None] * 4
        p2i = 0
        for ds in range(c.DSL):
            wi = ds % 2
            dwd = kb.dma("G", wdb[wi][:], w_denseT[ds],
                         waits=_toks(wdb_free[wi]), ch=f"wdd{wi}")
            mt = None
            for qt in range(J):
                ai = qt % 2
                da = kb.dma("Y", attn_t[ai][:], att_pk[qt],
                            waits=_toks(attn_t_free[ai]) + att_done,
                            ch=f"atr{ai}")
                bank = nb()
                mms = [(attn_t[ai][:, hh * 128: hh * 128 + 128],
                        wdb[wi][:, hh * 512: hh * 512 + 512])
                       for hh in range(c.NH)]
                mt = mm_group(kb, bank, bank.ap, mms, waits=[da, dwd])
                attn_t_free[ai] = mt
                q2 = p2i % 4
                ev = kb.emit("V", lambda e, a=bank.ap, q_=q2: e.tensor_copy(
                    pst2[q_][:], a), waits=[mt] + _toks(pst2_free[q2]))
                bank.free = ev
                pk = kb.dma("S", v3_park[qt][:, ds * 512:(ds + 1) * 512],
                            pst2[q2][:], waits=[ev], sbuf_dest=False,
                            ch=f"v3p{q2}")
                pst2_free[q2] = pk
                p2i += 1
            wdb_free[wi] = mt

        dn_stack.close()
        kb.close_regions()

        f_outer = stack.enter_context(ExitStack())
        x3T = sb(f_outer, "x3T", [128, DC * T], BF16)
        rwinb = sb(f_outer, "rwinb", [128, T])
        qfm = sb(f_outer, "qfm", [128, DC])
        yT = sb(f_outer, "yT", [128, T], BF16)
        BsT = sb(f_outer, "BsT", [128, D], BF16)
        ln3_stack = stack.enter_context(ExitStack())
        v3rd = sb(ln3_stack, "v3rd", [128, D])
        x2rd = sb(ln3_stack, "x2rd", [128, D])
        x3t = sb(ln3_stack, "x3t", [128, D])
        x3b = sb(ln3_stack, "x3b", [128, D], BF16)
        sqd3 = sb(ln3_stack, "sqd3", [128, D])
        v3_done = [kb.tok(f"dS_v3p{i}") for i in range(4)]
        x2pk_done = kb.tok("dS_x2p")
        v3rd_free = None
        x2rd_free = None
        x3t_free = None
        x3b_free = None
        x3_done = [None] * J
        x3T_ev = None
        for t in range(J):
            d1 = kb.dma("Y", v3rd[:], v3_park[t],
                        waits=_toks(v3rd_free) + v3_done, ch="v3r")
            d2 = kb.dma("G", x2rd[:], x2_park[t],
                        waits=_toks(x2rd_free) + [x2pk_done], ch="x2r")
            vv = kb.emit("V", lambda e: e.tensor_add(v3rd[:], v3rd[:], x2rd[:]),
                         waits=[d1, d2])
            x2rd_free = vv
            apt, asr = ln_apply(t, v3rd[:], x3t[:], sqd3[:],
                                [vv] + _toks(x3t_free))
            v3rd_free = [apt, asr]
            x3_done[t] = kb.dma("S", x3_park[t], x3t[:], waits=[apt],
                                sbuf_dest=False, ch="x3p")
            cb = kb.emit("V", lambda e: e.tensor_copy(x3b[:], x3t[:]),
                         waits=[apt] + _toks(x3b_free))
            x3t_free = [cb, x3_done[t]]
            ttok = None
            for dc in range(DC):
                ttok, x3T_ev = tp.run(
                    x3b[:, dc * 128: dc * 128 + 128],
                    x3T[:, dc * T + t * 128: dc * T + t * 128 + 128],
                    waits=[cb], evac="S")
            x3b_free = ttok

        if dbg == "x3":
            kb.dma("Y", dbgs, stat[:, 0:32 * J], waits=[kb.tok("V"), kb.tok("S")],
                   sbuf_dest=False, ch="out")
            for qt in range(J):
                kb.dma("Y", outv[qt], x3_park[qt], waits=[x3_done[qt]],
                       sbuf_dest=False, ch="out")
            kb.finish()
            with nc.Block() as block:
                kb.install(block)
            return nc

        ln3_stack.close()
        kb.close_regions()

        rt_stack = stack.enter_context(ExitStack())
        qtmp = sb(rt_stack, "qtmp", [128, T])
        rwt_s = sb(rt_stack, "rwt_s", [128, DC * 8])
        lg = sb(rt_stack, "lg", [1, 32])
        rbs = sb(rt_stack, "rbs", [1, 8])
        rtc = sb(rt_stack, "rtc", [8, 1])
        rsc_col = sb(rt_stack, "rsc_col", [128, 1])
        bt_s = sb(rt_stack, "bt_s", [128, D])
        atb = sb(rt_stack, "atb", [128, DC * 128], BF16)
        rb_ev = None
        for tc in range(NTC):
            bank = nb()
            bmt = kb.emit("T", lambda e, a=bank.ap, t=tc: e.matmul(
                a, oner_s[:], rwr_s[0:1, t * 512: t * 512 + 512],
                start=True, stop=True),
                waits=_toks(bank.free) + [ld_oner, ld_rwr])
            rb_ev = kb.emit("V", lambda e, a=bank.ap, t=tc: e.tensor_copy(
                rwinb[:, t * 512: t * 512 + 512], a), waits=[bmt])
            bank.free = rb_ev
        qtmp_free = None
        qs_last = None
        for dc in range(DC):
            v1 = kb.emit("V", lambda e, d_=dc: e.tensor_tensor(
                qtmp[:], x3T[:, d_ * T: (d_ + 1) * T], rwinb[:], AOT.mult),
                waits=[x3T_ev, rb_ev, qtmp_free])
            qs_last = kb.emit("V", lambda e, d_=dc: e.reduce_sum(
                qfm[:, d_:d_ + 1], qtmp[:], AXX), waits=[v1])
            qtmp_free = qs_last
        dq1 = kb.dma("G", q_src.rearrange("(p dc) -> p dc", p=128), qfm[:],
                     waits=[qs_last], sbuf_dest=False, ch="qsr")
        ccq = kb.emit("G", lambda e: e.collective_compute(
            "AllReduce", AOT.add,
            replica_groups=[[0, 1], [2, 3], [4, 5], [6, 7]],
            ins=[q_src], outs=[q_dst]), waits=[dq1])
        dq2 = kb.dma("G", qfm[:], q_dst.rearrange("(p dc) -> p dc", p=128),
                     waits=[ccq], ch="qsr")
        ld_rwt = kb.dma("Y", rwt_s[:], rW_T, ch="rwl")
        bank = nb()
        lg_mms = [(qfm[:, dc:dc + 1], rwt_s[:, dc * 8: dc * 8 + 8])
                  for dc in range(DC)]
        lmt = mm_group(kb, bank, bank.ap[0:1, 0:8], lg_mms,
                       waits=[dq2, ld_rwt])
        ld_rb = kb.dma("Y", rbs[:], rb_row, ch="rbl")
        l0 = kb.emit("V", lambda e, a=bank.ap: e.tensor_copy(
            lg[0:1, 0:8], a[0:1, 0:8]), waits=[lmt])
        bank.free = l0
        l1 = kb.emit("V", lambda e: e.tensor_scalar(
            lg[0:1, 0:8], lg[0:1, 0:8], 1.0 / c.WCNT, None, AOT.mult),
            waits=[l0])
        l2 = kb.emit("V", lambda e: e.tensor_tensor(
            lg[0:1, 0:8], lg[0:1, 0:8], rbs[0:1, :], AOT.add),
            waits=[l1, ld_rb])
        l3 = kb.emit("V", lambda e: e.tensor_reduce(
            lg[0:1, 8:9], lg[0:1, 0:8], AXX, AOT.max), waits=[l2])
        l4 = kb.emit("V", lambda e: e.tensor_scalar(
            lg[0:1, 9:10], lg[0:1, 8:9], -1.0, None, AOT.mult), waits=[l3])
        l5 = kb.emit("S", lambda e: e.activation(
            lg[0:1, 16:24], lg[0:1, 0:8], AFT.Exp, bias=lg[0:1, 9:10],
            accum_out=lg[0:1, 10:11]), waits=[l4])
        l6 = kb.emit("V", lambda e: e.reciprocal(
            lg[0:1, 11:12], lg[0:1, 10:11]), waits=[l5])
        l7 = kb.emit("V", lambda e: e.tensor_scalar(
            lg[0:1, 24:32], lg[0:1, 16:24], lg[0:1, 11:12], None, AOT.mult),
            waits=[l6])
        d3 = kb.dma("S", rt_d.rearrange("(o e) -> o e", o=1), lg[0:1, 24:32],
                    waits=[l7], sbuf_dest=False, ch="rtd")
        d4 = kb.dma("Y", rtc[:], rt_d.rearrange("(e o) -> e o", o=1),
                    waits=[d3], ch="rtd2")
        bank = nb()
        rmt = kb.emit("T", lambda e, a=bank.ap: e.matmul(
            a[:, 0:1], repm_s[:], rtc[:], start=True, stop=True),
            waits=_toks(bank.free) + [d4, ld_repm])
        rsc_ev = kb.emit("V", lambda e, a=bank.ap: e.tensor_copy(
            rsc_col[:], a[:, 0:1]), waits=[rmt])
        bank.free = rsc_ev
        ld_bt = kb.dma("Y", bt_s[:], B_T, ch="btl")
        bse = kb.emit("V", lambda e: e.tensor_scalar(
            BsT[:], bt_s[:], rsc_col[:, 0:1], None, AOT.mult),
            waits=[ld_bt, rsc_ev])
        da_ = kb.dma("G", atb[:], A_T, ch="atl")
        y_ev = None
        for tc in range(NTC):
            bank = nb()
            mms = [(atb[:, dc * 128: dc * 128 + 128],
                    x3T[:, dc * T + tc * 512: dc * T + tc * 512 + 512])
                   for dc in range(DC)]
            mt = mm_group(kb, bank, bank.ap, mms, waits=[da_, x3T_ev])
            y_ev = kb.emit("V", lambda e, a=bank.ap, t=tc: e.tensor_copy(
                yT[:, t * 512: t * 512 + 512], a), waits=[mt, y_ev])
            bank.free = y_ev

        rt_stack.close()
        kb.close_regions()

        fb_stack = stack.enter_context(ExitStack())
        wfb = [sb(fb_stack, f"wfb{i}", [128, DC * 512], BF16) for i in range(2)]
        fst = [sb(fb_stack, f"fst{i}", [128, 512]) for i in range(4)]
        wfb_free = [None, None]
        fst_free = [None] * 4
        fi = 0
        for ds in range(c.DSL):
            wi = ds % 2
            dwf = kb.dma("G", wfb[wi][:], WT[ds],
                         waits=_toks(wfb_free[wi]), ch=f"wff{wi}")
            mt = None
            for t in range(J):
                bank = nb()
                mms = [(x3T[:, dc * T + t * 128: dc * T + t * 128 + 128],
                        wfb[wi][:, dc * 512:(dc + 1) * 512])
                       for dc in range(DC)]
                mms.append((yT[:, t * 128: t * 128 + 128],
                            BsT[:, ds * 512:(ds + 1) * 512]))
                mt = mm_group(kb, bank, bank.ap, mms,
                              waits=[dwf, x3T_ev, y_ev, bse])
                q3_ = fi % 4
                ev = kb.emit("V", lambda e, a=bank.ap, q_=q3_: e.tensor_copy(
                    fst[q_][:], a), waits=[mt] + _toks(fst_free[q3_]))
                bank.free = ev
                pk = kb.dma("S", b4_park[t][:, ds * 512:(ds + 1) * 512],
                            fst[q3_][:], waits=[ev], sbuf_dest=False,
                            ch=f"b4p{q3_}")
                fst_free[q3_] = pk
                fi += 1
            wfb_free[wi] = mt

        fb_stack.close()
        f_outer.close()
        kb.close_regions()

        ln4_stack = stack.enter_context(ExitStack())
        bb = sb(ln4_stack, "bb", [128, D])
        brs = sb(ln4_stack, "brs", [1, D])
        b4rd = [sb(ln4_stack, f"b4rd{i}", [128, D]) for i in range(2)]
        x3rd = [sb(ln4_stack, f"x3rd{i}", [128, D]) for i in range(2)]
        outt = [sb(ln4_stack, f"outt{i}", [128, D]) for i in range(2)]
        sqd4 = sb(ln4_stack, "sqd4", [128, D])
        ld_br = kb.dma("Y", brs[:], b_row, ch="brl")
        bb_ev = None
        for ds2 in range(c.DSL):
            bank = nb()
            bmt = kb.emit("T", lambda e, a=bank.ap, d_=ds2: e.matmul(
                a, oner_s[:], brs[0:1, d_ * 512: d_ * 512 + 512],
                start=True, stop=True),
                waits=_toks(bank.free) + [ld_br, ld_oner])
            bb_ev = kb.emit("V", lambda e, a=bank.ap, d_=ds2: e.tensor_copy(
                bb[:, d_ * 512: d_ * 512 + 512], a), waits=[bmt, bb_ev])
            bank.free = bb_ev
        b4_done = [kb.tok(f"dS_b4p{i}") for i in range(4)]
        x3pk_done = kb.tok("dS_x3p")
        b4rd_free = [None, None]
        x3rd_free = [None, None]
        outt_free = [None, None]
        for t in range(J):
            bi = t % 2
            d1 = kb.dma("Y", b4rd[bi][:], b4_park[t],
                        waits=_toks(b4rd_free[bi]) + b4_done, ch=f"b4r{bi}")
            d2 = kb.dma("G", x3rd[bi][:], x3_park[t],
                        waits=_toks(x3rd_free[bi]) + [x3pk_done], ch=f"x3r{bi}")
            v1 = kb.emit("V", lambda e, b=bi: e.tensor_add(
                b4rd[b][:], b4rd[b][:], bb[:]), waits=[d1, bb_ev])
            v2 = kb.emit("V", lambda e, b=bi: e.tensor_add(
                b4rd[b][:], b4rd[b][:], x3rd[b][:]), waits=[v1, d2])
            x3rd_free[bi] = v2
            apt, asr = ln_apply(t, b4rd[bi][:], outt[bi][:], sqd4[:],
                                [v2] + _toks(outt_free[bi]))
            b4rd_free[bi] = [apt, asr]
            od = kb.dma("Y", outv[t], outt[bi][:], waits=[apt],
                        sbuf_dest=False, ch=f"out{bi}")
            outt_free[bi] = od

        kb.finish()
        with nc.Block() as block:
            kb.install(block)
    return nc


def kernel(**inputs):
    cfg = FULL
    in_maps, metas = _host_prepare(cfg, **inputs)
    nc = build_nc(cfg)
    res = run_bass_kernel_spmd(nc, in_maps, core_ids=list(range(8)))
    x = np.asarray(inputs["L0_attn_x"])
    outp = np.zeros_like(x)
    for r, m in zip(res.results, metas):
        outp[m["b"]][m["own"]] = r["out"]
    return outp


# version 08 (final)
# speedup vs baseline: 1.1627x; 1.0056x over previous
from contextlib import ExitStack

import numpy as np
import ml_dtypes

import concourse.bass as bass
import concourse.mybir as mybir
from concourse.bass_utils import run_bass_kernel_spmd

F32 = mybir.dt.float32
BF16 = mybir.dt.bfloat16
F16 = mybir.dt.float16
AOT = mybir.AluOpType
AFT = mybir.ActivationFunctionType
AXX = mybir.AxisListType.X

MAGIC = 12582912.0
NEG = -1.0e30
LN_EPS = 1e-5
Q_START, Q_END = 611, 2047


class Cfg:
    def __init__(self, D=4096, F=16384, S=2048, NH=32, KVH=8):
        self.D, self.F, self.S, self.NH, self.KVH = D, F, S, NH, KVH
        self.G = NH // KVH
        self.DC = D // 128
        self.FC = F // 128
        self.J = S // 256
        self.T = 128 * self.J
        self.DSL = D // 512
        self.UPSL = F // 512
        self.FT = F // 128
        self.NFB = F // 2048
        self.WCNT = S - 1 - 611


FULL = Cfg()
TINY = Cfg(D=512, F=2048, S=1024, NH=8, KVH=2)



def _host_prepare(cfg, L0_attn_x, L0_attn_gamma, L0_ffn_w_up, L0_ffn_w_down,
                  L1_attn_w_qkv, L1_attn_w_dense, L1_ffn_W, L1_ffn_b, L1_ffn_A,
                  L1_ffn_B, L1_ffn_router_W, L1_ffn_router_b):
    c = cfg
    f32 = np.float32

    def t(a):
        return np.ascontiguousarray(np.asarray(a).T)

    FTB = 16
    w_upT = np.ascontiguousarray(
        np.asarray(L0_ffn_w_up, f32).reshape(c.FT, 128, c.DC, 128)
        .transpose(0, 3, 2, 1).reshape(c.FT, 128, c.D))
    w_downT = np.ascontiguousarray(
        np.asarray(L0_ffn_w_down, f32).reshape(c.DSL, 512, c.NFB, FTB, 128)
        .transpose(2, 0, 4, 3, 1).reshape(c.NFB, c.DSL, 128, FTB * 512))
    w_qkvT = t(L1_attn_w_qkv)
    wq = np.ascontiguousarray(
        w_qkvT[:, : c.NH * 128].reshape(c.DC, 128, c.NH, 128)
        .transpose(2, 1, 0, 3).reshape(c.NH, 128, c.D))
    KVW = c.KVH * 128
    SW = min(512, KVW)
    NS = KVW // SW
    wk = np.ascontiguousarray(
        w_qkvT[:, c.NH * 128: (c.NH + c.KVH) * 128].reshape(c.DC, 128, NS, SW)
        .transpose(2, 1, 0, 3).reshape(NS, 128, c.DC * SW))
    wv = np.ascontiguousarray(
        w_qkvT[:, (c.NH + c.KVH) * 128:].reshape(c.DC, 128, NS, SW)
        .transpose(2, 1, 0, 3).reshape(NS, 128, c.DC * SW))
    w_denseT = np.ascontiguousarray(
        t(L1_attn_w_dense).reshape(c.NH, 128, c.DSL, 512)
        .transpose(2, 1, 0, 3).reshape(c.DSL, 128, c.NH * 512))
    WT = np.ascontiguousarray(
        t(L1_ffn_W).reshape(c.DC, 128, c.DSL, 512)
        .transpose(2, 1, 0, 3).reshape(c.DSL, 128, c.DC * 512))
    b_row = np.asarray(L1_ffn_b, f32).reshape(1, c.D)
    ER = np.asarray(L1_ffn_A).shape[0] * np.asarray(L1_ffn_A).shape[1]
    A_T = np.ascontiguousarray(
        t(np.asarray(L1_ffn_A).reshape(ER, c.D)).reshape(c.DC, 128, ER)
        .transpose(1, 0, 2).reshape(128, c.DC * ER))
    B_T = np.ascontiguousarray(
        np.transpose(np.asarray(L1_ffn_B), (0, 2, 1)).reshape(ER, c.D))
    rW_T = np.ascontiguousarray(
        t(L1_ffn_router_W).reshape(c.DC, 128, -1)
        .transpose(1, 0, 2).reshape(128, -1))
    rb_row = np.asarray(L1_ffn_router_b, f32).reshape(1, -1)
    gamma_b = np.ascontiguousarray(
        np.broadcast_to(np.asarray(L0_attn_gamma).reshape(1, 1), (128, 1)).astype(f32))
    eye = np.eye(128, dtype=ml_dtypes.bfloat16)
    eye32 = np.eye(128, dtype=f32)
    ones_col = np.ones((128, 1), f32)
    ones_row = np.ones((1, 128), f32)
    consts_v = np.ascontiguousarray(np.broadcast_to(
        np.array([[LN_EPS, MAGIC, 0.0625]], f32), (128, 3)))
    tri = np.where(np.tril(np.ones((128, 128), bool)), 0.0, NEG).astype(f32)
    negf = np.full((128, 128), NEG, f32)
    zf = np.zeros((128, 128), f32)
    repm = np.zeros((8, 128), f32)
    for e in range(8):
        repm[e, e * 16:(e + 1) * 16] = 2.0

    x = np.asarray(L0_attn_x)
    in_maps, metas = [], []
    for core in range(8):
        b, p = core // 2, core % 2
        own = np.concatenate([np.arange(128 * (2 * j + p), 128 * (2 * j + p) + 128)
                              for j in range(c.J)])
        par = np.concatenate([np.arange(128 * (2 * j + 1 - p), 128 * (2 * j + 1 - p) + 128)
                              for j in range(c.J)])
        xbc = np.ascontiguousarray(x[b][np.concatenate([own, par])])
        mask0 = tri if p == 0 else zf
        mask1 = negf if p == 0 else tri
        rwin_row = np.zeros((1, c.T), f32)
        for j in range(c.J):
            g = 128 * (2 * j + p) + np.arange(128)
            rwin_row[0, j * 128:(j + 1) * 128] = (
                (g >= Q_START) & (g < c.S - 1)).astype(f32)
        nsh = c.D * c.UPSL // 8
        in_maps.append(dict(
            xb=xbc, gamma_b=gamma_b, eye=eye, eye32=eye32, ones_col=ones_col,
            ones_row=ones_row, consts_v=consts_v,
            mask0=mask0, mask1=mask1, rwin_row=rwin_row, repm=repm,
            w_upT=w_upT, w_downT=w_downT, wq=wq, wk=wk, wv=wv, w_denseT=w_denseT,
            WT=WT, b_row=b_row, A_T=A_T, B_T=B_T, rW_T=rW_T, rb_row=rb_row,
            wup_shard=np.ascontiguousarray(w_upT.reshape(-1, 512)[core * nsh:(core + 1) * nsh]),
            wdn_shard=np.ascontiguousarray(
                w_downT.reshape(-1, c.D)[core * (c.F // 8):(core + 1) * (c.F // 8)]),
        ))
        metas.append(dict(b=b, own=own))
    return in_maps, metas



class KB:
    ENGS = ("T", "V", "S", "G", "Y")

    def __init__(self, nc, stack):
        self.nc = nc
        self.stack = stack
        self.ops = {k: [] for k in self.ENGS}
        self.cnt = {}
        self.sems = {}
        self._waited = {k: {} for k in self.ENGS}
        self.region_toks = []

    def _sem(self, name):
        if name not in self.sems:
            self.sems[name] = self.stack.enter_context(self.nc.semaphore(name))
            self.cnt[name] = 0
        return self.sems[name]

    def wait(self, eng, tok):
        if tok is None:
            return
        name, val = tok
        if val <= 0:
            return
        sem = self._sem(name)
        assert val <= self.cnt[name], (eng, name, val, self.cnt[name])
        if self._waited[eng].get(name, 0) >= val:
            return
        self._waited[eng][name] = val
        self.ops[eng].append(lambda e, s=sem, v=val: e.wait_ge(s, v))

    def emit(self, eng, fn, waits=(), inc=True):
        for tk in waits:
            self.wait(eng, tk)
        if inc:
            sem = self._sem(eng)
            self.cnt[eng] += 1
            tok = (eng, self.cnt[eng])
            self.ops[eng].append(lambda e, f=fn, s=sem: f(e).then_inc(s, 1))
            return tok
        self.ops[eng].append(fn)
        return None

    def dma(self, q, out, in_, waits=(), sbuf_dest=True, ch="g", **kw):
        assert q in ("Y", "S", "G")
        w = list(waits) + (self.region_toks if sbuf_dest else [])
        for tk in w:
            self.wait(q, tk)
        name = "d" + q + "_" + ch
        sem = self._sem(name)
        self.cnt[name] += 16
        tok = (name, self.cnt[name])
        self.ops[q].append(lambda e, o=out, i=in_, k=dict(kw), s=sem:
                           e.dma_start(out=o, in_=i, **k).then_inc(s, 16))
        return tok

    def tok(self, name):
        v = self.cnt.get(name, 0)
        return (name, v) if v else None

    def close_regions(self):
        toks = [(k, self.cnt.get(k, 0)) for k in ("T", "V", "S", "G")]
        toks = [t for t in toks if t[1] > 0]
        dtoks = [(k, v) for k, v in self.cnt.items()
                 if k.startswith("d") and v > 0]
        for k in ("T", "V", "S", "G"):
            for t in toks:
                if t[0] != k:
                    self.wait(k, t)
            for t in dtoks:
                self.wait(k, t)
        self.region_toks = toks + dtoks

    def finish(self):
        for q, v in sorted(self.cnt.items()):
            if q.startswith("d") and v:
                self.wait("Y", (q, v))

    def install(self, block):
        attr = {"T": "tensor", "V": "vector", "S": "scalar", "G": "gpsimd", "Y": "sync"}
        for k, a in attr.items():
            if not self.ops[k]:
                continue
            def body(e, ops=self.ops[k]):
                for f in ops:
                    f(e)
            getattr(block, a)(body)


class Bank:
    def __init__(self, ap):
        self.ap = ap
        self.free = None


def _toks(x):
    if x is None:
        return []
    if isinstance(x, list):
        return [t for t in x if t is not None]
    return [x]


def mm_group(kb, bank, out_ap, mms, waits=()):
    n = len(mms)
    first = _toks(bank.free) + list(waits)
    tok = None
    for i, (l, r) in enumerate(mms):
        tok = kb.emit(
            "T",
            lambda e, o=out_ap, l=l, r=r, st=(i == 0), sp=(i == n - 1):
                e.matmul(o, l, r, start=st, stop=sp),
            waits=first if i == 0 else (), inc=(i == n - 1))
    return tok


class TP:

    def __init__(self, kb, banks, eye_ap):
        self.kb = kb
        self.banks = banks
        self.eye = eye_ap
        self.i = 0

    def run(self, src_ap, dst_ap, waits=(), evac="V"):
        bank = self.banks[self.i % len(self.banks)]
        self.i += 1
        t = self.kb.emit("T", lambda e, o=bank.ap, s=src_ap: e.transpose(o, s, self.eye),
                         waits=_toks(bank.free) + list(waits))
        if evac == "V":
            fn = lambda e, d=dst_ap, s=bank.ap: e.tensor_copy(d, s)
        else:
            fn = lambda e, d=dst_ap, s=bank.ap: e.copy(d, s)
        ev = self.kb.emit(evac, fn, waits=[t])
        bank.free = ev
        return t, ev



def build_nc(cfg, dbg=None):
    c = cfg
    J, DC, FC, D, F, T = c.J, c.DC, c.FC, c.D, c.F, c.T
    S2 = 2 * T
    KT = 2 * J
    nc = bass.Bass("TRN2", target_bir_lowering=False, debug=False)

    dt = lambda n, sh, ty=F32, kind="ExternalInput": nc.dram_tensor(n, sh, ty, kind=kind).ap()
    di = lambda n, sh, ty=F32: nc.dram_tensor(n, sh, ty).ap()

    xb = dt("xb", [S2, D])
    gamma_b = dt("gamma_b", [128, 1])
    eye = dt("eye", [128, 128], BF16)
    ones_col = dt("ones_col", [128, 1])
    ones_row = dt("ones_row", [1, 128])
    consts_v = dt("consts_v", [128, 3])
    mask0 = dt("mask0", [128, 128])
    mask1 = dt("mask1", [128, 128])
    rwin_row = dt("rwin_row", [1, T])
    eye32 = dt("eye32", [128, 128])
    repm = dt("repm", [8, 128])
    FTB0 = 16
    KVW0 = c.KVH * 128
    SW0 = min(512, KVW0)
    NS0 = KVW0 // SW0
    w_upT = dt("w_upT", [c.FT, 128, D])
    w_downT = dt("w_downT", [c.NFB, c.DSL, 128, FTB0 * 512])
    wq = dt("wq", [c.NH, 128, D])
    wk = dt("wk", [NS0, 128, DC * SW0])
    wv = dt("wv", [NS0, 128, DC * SW0])
    w_denseT = dt("w_denseT", [c.DSL, 128, c.NH * 512])
    WT = dt("WT", [c.DSL, 128, DC * 512])
    b_row = dt("b_row", [1, D])
    A_T = dt("A_T", [128, DC * 128])
    B_T = dt("B_T", [128, D])
    rW_T = dt("rW_T", [128, DC * 8])
    rb_row = dt("rb_row", [1, 8])
    nsh = c.D * c.UPSL // 8
    wup_shard = dt("wup_shard", [nsh, 512])
    wdn_shard = dt("wdn_shard", [F // 8, D])
    out = dt("out", [T, D], F32, "ExternalOutput")
    outv = out.rearrange("(t p) d -> t p d", p=128)

    dbgs = dt("dbgs", [128, 32 * J], F32, "ExternalOutput") if dbg else None
    dbge = dt("dbge", [128, 4096], F32, "ExternalOutput") if dbg else None
    v_park = di("v_park", [J, 128, D])
    x1_park = di("x1_park", [J, 128, D])
    sc_src = di("sc_src", [1, 16])
    sc_dst = di("sc_dst", [1, 16])
    rh_pk = di("rh_pk", [c.FT, 128, T], F16)
    pdn = di("pdn", [c.NFB, J, 128, D])
    x2_park = di("x2_park", [J, 128, D])
    v3_park = di("v3_park", [J, 128, D])
    x3_park = di("x3_park", [J, 128, D])
    b4_park = di("b4_park", [J, 128, D])
    qT_pk = di("qT_pk", [c.NH, 128, T], BF16)
    att_pk = di("att_pk", [J, 128, c.NH * 128], BF16)
    kv_srcK = di("kv_srcK", [128, c.KVH * T], BF16)
    kv_srcV = di("kv_srcV", [128, c.KVH * T], BF16)
    kv_dstK = di("kv_dstK", [256, c.KVH * T], BF16)
    kv_dstV = di("kv_dstV", [256, c.KVH * T], BF16)
    q_src = di("q_src", [D])
    q_dst = di("q_dst", [D])
    rt_d = di("rt_d", [8])
    s2_d = di("s2_d", [T])

    xbv = xb.rearrange("(n p) d -> n p d", p=128)

    stack = ExitStack()
    kb = KB(nc, stack)

    def sb(st_, n, sh, ty=F32, side="left"):
        return st_.enter_context(nc.sbuf_tensor(n, sh, ty, side=side))

    def psb(n, sh, ty=F32):
        return Bank(stack.enter_context(nc.psum_tensor(n, sh, ty))[:])

    with stack:
        eye_s = sb(stack, "eye_s", [128, 128], BF16, side="right")
        mk0_s = sb(stack, "mk0_s", [128, 128], side="right")
        mk1_s = sb(stack, "mk1_s", [128, 128], side="right")
        gam_s = sb(stack, "gam_s", [128, 1], side="right")
        onec_s = sb(stack, "onec_s", [128, 1], side="right")
        oner_s = sb(stack, "oner_s", [1, 128], side="right")
        cv_s = sb(stack, "cv_s", [128, 3], side="right")
        eye32_s = sb(stack, "eye32_s", [128, 128], side="right")
        repm_s = sb(stack, "repm_s", [8, 128], side="right")
        rwr_s = sb(stack, "rwr_s", [1, T], side="right")
        stat = sb(stack, "stat", [128, 32 * J], side="right")
        sqp = sb(stack, "sqp", [128, J * c.DSL], side="right")
        sc_s = sb(stack, "sc_s", [1, 16], side="right")
        scb_s = sb(stack, "scb_s", [128, 8], side="right")

        ld_eye = kb.dma("Y", eye_s[:], eye, ch="c0")
        ld_mk0 = kb.dma("Y", mk0_s[:], mask0, ch="c1")
        ld_mk1 = kb.dma("Y", mk1_s[:], mask1, ch="c2")
        ld_gam = kb.dma("Y", gam_s[:], gamma_b, ch="c3")
        ld_one = kb.dma("Y", onec_s[:], ones_col, ch="c4")
        ld_oner = kb.dma("Y", oner_s[:], ones_row, ch="c7")
        ld_cv = kb.dma("Y", cv_s[:], consts_v, ch="c5")
        ld_eye32 = kb.dma("Y", eye32_s[:], eye32, ch="c6")
        ld_repm = kb.dma("Y", repm_s[:], repm, ch="c8")
        ld_rwr = kb.dma("Y", rwr_s[:], rwin_row, ch="c9")
        z_stat = kb.emit("V", lambda e: e.memset(stat[:], 0.0))
        z_sqp = kb.emit("V", lambda e: e.memset(sqp[:], 0.0))

        C_MIN, C_ESUM, C_PVS = 0, 1, 2
        C_SUM, C_MU, C_VAR, C_RST, C_NMU = 3, 4, 5, 6, 7
        C_AM1, C_S1, C_IS1 = 8, 9, 10
        C_AM2, C_S2, C_C2 = 11, 12, 13
        C_T0, C_T1, C_T2, C_T3 = 14, 15, 16, 17
        C_C1, C_HMX, C_S2P, C_R2M = 18, 19, 20, 21

        def st(j, col, n=1):
            return stat[:, 32 * j + col:32 * j + col + n]

        mmb = [psb(f"mm{i}", [128, 512]) for i in range(6)]
        tpb = [psb("tp0", [128, 1024], BF16), psb("tp1", [128, 1024], BF16)]
        tp = TP(kb, [Bank(b.ap[:, 0:128]) for b in tpb], eye_s[:])

        w1_stack = stack.enter_context(ExitStack())
        accw = sb(w1_stack, "accw", [128, 2])
        shard_b = [sb(w1_stack, f"shard_b{i}", [128, 4096], BF16) for i in range(2)]
        z0 = kb.emit("V", lambda e: e.memset(accw[:], 0.0))
        upv = wup_shard.rearrange("(n p) k -> n p k", p=128)
        dnv = wdn_shard.rearrange("(n p) k -> n p k", p=128)
        jobs = []
        k_up = 4096 // 512
        n_up_tiles = nsh // 128
        for i in range(0, n_up_tiles, k_up):
            n = min(k_up, n_up_tiles - i)
            jobs.append((0, upv[i:i + n], n, 512))
        k_dn = max(1, 4096 // D)
        n_dn_tiles = F // 8 // 128
        for i in range(0, n_dn_tiles, k_dn):
            n = min(k_dn, n_dn_tiles - i)
            jobs.append((1, dnv[i:i + n], n, D))
        sh_free = [z0, z0]
        vtok = z0
        for ji, (which, ap, n, kw_) in enumerate(jobs):
            bi = ji % 2
            width = n * kw_
            dsb = shard_b[bi][:, 0:width]
            dtok = kb.dma("G", dsb.rearrange("p (n k) -> p n k", n=n),
                          ap.rearrange("n p k -> p n k"),
                          waits=[sh_free[bi]], ch=f"sh{bi}")
            rtok = kb.emit("V", lambda e, d=dsb: e.reduce_sum(
                st(0, C_T3), d, AXX, apply_absolute_value=True), waits=[dtok, vtok])
            vtok = kb.emit("V", lambda e, w=which: e.tensor_add(
                accw[:, w:w + 1], accw[:, w:w + 1], st(0, C_T3)), waits=[rtok])
            sh_free[bi] = vtok
        tsc = kb.emit("T", lambda e: e.matmul(mmb[4].ap[0:1, 0:2], onec_s[:, 0:1],
                                              accw[:], start=True, stop=True),
                      waits=[vtok, ld_one, mmb[4].free])
        zs = kb.emit("V", lambda e: e.memset(sc_s[:], 0.0))
        cpy = kb.emit("V", lambda e: e.tensor_copy(sc_s[:, 0:2], mmb[4].ap[0:1, 0:2]),
                      waits=[tsc, zs])
        mmb[4].free = cpy
        dcc = kb.dma("G", sc_src, sc_s[:], waits=[cpy], sbuf_dest=False, ch="sc")
        cc1 = kb.emit("G", lambda e: e.collective_compute(
            "AllReduce", AOT.add, replica_groups=[[0, 1, 2, 3, 4, 5, 6, 7]],
            ins=[sc_src], outs=[sc_dst]), waits=[dcc])
        dcb = kb.dma("G", sc_s[:], sc_dst, waits=[cc1], ch="sc")
        t1 = kb.emit("V", lambda e: e.tensor_scalar(sc_s[:, 2:3], sc_s[:, 0:1],
                                                    1.0 / (D * F), LN_EPS, AOT.mult, AOT.add),
                     waits=[dcb, cpy])
        t2 = kb.emit("V", lambda e: e.tensor_scalar(sc_s[:, 4:5], sc_s[:, 1:2],
                                                    1.0 / (D * F), LN_EPS, AOT.mult, AOT.add),
                     waits=[dcb])
        t3 = kb.emit("V", lambda e: e.reciprocal(sc_s[:, 3:4], sc_s[:, 2:3]), waits=[t1])
        sc_done = kb.emit("V", lambda e: e.reciprocal(sc_s[:, 5:6], sc_s[:, 4:5]),
                          waits=[t2, t3])
        tb2 = kb.emit("T", lambda e: e.matmul(mmb[5].ap[:, 0:8], oner_s[0:1, :],
                                              sc_s[0:1, 0:8], start=True, stop=True),
                      waits=[sc_done, ld_oner, mmb[5].free])
        scb_done = kb.emit("V", lambda e: e.tensor_copy(scb_s[:], mmb[5].ap[:, 0:8]),
                           waits=[tb2])
        mmb[5].free = scb_done
        w1_stack.close()
        kb.close_regions()

        pt_stack = stack.enter_context(ExitStack())
        ab_stack = stack.enter_context(ExitStack())
        JH = max(1, J // 2)
        xq_fm = sb(ab_stack, "xq_fm", [128, DC * T], BF16)
        pan_tm = sb(ab_stack, "pan_tm", [128, 2 * D], BF16)
        pan_fm = sb(ab_stack, "pan_fm", [128, DC * 512], BF16)
        e_buf = sb(ab_stack, "e_buf", [128, JH * S2])
        p_buf = sb(ab_stack, "p_buf", [128, S2], BF16)
        p_T = sb(pt_stack, "p_T", [128, KT * T], BF16, side="right")

        NPAN = S2 // 512
        OWN_PAN = T // 512
        pan_rd_free = [None]
        xq_built = [False]
        pvs_tok = [None] * J
        pT_last = [None]
        min_tok = [None] * J

        def load_panel(pnl, into_own):
            ready = None
            for h in range(2):
                dtok = kb.dma("G", pan_tm[:].rearrange("p (t d) -> p t d", t=2),
                              xbv[4 * pnl + 2 * h: 4 * pnl + 2 * h + 2]
                              .rearrange("t p d -> p t d"),
                              waits=[pan_rd_free[0]], ch=f"pan{h}")
                for tt in range(2):
                    for dc in range(DC):
                        src = pan_tm[:, tt * D + dc * 128: tt * D + dc * 128 + 128]
                        if into_own:
                            col = (4 * pnl + 2 * h + tt) * 128
                            dst = xq_fm[:, dc * T + col: dc * T + col + 128]
                        else:
                            dst = pan_fm[:, dc * 512 + (2 * h + tt) * 128:
                                         dc * 512 + (2 * h + tt) * 128 + 128]
                        last_t, ready = tp.run(src, dst, waits=[dtok, ld_eye])
                pan_rd_free[0] = last_t
            return ready

        for half, qts in enumerate([list(range(0, JH)), list(range(JH, J))]):
            if not qts:
                continue
            if not xq_built[0]:
                own_ready = None
                for pnl in range(OWN_PAN):
                    own_ready = load_panel(pnl, True)
                xq_built[0] = own_ready
            for pnl in range(NPAN):
                if pnl < OWN_PAN:
                    rhs = [xq_fm[:, dc * T + pnl * 512: dc * T + pnl * 512 + 512]
                           for dc in range(DC)]
                    pan_ready = xq_built[0]
                else:
                    pan_ready = load_panel(pnl, False)
                    rhs = [pan_fm[:, dc * 512: dc * 512 + 512] for dc in range(DC)]
                for qi, qt in enumerate(qts):
                    bank = mmb[qi % 4]
                    mms = [(xq_fm[:, dc * T + qt * 128: dc * T + qt * 128 + 128], rhs[dc])
                           for dc in range(DC)]
                    mt = mm_group(kb, bank, bank.ap, mms, waits=[pan_ready])
                    dst = e_buf[:, qi * S2 + pnl * 512: qi * S2 + pnl * 512 + 512]
                    ev = kb.emit("S", lambda e, d=dst, s=bank.ap: e.copy(d, s), waits=[mt])
                    bank.free = ev
                    rt = kb.emit("V", lambda e, d=dst, j=qt: e.tensor_reduce(
                        st(j, C_T0), d, AXX, AOT.min), waits=[ev, min_tok[qt]])
                    if min_tok[qt] is None:
                        min_tok[qt] = kb.emit("V", lambda e, j=qt: e.tensor_copy(
                            st(j, C_MIN), st(j, C_T0)), waits=[rt])
                    else:
                        min_tok[qt] = kb.emit("V", lambda e, j=qt: e.tensor_tensor(
                            st(j, C_MIN), st(j, C_MIN), st(j, C_T0), AOT.min),
                            waits=[rt])
            pbuf_free = None
            for qi, qt in enumerate(qts):
                exp_tok = None
                add_tok = None
                for kk in range(S2 // 512):
                    src = e_buf[:, qi * S2 + kk * 512: qi * S2 + kk * 512 + 512]
                    dst = p_buf[:, kk * 512: kk * 512 + 512]
                    acc = st(qt, C_ESUM) if kk == 0 else st(qt, C_T1)
                    exp_tok = kb.emit("S", lambda e, s=src, d=dst, a=acc, j=qt: e.activation(
                        d, s, AFT.Exp, bias=st(j, C_MIN), scale=-1.0, accum_out=a),
                        waits=[min_tok[qt], pbuf_free, add_tok])
                    if kk > 0:
                        add_tok = kb.emit("V", lambda e, j=qt: e.tensor_add(
                            st(j, C_ESUM), st(j, C_ESUM), st(j, C_T1)),
                            waits=[exp_tok] + ([add_tok] if add_tok else []))
                rc = kb.emit("V", lambda e, j=qt: e.reciprocal(st(j, C_PVS), st(j, C_ESUM)),
                             waits=[exp_tok, add_tok])
                pvs_tok[qt] = kb.emit("V", lambda e, j=qt: e.tensor_scalar(
                    st(j, C_PVS), st(j, C_PVS), gam_s[:, 0:1], 0.5, AOT.mult, AOT.mult),
                    waits=[rc, ld_gam])
                ttok = None
                for kt in range(KT):
                    src = p_buf[:, kt * 128: kt * 128 + 128]
                    dst = p_T[:, kt * T + qt * 128: kt * T + qt * 128 + 128]
                    ttok, pT_last[0] = tp.run(src, dst, waits=[exp_tok], evac="S")
                pbuf_free = ttok
        pT_done = pT_last[0]

        if dbg == "E":
            kb.dma("Y", dbgs, stat[:, 0:32 * J], waits=[kb.tok("V"), kb.tok("S")],
                   sbuf_dest=False, ch="out")
            kb.dma("Y", dbge, e_buf[:, 0:4096], waits=[kb.tok("S")], sbuf_dest=False, ch="out")
            kb.finish()
            with nc.Block() as block:
                kb.install(block)
            return nc

        ab_stack.close()
        kb.close_regions()

        c_stack = stack.enter_context(ExitStack())
        xcol = [sb(c_stack, f"xcol{i}", [128, KT * 512], BF16) for i in range(2)]
        xown = [sb(c_stack, f"xown{i}", [128, J * 512]) for i in range(2)]
        vstg = [sb(c_stack, f"vstg{i}", [128, J * 512]) for i in range(2)]
        sq_dump = sb(c_stack, "sq_dump", [128, 512])
        xc_free = [None, None]
        xo_free = [None, None]
        v_free = [None, None]
        sum_tok = [None] * J
        sq_last = [None] * J
        sqd_tok = None
        for ds in range(c.DSL):
            bi = ds % 2
            ctok = kb.dma("G", xcol[bi][:].rearrange("p (t k) -> p t k", t=KT),
                          xbv[:, :, ds * 512:(ds + 1) * 512].rearrange("t p k -> p t k"),
                          waits=[xc_free[bi]], ch=f"xc{bi}")
            otok = kb.dma("Y", xown[bi][:].rearrange("p (t k) -> p t k", t=J),
                          xbv[0:J, :, ds * 512:(ds + 1) * 512].rearrange("t p k -> p t k"),
                          waits=[xo_free[bi]], ch=f"xo{bi}")
            last_mm = None
            for qt in range(J):
                bank = mmb[qt % 6]
                mms = [(p_T[:, kt * T + qt * 128: kt * T + qt * 128 + 128],
                        xcol[bi][:, kt * 512: kt * 512 + 512]) for kt in range(KT)]
                last_mm = mm_group(kb, bank, bank.ap, mms, waits=[ctok, pT_done])
                dst = vstg[bi][:, qt * 512: qt * 512 + 512]
                xo = xown[bi][:, qt * 512: qt * 512 + 512]
                ev = kb.emit("V", lambda e, d=dst, s=bank.ap, x=xo, j=qt: e.scalar_tensor_tensor(
                    d, s, st(j, C_PVS), x, AOT.mult, AOT.add),
                    waits=[last_mm, otok, v_free[bi], pvs_tok[qt]])
                bank.free = ev
                rt = kb.emit("V", lambda e, d=dst, j=qt: e.reduce_sum(st(j, C_T0), d, AXX),
                             waits=[ev] + ([sum_tok[qt]] if sum_tok[qt] else []))
                if ds == 0:
                    sum_tok[qt] = kb.emit("V", lambda e, j=qt: e.tensor_copy(
                        st(j, C_SUM), st(j, C_T0)), waits=[rt])
                else:
                    sum_tok[qt] = kb.emit("V", lambda e, j=qt: e.tensor_add(
                        st(j, C_SUM), st(j, C_SUM), st(j, C_T0)), waits=[rt])
                sq_last[qt] = kb.emit("S", lambda e, d=dst, j=qt, k=ds: e.activation(
                    sq_dump[:], d, AFT.Square,
                    accum_out=sqp[:, j * c.DSL + k: j * c.DSL + k + 1]),
                    waits=[ev] + ([sqd_tok] if sqd_tok else []))
                sqd_tok = sq_last[qt]
            ptok = kb.dma("S", v_park[:, :, ds * 512:(ds + 1) * 512].rearrange("t p k -> p t k"),
                          vstg[bi][:].rearrange("p (t k) -> p t k", t=J),
                          waits=[sq_last[J - 1]], sbuf_dest=False, ch="vpk")
            v_free[bi] = ptok
            xc_free[bi] = last_mm
            xo_free[bi] = sum_tok[J - 1]

        if dbg == "v1":
            kb.dma("Y", dbgs, stat[:, 0:32 * J], waits=[kb.tok("V"), kb.tok("S")],
                   sbuf_dest=False, ch="out")
            for qt in range(J):
                kb.dma("Y", outv[qt], v_park[qt], waits=[kb.tok("dS_vpk")],
                       sbuf_dest=False, ch="out")
            kb.finish()
            with nc.Block() as block:
                kb.install(block)
            return nc

        c_stack.close()
        pt_stack.close()
        kb.close_regions()

        d_outer = stack.enter_context(ExitStack())
        M_up = sb(d_outer, "M_up", [128, T])
        Vs2 = sb(d_outer, "Vs2", [128, T])
        s2c = sb(d_outer, "s2c", [128, 8])
        s2row = sb(d_outer, "s2row", [1, T])
        up_stack = stack.enter_context(ExitStack())
        xiT = sb(up_stack, "xiT", [128, DC * T], BF16)
        ln1_stack = stack.enter_context(ExitStack())
        v_in = [sb(ln1_stack, f"v_in{i}", [128, D]) for i in range(2)]
        x1t = sb(ln1_stack, "x1t", [128, D])
        xit = sb(ln1_stack, "xit", [128, D], BF16)
        vin_free = [None, None]
        x1t_free = None
        xit_free = None
        x1_done = [None] * J
        c1_tok = [None] * J
        xiT_ev = None
        parks_done = kb.tok("dS_vpk")
        for qt in range(J):
            bi = qt % 2
            dtok = kb.dma("Y", v_in[bi][:], v_park[qt], waits=[vin_free[bi], parks_done],
                          ch=f"vi{bi}")
            a1 = kb.emit("V", lambda e, j=qt: e.tensor_scalar(
                st(j, C_MU), st(j, C_SUM), 1.0 / D, None, AOT.mult),
                waits=[sum_tok[qt]])
            a2 = kb.emit("V", lambda e, j=qt: e.tensor_reduce(
                st(j, C_VAR), sqp[:, j * c.DSL:(j + 1) * c.DSL], AXX, AOT.add),
                waits=[sq_last[qt], sqd_tok])
            a3 = kb.emit("V", lambda e, j=qt: e.tensor_scalar(
                st(j, C_VAR), st(j, C_VAR), 1.0 / D, None, AOT.mult), waits=[a2])
            a4 = kb.emit("V", lambda e, j=qt: e.tensor_tensor(
                st(j, C_T0), st(j, C_MU), st(j, C_MU), AOT.mult), waits=[a1])
            var_tok = kb.emit("V", lambda e, j=qt: e.tensor_tensor(
                st(j, C_VAR), st(j, C_VAR), st(j, C_T0), AOT.subtract), waits=[a3, a4])
            sd_tok = kb.emit("S", lambda e, j=qt: e.activation(
                st(j, C_RST), st(j, C_VAR), AFT.Sqrt, bias=cv_s[:, 0:1]),
                waits=[var_tok, ld_cv])
            r1 = kb.emit("V", lambda e, j=qt: e.reciprocal(st(j, C_RST), st(j, C_RST)),
                         waits=[sd_tok])
            r2_ = kb.emit("V", lambda e, j=qt: e.tensor_tensor(
                st(j, C_NMU), st(j, C_MU), st(j, C_RST), AOT.mult), waits=[r1, a1])
            nmu_tok = kb.emit("V", lambda e, j=qt: e.tensor_scalar(
                st(j, C_NMU), st(j, C_NMU), -1.0, None, AOT.mult), waits=[r2_])
            ln_tok = kb.emit("S", lambda e, j=qt, b=bi: e.activation(
                x1t[:], v_in[b][:], AFT.Identity, bias=st(j, C_NMU), scale=st(j, C_RST)),
                waits=[dtok, nmu_tok, x1t_free])
            vin_free[bi] = ln_tok
            x1_done[qt] = kb.dma("S", x1_park[qt], x1t[:], waits=[ln_tok],
                                 sbuf_dest=False, ch="x1p")
            m1 = kb.emit("V", lambda e, j=qt: e.reduce_max(
                st(j, C_AM1), x1t[:], AXX, apply_absolute_value=True), waits=[ln_tok])
            m2 = kb.emit("V", lambda e, j=qt: e.tensor_scalar(
                st(j, C_AM1), st(j, C_AM1), 1e-5, None, AOT.max), waits=[m1])
            m3 = kb.emit("V", lambda e, j=qt: e.reciprocal(st(j, C_S1), st(j, C_AM1)),
                         waits=[m2])
            m4 = kb.emit("V", lambda e, j=qt: e.tensor_scalar(
                st(j, C_S1), st(j, C_S1), 127.0, None, AOT.mult), waits=[m3])
            s1_tok = kb.emit("V", lambda e, j=qt: e.tensor_scalar(
                st(j, C_IS1), st(j, C_AM1), 1.0 / 127, None, AOT.mult), waits=[m2])
            c1_tok[qt] = kb.emit("V", lambda e, j=qt: e.tensor_tensor(
                st(j, C_C1), st(j, C_IS1), scb_s[:, 2:3], AOT.mult),
                waits=[s1_tok, scb_done])
            q1 = kb.emit("S", lambda e, j=qt: e.activation(
                x1t[:], x1t[:], AFT.Identity, bias=cv_s[:, 1:2], scale=st(j, C_S1)),
                waits=[m4, x1_done[qt], m1, ln_tok])
            q2 = kb.emit("V", lambda e: e.tensor_scalar(
                x1t[:], x1t[:], MAGIC - 128.0, MAGIC + 127.0, AOT.max, AOT.min),
                waits=[q1])
            q3 = kb.emit("V", lambda e: e.tensor_scalar(
                xit[:], x1t[:], MAGIC, None, AOT.subtract), waits=[q2, xit_free])
            x1t_free = q3
            ttok = None
            for dc in range(DC):
                src = xit[:, dc * 128: dc * 128 + 128]
                dst = xiT[:, dc * T + qt * 128: dc * T + qt * 128 + 128]
                ttok, xiT_ev = tp.run(src, dst, waits=[q3], evac="S")
            xit_free = ttok

        if dbg == "x1":
            kb.dma("Y", dbgs, stat[:, 0:32 * J], waits=[kb.tok("V"), kb.tok("S")],
                   sbuf_dest=False, ch="out")
            for qt in range(J):
                kb.dma("Y", outv[qt], x1_park[qt], waits=[x1_done[qt]],
                       sbuf_dest=False, ch="out")
            kb.finish()
            with nc.Block() as block:
                kb.install(block)
            return nc

        ln1_stack.close()
        kb.close_regions()

        C_SUM2, C_MU2, C_VAR2, C_RST2, C_NMU2, C_T4 = 22, 23, 24, 25, 26, 27
        C_SMX, C_SNEG, C_SSUM, C_SRC = 28, 29, 30, 31
        sq_free = [None]
        stat2_free = [None] * J

        def ln_apply(j, src, dst, sqbuf, deps):
            a_s = kb.emit("V", lambda e: e.reduce_sum(st(j, C_SUM2), src, AXX),
                          waits=list(deps) + _toks(stat2_free[j]))
            a_q = kb.emit("S", lambda e: e.activation(
                sqbuf, src, AFT.Square, accum_out=st(j, C_VAR2)),
                waits=list(deps) + _toks(sq_free[0]) + _toks(stat2_free[j]))
            sq_free[0] = a_q
            m1 = kb.emit("V", lambda e: e.tensor_scalar(
                st(j, C_MU2), st(j, C_SUM2), 1.0 / D, None, AOT.mult), waits=[a_s])
            m2 = kb.emit("V", lambda e: e.tensor_scalar(
                st(j, C_VAR2), st(j, C_VAR2), 1.0 / D, None, AOT.mult), waits=[a_q])
            m3 = kb.emit("V", lambda e: e.tensor_tensor(
                st(j, C_T4), st(j, C_MU2), st(j, C_MU2), AOT.mult), waits=[m1])
            m4 = kb.emit("V", lambda e: e.tensor_tensor(
                st(j, C_VAR2), st(j, C_VAR2), st(j, C_T4), AOT.subtract),
                waits=[m2, m3])
            m5 = kb.emit("S", lambda e: e.activation(
                st(j, C_RST2), st(j, C_VAR2), AFT.Sqrt, bias=cv_s[:, 0:1]),
                waits=[m4, ld_cv])
            m6 = kb.emit("V", lambda e: e.reciprocal(st(j, C_RST2), st(j, C_RST2)),
                         waits=[m5])
            m7 = kb.emit("V", lambda e: e.tensor_tensor(
                st(j, C_NMU2), st(j, C_MU2), st(j, C_RST2), AOT.mult),
                waits=[m6, m1])
            m8 = kb.emit("V", lambda e: e.tensor_scalar(
                st(j, C_NMU2), st(j, C_NMU2), -1.0, None, AOT.mult), waits=[m7])
            ap_ = kb.emit("S", lambda e: e.activation(
                dst, src, AFT.Identity, bias=st(j, C_NMU2), scale=st(j, C_RST2)),
                waits=list(deps) + [m8])
            stat2_free[j] = ap_
            return ap_, a_s

        bank_i = [0]

        def nb():
            b = mmb[bank_i[0] % 6]
            bank_i[0] += 1
            return b

        NTC = T // 512
        FTB = 2048 // 128
        wup_stg = [sb(up_stack, f"wup_stg{i}", [128, DC * 128]) for i in range(2)]
        wup_tmp = [sb(up_stack, f"wup_tmp{i}", [128, DC * 128]) for i in range(2)]
        wuq = [sb(up_stack, f"wuq{i}", [128, DC * 128], BF16) for i in range(2)]
        rstg = [sb(up_stack, f"rstg{i}", [128, T], F16) for i in range(2)]
        zM = kb.emit("V", lambda e: e.memset(M_up[:], 0.0))
        wstg_free = [None, None]
        wtmp_free = [None, None]
        wuq_free = [None, None]
        rstg_free = [None, None]
        mx_tok = [zM] * NTC
        xiT_done = xiT_ev
        up2 = dbg is not None and dbg.startswith("up2")
        FT_eff = 2 if up2 else c.FT
        WUH = DC * 64
        for ot in range(FT_eff):
            bi = ot % 2
            dt0 = kb.dma("Y", wup_stg[bi][:, 0:WUH], w_upT[ot][:, 0:WUH],
                         waits=[wstg_free[bi]], ch=f"wua{bi}")
            dt1 = kb.dma("Y", wup_stg[bi][:, WUH:], w_upT[ot][:, WUH:],
                         waits=[wstg_free[bi]], ch=f"wub{bi}")
            uq1 = kb.emit("S", lambda e, b=bi: e.activation(
                wup_tmp[b][:], wup_stg[b][:], AFT.Identity, bias=cv_s[:, 1:2],
                scale=scb_s[:, 3:4]),
                waits=[dt0, dt1, scb_done, ld_cv, wtmp_free[bi]])
            wstg_free[bi] = uq1
            uq2 = kb.emit("V", lambda e, b=bi: e.tensor_scalar(
                wup_tmp[b][:], wup_tmp[b][:], MAGIC - 1.0, MAGIC + 1.0,
                AOT.max, AOT.min),
                waits=[uq1])
            uq3 = kb.emit("V", lambda e, b=bi: e.tensor_scalar(
                wuq[b][:], wup_tmp[b][:], MAGIC, None, AOT.subtract),
                waits=[uq2] + _toks(wuq_free[bi]))
            wtmp_free[bi] = uq3
            rel_toks = []
            mt = None
            for tc in range(NTC):
                bank = nb()
                mms = [(wuq[bi][:, dc * 128: dc * 128 + 128],
                        xiT[:, dc * T + tc * 512: dc * T + tc * 512 + 512])
                       for dc in range(DC)]
                mt = mm_group(kb, bank, bank.ap, mms, waits=[uq3, xiT_done])
                rel = kb.emit("S", lambda e, b=bi, t=tc, a=bank.ap:
                              e.activation(
                                  rstg[b][:, t * 512: t * 512 + 512], a,
                                  AFT.Relu, scale=cv_s[:, 2:3]),
                              waits=[mt, ld_cv] + _toks(rstg_free[bi]))
                rel_toks.append(rel)
                bank.free = [rel]
                mx = kb.emit("V", lambda e, b=bi, t=tc: e.tensor_max(
                    M_up[:, t * 512: t * 512 + 512],
                    rstg[b][:, t * 512: t * 512 + 512],
                    M_up[:, t * 512: t * 512 + 512]),
                    waits=[rel, mx_tok[tc]])
                mx_tok[tc] = mx
                rel_toks.append(mx)
            ptok = kb.dma("S", rh_pk[ot], rstg[bi][:], waits=rel_toks,
                          sbuf_dest=False, ch=f"rhp{bi}")
            rstg_free[bi] = [ptok]
            wuq_free[bi] = [mt]

        if dbg == "upq" or up2:
            kb.dma("Y", dbgs, stat[:, 0:32 * J], waits=[kb.tok("V"), kb.tok("S")],
                   sbuf_dest=False, ch="out")
            for qt in range(J):
                kb.dma("Y", outv[qt], x1_park[qt], waits=[x1_done[qt]],
                       sbuf_dest=False, ch="out")
            kb.finish()
            with nc.Block() as block:
                kb.install(block)
            return nc

        c2_tok = [None] * J
        s2c_toks = []
        for j in range(J):
            bank = nb()
            mmt = kb.emit("T", lambda e, j_=j, a=bank.ap: e.matmul(
                a[:, 0:128], M_up[:, j_ * 128: j_ * 128 + 128], eye32_s[:],
                start=True, stop=True),
                waits=_toks(bank.free) + [mx_tok[(j * 128) // 512], ld_eye32])
            hv = kb.emit("V", lambda e, j_=j, a=bank.ap: e.tensor_reduce(
                st(j_, C_HMX), a[:, 0:128], AXX, AOT.max), waits=[mmt])
            bank.free = hv
            e1 = kb.emit("V", lambda e, j_=j: e.tensor_tensor(
                st(j_, C_T2), st(j_, C_HMX), st(j_, C_HMX), AOT.mult), waits=[hv])
            e2 = kb.emit("V", lambda e, j_=j: e.tensor_scalar(
                st(j_, C_T0), st(j_, C_T2), 1e-30, None, AOT.max), waits=[e1])
            e3 = kb.emit("V", lambda e, j_=j: e.reciprocal(
                st(j_, C_T1), st(j_, C_T0)), waits=[e2])
            e4 = kb.emit("V", lambda e, j_=j: e.tensor_scalar(
                st(j_, C_S2P), st(j_, C_T1), 127.0, None, AOT.mult), waits=[e3])
            e5 = kb.emit("V", lambda e, j_=j: e.tensor_copy(
                s2c[:, j_:j_ + 1], st(j_, C_S2P)), waits=[e4])
            s2c_toks.append(e5)
            e6 = kb.emit("V", lambda e, j_=j: e.tensor_tensor(
                st(j_, C_T3), st(j_, C_C1), st(j_, C_C1), AOT.mult),
                waits=[c1_tok[j]])
            e7 = kb.emit("V", lambda e, j_=j: e.scalar_tensor_tensor(
                st(j_, C_R2M), st(j_, C_T2), 256.0, st(j_, C_T3),
                AOT.mult, AOT.mult),
                waits=[e6, e1])
            e8 = kb.emit("V", lambda e, j_=j: e.tensor_scalar(
                st(j_, C_R2M), st(j_, C_R2M), 1e-5, None, AOT.max), waits=[e7])
            c2_tok[j] = kb.emit("V", lambda e, j_=j: e.tensor_scalar(
                st(j_, C_C2), st(j_, C_R2M), scb_s[:, 4:5], 1.0 / 127,
                AOT.mult, AOT.mult), waits=[e8, scb_done])
        s2r8 = sb(up_stack, "s2r8", [J, 128])
        bank = nb()
        mmtr = kb.emit("T", lambda e, a=bank.ap: e.matmul(
            a[0:J, 0:128], s2c[:, 0:J], eye32_s[:], start=True, stop=True),
            waits=_toks(bank.free) + s2c_toks + [ld_eye32])
        cp8 = kb.emit("V", lambda e, a=bank.ap: e.tensor_copy(
            s2r8[:], a[0:J, 0:128]), waits=[mmtr])
        bank.free = cp8
        ds2w = kb.dma("S", s2_d.rearrange("(j ccc) -> j ccc", j=J), s2r8[:],
                      waits=[cp8], sbuf_dest=False, ch="s2d")
        rc_last = kb.dma("Y", s2row[0:1, 0:T],
                         s2_d.rearrange("(o t) -> o t", o=1),
                         waits=[ds2w], ch="s2d2")
        vs2_done = None
        for tc in range(NTC):
            bank2 = nb()
            bmt = kb.emit("T", lambda e, t=tc, a=bank2.ap: e.matmul(
                a, oner_s[:], s2row[0:1, t * 512: t * 512 + 512],
                start=True, stop=True),
                waits=_toks(bank2.free) + [rc_last, ld_oner])
            vs2_done = kb.emit("V", lambda e, t=tc, a=bank2.ap: e.tensor_copy(
                Vs2[:, t * 512: t * 512 + 512], a), waits=[bmt])
            bank2.free = vs2_done

        if dbg == "up":
            kb.dma("Y", dbgs, stat[:, 0:32 * J], waits=[kb.tok("V"), kb.tok("S")],
                   sbuf_dest=False, ch="out")
            for qt in range(J):
                kb.dma("Y", outv[qt], x1_park[qt], waits=[x1_done[qt]],
                       sbuf_dest=False, ch="out")
            kb.finish()
            with nc.Block() as block:
                kb.install(block)
            return nc

        up_stack.close()
        kb.close_regions()

        down_stack = stack.enter_context(ExitStack())
        rhrd = [sb(down_stack, f"rhrd{i}", [128, T], F16) for i in range(2)]
        dtmp = sb(down_stack, "dtmp", [128, T])
        xi2T = [sb(down_stack, f"xi2T{i}", [128, FTB * T], BF16) for i in range(2)]
        wdn_stg = [sb(down_stack, f"wdn_stg{i}", [128, FTB * 512])
                   for i in range(2)]
        wdq = [sb(down_stack, f"wdq{i}", [128, FTB * 512], BF16) for i in range(2)]
        pst = [sb(down_stack, f"pst{i}", [128, 512]) for i in range(4)]
        rh_parked = [kb.tok("dS_rhp0"), kb.tok("dS_rhp1")]
        rhrd_free = [None, None]
        dtmp_free = None
        xi2T_busy = [None, None]
        wstg_free2 = [None, None]
        wdq_free = [None, None]
        pst_free = [None] * 4
        pi = 0
        prep_done = [None, None]

        def emit_prep(fb_, ft0, ft1):
            nonlocal dtmp_free
            xb2 = fb_ % 2
            for ft in range(ft0, ft1):
                bi = ft % 2
                dr = kb.dma("G", rhrd[bi][:], rh_pk[fb_ * FTB + ft],
                            waits=_toks(rhrd_free[bi]) + rh_parked, ch=f"rhr{bi}")
                v1 = kb.emit("V", lambda e, b=bi: e.tensor_tensor(
                    dtmp[:], rhrd[b][:], rhrd[b][:], AOT.mult),
                    waits=[dr, dtmp_free])
                rhrd_free[bi] = v1
                v2 = kb.emit("V", lambda e: e.tensor_tensor(
                    dtmp[:], dtmp[:], Vs2[:], AOT.mult), waits=[v1, vs2_done])
                v3 = kb.emit("V", lambda e, x=xb2, f=ft: e.tensor_scalar(
                    xi2T[x][:, f * T: (f + 1) * T], dtmp[:], MAGIC, MAGIC,
                    AOT.add, AOT.subtract),
                    waits=[v2] + _toks(xi2T_busy[xb2]))
                dtmp_free = v3
                if ft == FTB - 1:
                    prep_done[xb2] = v3

        PPF = FTB // c.DSL
        emit_prep(0, 0, FTB)
        for fb in range(c.NFB):
            xb_ = fb % 2
            mm_last = None
            for os_ in range(c.DSL):
                wi = (fb * c.DSL + os_) % 2
                dw = kb.dma("Y", wdn_stg[wi][:], w_downT[fb, os_],
                            waits=[wstg_free2[wi]], ch=f"wdn{wi}")
                dq1 = kb.emit("S", lambda e, w=wi: e.activation(
                    wdn_stg[w][:], wdn_stg[w][:], AFT.Identity, bias=cv_s[:, 1:2],
                    scale=scb_s[:, 5:6]), waits=[dw, scb_done, ld_cv])
                dq2 = kb.emit("V", lambda e, w=wi: e.tensor_scalar(
                    wdn_stg[w][:], wdn_stg[w][:], MAGIC - 1.0, MAGIC + 1.0,
                    AOT.max, AOT.min), waits=[dq1])
                dq3 = kb.emit("V", lambda e, w=wi: e.tensor_scalar(
                    wdq[w][:], wdn_stg[w][:], MAGIC, None, AOT.subtract),
                    waits=[dq2] + _toks(wdq_free[wi]))
                wstg_free2[wi] = dq3
                if fb + 1 < c.NFB:
                    emit_prep(fb + 1, PPF * os_, PPF * (os_ + 1))
                for t in range(J):
                    bank = nb()
                    mms = [(xi2T[xb_][:, kt * T + t * 128: kt * T + t * 128 + 128],
                            wdq[wi][:, kt * 512:(kt + 1) * 512])
                           for kt in range(FTB)]
                    mt = mm_group(kb, bank, bank.ap, mms,
                                  waits=[dq3, prep_done[xb_]])
                    mm_last = mt
                    p_ = pi % 4
                    ev = kb.emit("V", lambda e, a=bank.ap, q=p_: e.tensor_copy(
                        pst[q][:], a), waits=[mt] + _toks(pst_free[p_]))
                    bank.free = ev
                    pk = kb.dma("S", pdn[fb][t][:, os_ * 512:(os_ + 1) * 512],
                                pst[p_][:], waits=[ev], sbuf_dest=False,
                                ch=f"pdw{p_}")
                    pst_free[p_] = pk
                    pi += 1
                wdq_free[wi] = mm_last
            xi2T_busy[xb_] = mm_last

        down_stack.close()
        d_outer.close()
        kb.close_regions()

        x2_stack = stack.enter_context(ExitStack())
        x2T = sb(x2_stack, "x2T", [128, DC * T], BF16)
        ln2_stack = stack.enter_context(ExitStack())
        acc2 = sb(ln2_stack, "acc2", [128, D])
        pstg = [sb(ln2_stack, f"pstg{i}", [128, D]) for i in range(2)]
        x1rd = sb(ln2_stack, "x1rd", [128, D])
        x2t = sb(ln2_stack, "x2t", [128, D])
        x2b = sb(ln2_stack, "x2b", [128, D], BF16)
        sqd2 = sb(ln2_stack, "sqd2", [128, D])
        pdn_done = [kb.tok(f"dS_pdw{i}") for i in range(4)]
        x1p_done = kb.tok("dS_x1p")
        acc_free = None
        pstg_free = [None, None]
        x1rd_free = None
        x2t_free = None
        x2b_free = None
        x2_done = [None] * J
        x2T_ev = None
        for t in range(J):
            d0 = kb.dma("Y", acc2[:], pdn[0][t],
                        waits=_toks(acc_free) + pdn_done, ch="pr0")
            ad = d0
            for fb in range(1, c.NFB):
                bi = fb % 2
                dfb = kb.dma("G", pstg[bi][:], pdn[fb][t],
                             waits=_toks(pstg_free[bi]) + pdn_done,
                             ch=f"pr{bi + 1}")
                ad = kb.emit("V", lambda e, b=bi: e.tensor_add(
                    acc2[:], acc2[:], pstg[b][:]), waits=[dfb, ad])
                pstg_free[bi] = ad
            dx1 = kb.dma("Y", x1rd[:], x1_park[t],
                         waits=_toks(x1rd_free) + [x1p_done], ch="x1r")
            vv = kb.emit("V", lambda e, j=t: e.scalar_tensor_tensor(
                acc2[:], acc2[:], st(j, C_C2), x1rd[:], AOT.mult, AOT.add),
                waits=[ad, dx1, c2_tok[t]])
            x1rd_free = vv
            apt, asr = ln_apply(t, acc2[:], x2t[:], sqd2[:],
                                [vv] + _toks(x2t_free))
            acc_free = [apt, asr]
            x2_done[t] = kb.dma("S", x2_park[t], x2t[:], waits=[apt],
                                sbuf_dest=False, ch="x2p")
            cb = kb.emit("V", lambda e: e.tensor_copy(x2b[:], x2t[:]),
                         waits=[apt] + _toks(x2b_free))
            x2t_free = [cb, x2_done[t]]
            ttok = None
            for dc in range(DC):
                ttok, x2T_ev = tp.run(
                    x2b[:, dc * 128: dc * 128 + 128],
                    x2T[:, dc * T + t * 128: dc * T + t * 128 + 128],
                    waits=[cb], evac="S")
            x2b_free = ttok

        if dbg == "x2":
            kb.dma("Y", dbgs, stat[:, 0:32 * J], waits=[kb.tok("V"), kb.tok("S")],
                   sbuf_dest=False, ch="out")
            for qt in range(J):
                kb.dma("Y", outv[qt], x2_park[qt], waits=[x2_done[qt]],
                       sbuf_dest=False, ch="out")
            kb.finish()
            with nc.Block() as block:
                kb.install(block)
            return nc

        ln2_stack.close()
        kb.close_regions()

        qkv_stack = stack.enter_context(ExitStack())
        KVW = c.KVH * 128
        wqb = [sb(qkv_stack, f"wqb{i}", [128, DC * 128], BF16) for i in range(2)]
        qstg = [sb(qkv_stack, f"qstg{i}", [128, T], BF16) for i in range(2)]
        wkvb = sb(qkv_stack, "wkvb", [128, DC * 512], BF16)
        kvst = [sb(qkv_stack, f"kvst{i}", [128, 512], BF16) for i in range(4)]
        SC = 0.08838834764831845
        x2T_done = x2T_ev
        wqb_free = [None, None]
        qstg_free = [None, None]
        for h in range(c.NH):
            bi = h % 2
            dq = kb.dma("G", wqb[bi][:], wq[h],
                        waits=_toks(wqb_free[bi]), ch=f"wq{bi}")
            evs = []
            mt = None
            for tc in range(NTC):
                bank = nb()
                mms = [(wqb[bi][:, dc * 128: dc * 128 + 128],
                        x2T[:, dc * T + tc * 512: dc * T + tc * 512 + 512])
                       for dc in range(DC)]
                mt = mm_group(kb, bank, bank.ap, mms, waits=[dq, x2T_done])
                ev = kb.emit("S", lambda e, b=bi, t=tc, a=bank.ap: e.activation(
                    qstg[b][:, t * 512: t * 512 + 512], a, AFT.Identity,
                    scale=SC), waits=[mt] + _toks(qstg_free[bi]))
                evs.append(ev)
                bank.free = ev
            pk = kb.dma("S", qT_pk[h], qstg[bi][:], waits=evs,
                        sbuf_dest=False, ch=f"qtp{bi}")
            qstg_free[bi] = pk
            wqb_free[bi] = mt
        wkv_free = None
        kvst_free = [None] * 4
        ki = 0
        kv_srcK_v = kv_srcK.rearrange("p (kvh tok) -> p kvh tok", kvh=c.KVH)
        kv_srcV_v = kv_srcV.rearrange("p (kvh j dh) -> p kvh j dh",
                                      kvh=c.KVH, j=J)
        for s0 in range(0, KVW, 512):
            sw = min(512, KVW - s0)
            dk = kb.dma("G", wkvb[:, 0:DC * sw], wk[s0 // sw],
                        waits=[wkv_free], ch="wkv")
            last = None
            for m0 in range(0, sw, 128):
                kvh = (s0 + m0) // 128
                for tc in range(NTC):
                    bank = nb()
                    mms = [(wkvb[:, dc * sw + m0: dc * sw + m0 + 128],
                            x2T[:, dc * T + tc * 512: dc * T + tc * 512 + 512])
                           for dc in range(DC)]
                    mt = mm_group(kb, bank, bank.ap, mms, waits=[dk, x2T_done])
                    last = mt
                    kq = ki % 4
                    ev = kb.emit("V", lambda e, a=bank.ap, q=kq: e.tensor_copy(
                        kvst[q][:], a), waits=[mt] + _toks(kvst_free[kq]))
                    bank.free = ev
                    dko = kb.dma("S", kv_srcK_v[:, kvh,
                                               tc * 512: tc * 512 + 512],
                                 kvst[kq][:], waits=[ev], sbuf_dest=False,
                                 ch=f"kvo{kq}")
                    kvst_free[kq] = dko
                    ki += 1
            wkv_free = last
        for s0 in range(0, KVW, 512):
            sw = min(512, KVW - s0)
            nk = sw // 128
            dv = kb.dma("G", wkvb[:, 0:DC * sw], wv[s0 // sw],
                        waits=[wkv_free], ch="wkv")
            last = None
            for t in range(J):
                bank = nb()
                mms = [(x2T[:, dc * T + t * 128: dc * T + t * 128 + 128],
                        wkvb[:, dc * sw: dc * sw + sw]) for dc in range(DC)]
                mt = mm_group(kb, bank, bank.ap[:, 0:sw], mms,
                              waits=[dv, x2T_done])
                last = mt
                kq = ki % 4
                ev = kb.emit("V", lambda e, a=bank.ap, q=kq, w=sw: e.tensor_copy(
                    kvst[q][:, 0:w], a[:, 0:w]), waits=[mt] + _toks(kvst_free[kq]))
                bank.free = ev
                dvo = kb.dma("S", kv_srcV_v[:, s0 // 128: s0 // 128 + nk,
                                            t, :],
                             kvst[kq][:, 0:sw].rearrange(
                                 "p (k dh) -> p k dh", k=nk),
                             waits=[ev], sbuf_dest=False, ch=f"kvo{kq}")
                kvst_free[kq] = dvo
                ki += 1
            wkv_free = last
        kv_out_done = [kb.tok(f"dS_kvo{i}") for i in range(4)]
        ccK = kb.emit("G", lambda e: e.collective_compute(
            "AllGather", AOT.bypass,
            replica_groups=[[0, 1], [2, 3], [4, 5], [6, 7]],
            ins=[kv_srcK], outs=[kv_dstK]), waits=kv_out_done)
        ccV = kb.emit("G", lambda e: e.collective_compute(
            "AllGather", AOT.bypass,
            replica_groups=[[0, 1], [2, 3], [4, 5], [6, 7]],
            ins=[kv_srcV], outs=[kv_dstV]), waits=[ccK])

        qkv_stack.close()
        x2_stack.close()
        kb.close_regions()

        sc_stack = stack.enter_context(ExitStack())
        kT_f = sb(sc_stack, "kT_f", [128, c.KVH * S2], BF16)
        V_f = sb(sc_stack, "V_f", [128, c.KVH * KT * 128], BF16)
        P_T = [sb(sc_stack, f"P_T{i}", [128, KT * T], BF16) for i in range(2)]
        sc_buf = [sb(sc_stack, f"sc_buf{i}", [128, S2]) for i in range(2)]
        p_sm = [sb(sc_stack, f"p_sm{i}", [128, S2], BF16) for i in range(2)]
        qTh = [sb(sc_stack, f"qTh{i}", [128, T], BF16) for i in range(2)]
        attn_h = [sb(sc_stack, f"attn_h{i}", [128, T], BF16) for i in range(2)]
        kT_f4 = kT_f[:].rearrange("p (kvh loc ccc) -> p kvh loc ccc",
                                  kvh=c.KVH, loc=KT)
        V_f4 = V_f[:].rearrange("p (kvh loc dh) -> p kvh loc dh",
                                kvh=c.KVH, loc=KT)
        kv_rd = []
        for q in range(2):
            dk = kb.dma("G", kT_f4[:, :, q * J:(q + 1) * J, :],
                        kv_dstK[q * 128:(q + 1) * 128, :].rearrange(
                            "p (kvh j ccc) -> p kvh j ccc", kvh=c.KVH, j=J),
                        waits=[ccK], ch="agk")
            dv = kb.dma("G", V_f4[:, :, q * J:(q + 1) * J, :],
                        kv_dstV[q * 128:(q + 1) * 128, :].rearrange(
                            "p (kvh j dh) -> p kvh j dh", kvh=c.KVH, j=J),
                        waits=[ccV], ch="agv")
            kv_rd += [dk, dv]
        kv_rd = [kb.tok("dG_agk"), kb.tok("dG_agv")]
        zp = [kb.emit("V", lambda e, i_=i: e.memset(P_T[i_][:], 0.0))
              for i in range(2)]
        qT_parked = [kb.tok("dS_qtp0"), kb.tok("dS_qtp1")]
        qTh_free = [None, None]
        attn_free = [None, None]
        sm_free = [[None] * J, [None] * J]
        psm_free = [None, None]
        scb_free = [None, None]
        pt_readlast = [zp[0], zp[1]]
        SCOL = [(C_SMX, C_SNEG, C_SSUM, C_SRC), (C_T0, C_T1, C_T2, C_T3)]
        for h in range(c.NH):
            kvh = h // c.G
            bi = h % 2
            dqh = kb.dma("Y", qTh[bi][:], qT_pk[h],
                         waits=_toks(qTh_free[bi]) + qT_parked,
                         ch=f"qtr{bi}")
            CX, CN, CS, CR = SCOL[bi]
            last_ev = {"V": None, "S": None}
            last_tp = None
            for qt in range(J):
                W_own = (qt + 1) * 128
                sevs = []
                for reg in range(2):
                    base = kvh * S2 + reg * T
                    dstb = reg * W_own
                    for b0 in range(0, W_own, 512):
                        bw = min(512, W_own - b0)
                        bank = nb()
                        mtS = kb.emit(
                            "T",
                            lambda e, a=bank.ap, b_=bi, q_=qt, ba=base, b0_=b0,
                            w=bw: e.matmul(
                                a[:, 0:w], qTh[b_][:, q_ * 128: q_ * 128 + 128],
                                kT_f[:, ba + b0_: ba + b0_ + w],
                                start=True, stop=True),
                            waits=_toks(bank.free) + [dqh] + kv_rd)
                        ev = kb.emit(
                            "S",
                            lambda e, a=bank.ap, b_=bi, d0=dstb + b0, w=bw:
                            e.copy(sc_buf[b_][:, d0: d0 + w], a[:, 0:w]),
                            waits=[mtS, scb_free[bi]])
                        bank.free = ev
                        sevs.append(ev)
                mk0 = kb.emit("V", lambda e, b_=bi, q_=qt: e.tensor_tensor(
                    sc_buf[b_][:, q_ * 128: q_ * 128 + 128],
                    sc_buf[b_][:, q_ * 128: q_ * 128 + 128], mk0_s[:], AOT.add),
                    waits=sevs + [ld_mk0])
                mk1 = kb.emit("V", lambda e, b_=bi, q_=qt, w=W_own:
                              e.tensor_tensor(
                    sc_buf[b_][:, w + q_ * 128: w + q_ * 128 + 128],
                    sc_buf[b_][:, w + q_ * 128: w + q_ * 128 + 128], mk1_s[:],
                    AOT.add), waits=sevs + [ld_mk1])
                W2 = 2 * W_own
                rmx = kb.emit("V", lambda e, b_=bi, q_=qt, w=W2, cx=CX:
                              e.tensor_reduce(
                    st(q_, cx), sc_buf[b_][:, 0:w], AXX, AOT.max),
                    waits=[mk0, mk1] + _toks(sm_free[bi][qt]))
                ng = kb.emit("V", lambda e, q_=qt, cx=CX, cn=CN:
                             e.tensor_scalar(
                    st(q_, cn), st(q_, cx), -1.0, None, AOT.mult),
                    waits=[rmx])
                ex = kb.emit("S", lambda e, b_=bi, q_=qt, w=W2, cn=CN, cs=CS:
                             e.activation(
                    sc_buf[b_][:, 0:w], sc_buf[b_][:, 0:w], AFT.Exp,
                    bias=st(q_, cn), accum_out=st(q_, cs)),
                    waits=[ng])
                scb_free[bi] = ex
                rc = kb.emit("V", lambda e, q_=qt, cr=CR, cs=CS: e.reciprocal(
                    st(q_, cr), st(q_, cs)), waits=[ex])
                nm = kb.emit("V", lambda e, b_=bi, q_=qt, w=W2, cr=CR:
                             e.tensor_scalar(
                    p_sm[b_][:, 0:w], sc_buf[b_][:, 0:w], st(q_, cr), None,
                    AOT.mult), waits=[rc, psm_free[bi]])
                sm_free[bi][qt] = nm
                for i in range(2 * (qt + 1)):
                    loc = i if i <= qt else J + (i - qt - 1)
                    ttok, evt = tp.run(
                        p_sm[bi][:, i * 128: i * 128 + 128],
                        P_T[bi][:, loc * T + qt * 128: loc * T + qt * 128 + 128],
                        waits=[nm, pt_readlast[bi]],
                        evac=("V" if i % 2 else "S"))
                    last_ev["V" if i % 2 else "S"] = evt
                    last_tp = ttok
                psm_free[bi] = last_tp
            aevs = []
            for qc in range(NTC):
                qtmax = 4 * qc + 3
                bank = nb()
                mms = []
                for loc0 in (0, J):
                    for jj in range(qtmax + 1):
                        loc = loc0 + jj
                        mms.append((
                            V_f[:, kvh * KT * 128 + loc * 128:
                                kvh * KT * 128 + loc * 128 + 128],
                            P_T[bi][:, loc * T + qc * 512:
                                     loc * T + qc * 512 + 512]))
                mt = mm_group(kb, bank, bank.ap, mms,
                              waits=_toks(last_ev["V"]) + _toks(last_ev["S"])
                              + kv_rd)
                pt_readlast[bi] = mt
                ev = kb.emit("V", lambda e, a=bank.ap, b_=bi, q_=qc:
                             e.tensor_copy(
                                 attn_h[b_][:, q_ * 512: q_ * 512 + 512], a),
                             waits=[mt] + _toks(attn_free[bi]))
                bank.free = ev
                aevs.append(ev)
            apk = kb.dma("S", att_pk[:, :, h * 128:(h + 1) * 128]
                         .rearrange("j p q -> p j q"),
                         attn_h[bi][:].rearrange("p (j q) -> p j q", j=J),
                         waits=aevs, sbuf_dest=False, ch=f"atw{bi}")
            attn_free[bi] = apk
            qTh_free[bi] = pt_readlast[bi]

        sc_stack.close()
        kb.close_regions()

        dn_stack = stack.enter_context(ExitStack())
        wdb = [sb(dn_stack, f"wdb{i}", [128, c.NH * 512], BF16) for i in range(2)]
        attn_t = [sb(dn_stack, f"attn_t{i}", [128, c.NH * 128], BF16)
                  for i in range(2)]
        pst2 = [sb(dn_stack, f"pst2{i}", [128, 512]) for i in range(4)]
        att_done = [kb.tok("dS_atw0"), kb.tok("dS_atw1")]
        wdb_free = [None, None]
        attn_t_free = [None, None]
        pst2_free = [None] * 4
        p2i = 0
        for ds in range(c.DSL):
            wi = ds % 2
            dwd = kb.dma("G", wdb[wi][:], w_denseT[ds],
                         waits=_toks(wdb_free[wi]), ch=f"wdd{wi}")
            mt = None
            for qt in range(J):
                ai = qt % 2
                da = kb.dma("Y", attn_t[ai][:], att_pk[qt],
                            waits=_toks(attn_t_free[ai]) + att_done,
                            ch=f"atr{ai}")
                bank = nb()
                mms = [(attn_t[ai][:, hh * 128: hh * 128 + 128],
                        wdb[wi][:, hh * 512: hh * 512 + 512])
                       for hh in range(c.NH)]
                mt = mm_group(kb, bank, bank.ap, mms, waits=[da, dwd])
                attn_t_free[ai] = mt
                q2 = p2i % 4
                ev = kb.emit("V", lambda e, a=bank.ap, q_=q2: e.tensor_copy(
                    pst2[q_][:], a), waits=[mt] + _toks(pst2_free[q2]))
                bank.free = ev
                pk = kb.dma("S", v3_park[qt][:, ds * 512:(ds + 1) * 512],
                            pst2[q2][:], waits=[ev], sbuf_dest=False,
                            ch=f"v3p{q2}")
                pst2_free[q2] = pk
                p2i += 1
            wdb_free[wi] = mt

        dn_stack.close()
        kb.close_regions()

        f_outer = stack.enter_context(ExitStack())
        x3T = sb(f_outer, "x3T", [128, DC * T], BF16)
        rwinb = sb(f_outer, "rwinb", [128, T])
        qfm = sb(f_outer, "qfm", [128, DC])
        yT = sb(f_outer, "yT", [128, T], BF16)
        BsT = sb(f_outer, "BsT", [128, D], BF16)
        ln3_stack = stack.enter_context(ExitStack())
        v3rd = sb(ln3_stack, "v3rd", [128, D])
        x2rd = sb(ln3_stack, "x2rd", [128, D])
        x3t = sb(ln3_stack, "x3t", [128, D])
        x3b = sb(ln3_stack, "x3b", [128, D], BF16)
        sqd3 = sb(ln3_stack, "sqd3", [128, D])
        v3_done = [kb.tok(f"dS_v3p{i}") for i in range(4)]
        x2pk_done = kb.tok("dS_x2p")
        v3rd_free = None
        x2rd_free = None
        x3t_free = None
        x3b_free = None
        x3_done = [None] * J
        x3T_ev = None
        for t in range(J):
            d1 = kb.dma("Y", v3rd[:], v3_park[t],
                        waits=_toks(v3rd_free) + v3_done, ch="v3r")
            d2 = kb.dma("G", x2rd[:], x2_park[t],
                        waits=_toks(x2rd_free) + [x2pk_done], ch="x2r")
            vv = kb.emit("V", lambda e: e.tensor_add(v3rd[:], v3rd[:], x2rd[:]),
                         waits=[d1, d2])
            x2rd_free = vv
            apt, asr = ln_apply(t, v3rd[:], x3t[:], sqd3[:],
                                [vv] + _toks(x3t_free))
            v3rd_free = [apt, asr]
            x3_done[t] = kb.dma("S", x3_park[t], x3t[:], waits=[apt],
                                sbuf_dest=False, ch="x3p")
            cb = kb.emit("V", lambda e: e.tensor_copy(x3b[:], x3t[:]),
                         waits=[apt] + _toks(x3b_free))
            x3t_free = [cb, x3_done[t]]
            ttok = None
            for dc in range(DC):
                ttok, x3T_ev = tp.run(
                    x3b[:, dc * 128: dc * 128 + 128],
                    x3T[:, dc * T + t * 128: dc * T + t * 128 + 128],
                    waits=[cb], evac="S")
            x3b_free = ttok

        if dbg == "x3":
            kb.dma("Y", dbgs, stat[:, 0:32 * J], waits=[kb.tok("V"), kb.tok("S")],
                   sbuf_dest=False, ch="out")
            for qt in range(J):
                kb.dma("Y", outv[qt], x3_park[qt], waits=[x3_done[qt]],
                       sbuf_dest=False, ch="out")
            kb.finish()
            with nc.Block() as block:
                kb.install(block)
            return nc

        ln3_stack.close()
        kb.close_regions()

        rt_stack = stack.enter_context(ExitStack())
        qtmp = sb(rt_stack, "qtmp", [128, T])
        rwt_s = sb(rt_stack, "rwt_s", [128, DC * 8])
        lg = sb(rt_stack, "lg", [1, 32])
        rbs = sb(rt_stack, "rbs", [1, 8])
        rtc = sb(rt_stack, "rtc", [8, 1])
        rsc_col = sb(rt_stack, "rsc_col", [128, 1])
        bt_s = sb(rt_stack, "bt_s", [128, D])
        atb = sb(rt_stack, "atb", [128, DC * 128], BF16)
        rb_ev = None
        for tc in range(NTC):
            bank = nb()
            bmt = kb.emit("T", lambda e, a=bank.ap, t=tc: e.matmul(
                a, oner_s[:], rwr_s[0:1, t * 512: t * 512 + 512],
                start=True, stop=True),
                waits=_toks(bank.free) + [ld_oner, ld_rwr])
            rb_ev = kb.emit("V", lambda e, a=bank.ap, t=tc: e.tensor_copy(
                rwinb[:, t * 512: t * 512 + 512], a), waits=[bmt])
            bank.free = rb_ev
        qtmp_free = None
        qs_last = None
        for dc in range(DC):
            v1 = kb.emit("V", lambda e, d_=dc: e.tensor_tensor(
                qtmp[:], x3T[:, d_ * T: (d_ + 1) * T], rwinb[:], AOT.mult),
                waits=[x3T_ev, rb_ev, qtmp_free])
            qs_last = kb.emit("V", lambda e, d_=dc: e.reduce_sum(
                qfm[:, d_:d_ + 1], qtmp[:], AXX), waits=[v1])
            qtmp_free = qs_last
        dq1 = kb.dma("G", q_src.rearrange("(p dc) -> p dc", p=128), qfm[:],
                     waits=[qs_last], sbuf_dest=False, ch="qsr")
        ccq = kb.emit("G", lambda e: e.collective_compute(
            "AllReduce", AOT.add,
            replica_groups=[[0, 1], [2, 3], [4, 5], [6, 7]],
            ins=[q_src], outs=[q_dst]), waits=[dq1])
        dq2 = kb.dma("G", qfm[:], q_dst.rearrange("(p dc) -> p dc", p=128),
                     waits=[ccq], ch="qsr")
        ld_rwt = kb.dma("Y", rwt_s[:], rW_T, ch="rwl")
        bank = nb()
        lg_mms = [(qfm[:, dc:dc + 1], rwt_s[:, dc * 8: dc * 8 + 8])
                  for dc in range(DC)]
        lmt = mm_group(kb, bank, bank.ap[0:1, 0:8], lg_mms,
                       waits=[dq2, ld_rwt])
        ld_rb = kb.dma("Y", rbs[:], rb_row, ch="rbl")
        l0 = kb.emit("V", lambda e, a=bank.ap: e.tensor_copy(
            lg[0:1, 0:8], a[0:1, 0:8]), waits=[lmt])
        bank.free = l0
        l1 = kb.emit("V", lambda e: e.tensor_scalar(
            lg[0:1, 0:8], lg[0:1, 0:8], 1.0 / c.WCNT, None, AOT.mult),
            waits=[l0])
        l2 = kb.emit("V", lambda e: e.tensor_tensor(
            lg[0:1, 0:8], lg[0:1, 0:8], rbs[0:1, :], AOT.add),
            waits=[l1, ld_rb])
        l3 = kb.emit("V", lambda e: e.tensor_reduce(
            lg[0:1, 8:9], lg[0:1, 0:8], AXX, AOT.max), waits=[l2])
        l4 = kb.emit("V", lambda e: e.tensor_scalar(
            lg[0:1, 9:10], lg[0:1, 8:9], -1.0, None, AOT.mult), waits=[l3])
        l5 = kb.emit("S", lambda e: e.activation(
            lg[0:1, 16:24], lg[0:1, 0:8], AFT.Exp, bias=lg[0:1, 9:10],
            accum_out=lg[0:1, 10:11]), waits=[l4])
        l6 = kb.emit("V", lambda e: e.reciprocal(
            lg[0:1, 11:12], lg[0:1, 10:11]), waits=[l5])
        l7 = kb.emit("V", lambda e: e.tensor_scalar(
            lg[0:1, 24:32], lg[0:1, 16:24], lg[0:1, 11:12], None, AOT.mult),
            waits=[l6])
        d3 = kb.dma("S", rt_d.rearrange("(o e) -> o e", o=1), lg[0:1, 24:32],
                    waits=[l7], sbuf_dest=False, ch="rtd")
        d4 = kb.dma("Y", rtc[:], rt_d.rearrange("(e o) -> e o", o=1),
                    waits=[d3], ch="rtd2")
        bank = nb()
        rmt = kb.emit("T", lambda e, a=bank.ap: e.matmul(
            a[:, 0:1], repm_s[:], rtc[:], start=True, stop=True),
            waits=_toks(bank.free) + [d4, ld_repm])
        rsc_ev = kb.emit("V", lambda e, a=bank.ap: e.tensor_copy(
            rsc_col[:], a[:, 0:1]), waits=[rmt])
        bank.free = rsc_ev
        ld_bt = kb.dma("Y", bt_s[:], B_T, ch="btl")
        bse = kb.emit("V", lambda e: e.tensor_scalar(
            BsT[:], bt_s[:], rsc_col[:, 0:1], None, AOT.mult),
            waits=[ld_bt, rsc_ev])
        da_ = kb.dma("G", atb[:], A_T, ch="atl")
        y_ev = None
        for tc in range(NTC):
            bank = nb()
            mms = [(atb[:, dc * 128: dc * 128 + 128],
                    x3T[:, dc * T + tc * 512: dc * T + tc * 512 + 512])
                   for dc in range(DC)]
            mt = mm_group(kb, bank, bank.ap, mms, waits=[da_, x3T_ev])
            y_ev = kb.emit("V", lambda e, a=bank.ap, t=tc: e.tensor_copy(
                yT[:, t * 512: t * 512 + 512], a), waits=[mt, y_ev])
            bank.free = y_ev

        rt_stack.close()
        kb.close_regions()

        fb_stack = stack.enter_context(ExitStack())
        wfb = [sb(fb_stack, f"wfb{i}", [128, DC * 512], BF16) for i in range(2)]
        fst = [sb(fb_stack, f"fst{i}", [128, 512]) for i in range(4)]
        wfb_free = [None, None]
        fst_free = [None] * 4
        fi = 0
        for ds in range(c.DSL):
            wi = ds % 2
            dwf = kb.dma("G", wfb[wi][:], WT[ds],
                         waits=_toks(wfb_free[wi]), ch=f"wff{wi}")
            mt = None
            for t in range(J):
                bank = nb()
                mms = [(x3T[:, dc * T + t * 128: dc * T + t * 128 + 128],
                        wfb[wi][:, dc * 512:(dc + 1) * 512])
                       for dc in range(DC)]
                mms.append((yT[:, t * 128: t * 128 + 128],
                            BsT[:, ds * 512:(ds + 1) * 512]))
                mt = mm_group(kb, bank, bank.ap, mms,
                              waits=[dwf, x3T_ev, y_ev, bse])
                q3_ = fi % 4
                ev = kb.emit("V", lambda e, a=bank.ap, q_=q3_: e.tensor_copy(
                    fst[q_][:], a), waits=[mt] + _toks(fst_free[q3_]))
                bank.free = ev
                pk = kb.dma("S", b4_park[t][:, ds * 512:(ds + 1) * 512],
                            fst[q3_][:], waits=[ev], sbuf_dest=False,
                            ch=f"b4p{q3_}")
                fst_free[q3_] = pk
                fi += 1
            wfb_free[wi] = mt

        fb_stack.close()
        f_outer.close()
        kb.close_regions()

        ln4_stack = stack.enter_context(ExitStack())
        bb = sb(ln4_stack, "bb", [128, D])
        brs = sb(ln4_stack, "brs", [1, D])
        b4rd = [sb(ln4_stack, f"b4rd{i}", [128, D]) for i in range(2)]
        x3rd = [sb(ln4_stack, f"x3rd{i}", [128, D]) for i in range(2)]
        outt = [sb(ln4_stack, f"outt{i}", [128, D]) for i in range(2)]
        sqd4 = sb(ln4_stack, "sqd4", [128, D])
        ld_br = kb.dma("Y", brs[:], b_row, ch="brl")
        bb_ev = None
        for ds2 in range(c.DSL):
            bank = nb()
            bmt = kb.emit("T", lambda e, a=bank.ap, d_=ds2: e.matmul(
                a, oner_s[:], brs[0:1, d_ * 512: d_ * 512 + 512],
                start=True, stop=True),
                waits=_toks(bank.free) + [ld_br, ld_oner])
            bb_ev = kb.emit("V", lambda e, a=bank.ap, d_=ds2: e.tensor_copy(
                bb[:, d_ * 512: d_ * 512 + 512], a), waits=[bmt, bb_ev])
            bank.free = bb_ev
        b4_done = [kb.tok(f"dS_b4p{i}") for i in range(4)]
        x3pk_done = kb.tok("dS_x3p")
        b4rd_free = [None, None]
        x3rd_free = [None, None]
        outt_free = [None, None]
        for t in range(J):
            bi = t % 2
            d1 = kb.dma("Y", b4rd[bi][:], b4_park[t],
                        waits=_toks(b4rd_free[bi]) + b4_done, ch=f"b4r{bi}")
            d2 = kb.dma("G", x3rd[bi][:], x3_park[t],
                        waits=_toks(x3rd_free[bi]) + [x3pk_done], ch=f"x3r{bi}")
            v1 = kb.emit("V", lambda e, b=bi: e.tensor_add(
                b4rd[b][:], b4rd[b][:], bb[:]), waits=[d1, bb_ev])
            v2 = kb.emit("V", lambda e, b=bi: e.tensor_add(
                b4rd[b][:], b4rd[b][:], x3rd[b][:]), waits=[v1, d2])
            x3rd_free[bi] = v2
            apt, asr = ln_apply(t, b4rd[bi][:], outt[bi][:], sqd4[:],
                                [v2] + _toks(outt_free[bi]))
            b4rd_free[bi] = [apt, asr]
            od = kb.dma("Y", outv[t], outt[bi][:], waits=[apt],
                        sbuf_dest=False, ch=f"out{bi}")
            outt_free[bi] = od

        kb.finish()
        with nc.Block() as block:
            kb.install(block)
    return nc


def kernel(**inputs):
    cfg = FULL
    in_maps, metas = _host_prepare(cfg, **inputs)
    nc = build_nc(cfg)
    res = run_bass_kernel_spmd(nc, in_maps, core_ids=list(range(8)))
    x = np.asarray(inputs["L0_attn_x"])
    outp = np.zeros_like(x)
    for r, m in zip(res.results, metas):
        outp[m["b"]][m["own"]] = r["out"]
    return outp
